# version 28
# baseline (speedup 1.0000x reference)
import numpy as np
from contextlib import ExitStack
import concourse.bass as bass
import concourse.mybir as mybir
from concourse.bass_utils import run_bass_kernel_spmd
from concourse.alu_op_type import AluOpType as OP

F32 = mybir.dt.float32
BF16 = mybir.dt.bfloat16
AF = mybir.ActivationFunctionType
AX = mybir.AxisListType.X

D = 1024
T = 4096
EPS = 1e-5


class B:
    ENGINES = ("sync", "tensor", "vector", "scalar", "gpsimd")

    def __init__(self):
        self.nc = bass.Bass("TRN2", target_bir_lowering=False)
        self.qs = {e: [] for e in self.ENGINES}
        self.ctx = ExitStack()
        self.phase_ctx = None
        self.semh = {}
        self.cnt = {}

    def push_scope(self):
        if not hasattr(self, "_scopes"):
            self._scopes = []
        self._scopes.append(ExitStack())

    def pop_scope(self):
        self._scopes.pop().close()

    def begin_phase(self):
        self.phase_ctx = ExitStack()

    def end_phase(self):
        if self.phase_ctx is not None:
            self.phase_ctx.close()
            self.phase_ctx = None

    def barrier(self):
        snap = [(s, v) for s, v in self.cnt.items() if v > 0]
        for e in self.ENGINES:
            self.on(e, None, wait=snap)

    def sem(self, name):
        if name not in self.semh:
            self.semh[name] = self.ctx.enter_context(self.nc.semaphore(name))
            self.cnt[name] = 0
        return self.semh[name]

    def _ctx(self):
        if getattr(self, "_scopes", None):
            return self._scopes[-1]
        return self.phase_ctx if self.phase_ctx is not None else self.ctx

    def sb(self, name, shape, dt):
        return self._ctx().enter_context(self.nc.sbuf_tensor(name, list(shape), dt))

    def ps(self, name, shape, dt=F32):
        return self._ctx().enter_context(self.nc.psum_tensor(name, list(shape), dt))

    def dram_in(self, name, shape, dt=F32):
        return self.nc.dram_tensor(name, list(shape), dt, kind="ExternalInput").ap()

    def dram_out(self, name, shape, dt=F32):
        return self.nc.dram_tensor(name, list(shape), dt, kind="ExternalOutput").ap()

    def dram_tmp(self, name, shape, dt=F32):
        return self.nc.dram_tensor(name, list(shape), dt, kind="Internal").ap()

    def presem(self):
        for s in ("mm", "ve", "sc", "gp", "st", "lc"):
            self.sem(s)

    def on(self, eng, fn, wait=None, inc=None, inc_n=1):
        waits = []
        if wait:
            for w in wait:
                if w is None:
                    continue
                if isinstance(w, str):
                    waits.append((self.sem(w), self.cnt[w]))
                else:
                    waits.append((self.sem(w[0]), w[1]))
        ret = None
        if inc is not None:
            semh = self.sem(inc)
            self.cnt[inc] += inc_n
            ret = self.cnt[inc]

            def f(e, fn=fn, waits=waits, semh=semh, n=inc_n):
                for s, v in waits:
                    e.wait_ge(s, v)
                fn(e).then_inc(semh, n)
        else:
            def f(e, fn=fn, waits=waits):
                for s, v in waits:
                    e.wait_ge(s, v)
                if fn is not None:
                    fn(e)
        self.qs[eng].append(f)
        return ret

    def wait(self, eng, sem, val=None):
        self.on(eng, None, wait=[sem if val is None else (sem, val)])

    def dma(self, eng, out, in_, inc="ld", wait=None):
        return self.on(eng, lambda e, o=out, i=in_: e.dma_start(out=o, in_=i),
                       wait=wait, inc=inc, inc_n=16)

    def finalize(self):
        with self.nc.Block() as block:
            for e in self.ENGINES:
                q = self.qs[e]
                if not q:
                    continue

                def runner(eng, q=q):
                    for f in q:
                        f(eng)
                getattr(block, e)(runner)
        self.ctx.close()
        return self.nc


class Rot:

    def __init__(self, slots):
        self.slots = list(slots)
        self.tok = [None] * len(self.slots)
        self.i = 0

    def get(self):
        i = self.i
        self.i = (self.i + 1) % len(self.slots)
        return i, self.slots[i], self.tok[i]


def pp_col(v):
    v = np.asarray(v)
    return np.ascontiguousarray(v.reshape(-1, 128).T.astype(np.float32, copy=False))


def host_consts_l0():
    onesblk = np.zeros((128, 2), np.float32)
    onesblk[:64, 0] = 1.0
    onesblk[64:, 1] = 1.0
    onesblk_t = np.ascontiguousarray(onesblk.T)
    sel = np.zeros((16, 128), np.float32)
    for e in range(16):
        sel[e, (e % 2) * 64:(e % 2) * 64 + 64] = 1.0
    decmask = np.zeros((16, 8), np.float32)
    for e in range(16):
        decmask[e, e // 2] = 1.0
    return {"c_onesblk": onesblk, "c_onesblk_t": onesblk_t,
            "c_sel": sel, "c_decmask": decmask}


def emit_post_ln(b, zbuf, TT, sqtile, rows, pstat1, pstat2, pbc1, pbc2,
                 onescol, onesrow, epscol, outf=None, outb=None, zoff=0):
    zc = lambda cc: zbuf[:, zoff + cc * TT:zoff + (cc + 1) * TT]
    zready = ("ve", b.cnt["ve"])
    scready = ("sc", b.cnt["sc"])
    zbf = b._ln_zbf
    m_ = rows[0:1, 0 * TT:1 * TT]
    msq = rows[0:1, 1 * TT:2 * TT]
    var = rows[0:1, 1 * TT:2 * TT]
    tmp = rows[0:1, 2 * TT:3 * TT]
    rb2 = b._ln_rows_bf
    mm_prev = None
    for cc in range(8):
        sct = b.on("scalar",
                   lambda e, o=zbf[:, 0:TT], i=zc(cc): e.copy(out=o, in_=i),
                   wait=([zready, scready] if cc == 0 else [mm_prev]), inc="sc")
        b.on("tensor",
             lambda e, p=pstat1[0:1, 0:TT], l=onescol[:, 0:1], r=zbf[:, 0:TT],
                    st=(cc == 0), sp=(cc == 7): e.matmul(p, l, r, start=st, stop=sp),
             wait=[("sc", sct)], inc="mm")
        sct2 = b.on("scalar",
                    lambda e, o=sqtile[:, 0:TT], i=zc(cc):
                    e.activation(out=o, in_=i, func=AF.Square),
                    wait=[scready] if cc == 0 else None, inc="sc")
        tok = b.on("tensor",
                   lambda e, p=pstat2[0:1, 0:TT], l=onescol[:, 0:1], r=sqtile[:, 0:TT],
                          st=(cc == 0), sp=(cc == 7): e.matmul(p, l, r, start=st, stop=sp),
                   wait=[("sc", sct2)], inc="mm")
        mm_prev = ("mm", tok)
    mm_sq = b.cnt["mm"]
    b.on("vector", lambda e, o=m_, i=pstat1[0:1, 0:TT]:
         e.tensor_scalar(out=o, in0=i, scalar1=1.0 / 1024, scalar2=None, op0=OP.mult),
         wait=[("mm", mm_sq)])
    b.on("vector", lambda e, o=msq, i=pstat2[0:1, 0:TT]:
         e.tensor_scalar(out=o, in0=i, scalar1=1.0 / 1024, scalar2=None, op0=OP.mult),
         inc="ve")
    ve_ms = b.cnt["ve"]
    b.on("vector", lambda e, o=tmp, a=m_, c=m_: e.tensor_tensor(out=o, in0=a, in1=c, op=OP.mult),
         wait=[("ve", ve_ms)])
    b.on("vector", lambda e, o=var, a=msq, c=tmp:
         e.tensor_tensor(out=o, in0=a, in1=c, op=OP.subtract), inc="ve")
    ve_var = b.cnt["ve"]
    sct = b.on("scalar", lambda e, o=tmp, i=var, bi=epscol[0:1, 0:1]:
               e.activation(out=o, in_=i, func=AF.Ln, bias=bi, scale=1.0),
               wait=[("ve", ve_var)], inc="sc")
    vth = b.on("vector", lambda e, o=tmp, i=tmp:
               e.tensor_scalar(out=o, in0=i, scalar1=-0.5, scalar2=None, op0=OP.mult),
               wait=[("sc", sct)], inc="ve")
    sct = b.on("scalar", lambda e, o=rb2[0:1, 0:TT], i=tmp:
               e.activation(out=o, in_=i, func=AF.Exp),
               wait=[("ve", vth)], inc="sc")
    b.on("vector", lambda e, o=rb2[0:1, TT:2 * TT], a=m_, c=rb2[0:1, 0:TT]:
         e.tensor_tensor(out=o, in0=a, in1=c, op=OP.mult),
         wait=[("sc", sct)], inc="ve")
    ve_rows = b.cnt["ve"]
    b.on("tensor", lambda e, p=pbc1[:, 0:TT], l=onesrow[0:1, :], r=rb2[0:1, 0:TT]:
         e.matmul(p, l, r, start=True, stop=True), wait=[("ve", ve_rows)])
    b.on("tensor", lambda e, p=pbc2[:, 0:TT], l=onesrow[0:1, :], r=rb2[0:1, TT:2 * TT]:
         e.matmul(p, l, r, start=True, stop=True), inc="mm")
    mm_bc = b.cnt["mm"]
    for cc in range(8):
        b.on("vector",
             lambda e, o=sqtile[:, 0:TT], a=zc(cc), c=pbc1[:, 0:TT]:
             e.tensor_tensor(out=o, in0=a, in1=c, op=OP.mult),
             wait=[("mm", mm_bc), scready] if cc == 0 else None)
        if outf is not None:
            b.on("vector",
                 lambda e, o=outf(cc), a=sqtile[:, 0:TT], c=pbc2[:, 0:TT]:
                 e.tensor_tensor(out=o, in0=a, in1=c, op=OP.subtract), inc="ve")
        if outb is not None:
            b.on("vector",
                 lambda e, o=outb(cc), a=sqtile[:, 0:TT], c=pbc2[:, 0:TT]:
                 e.tensor_tensor(out=o, in0=a, in1=c, op=OP.subtract), inc="ve")
    return b.cnt["ve"]


def emit_l0_attn(b, x0T, x1T, wqkvT, wgateT, woutT, bgate_pp, decay_p, consts,
                 wqkvb=None, wgateb=None, woutb=None):
    b.presem()
    use_scr = wqkvb is not None
    if use_scr:
        for srcw, dstw, n in ((wqkvT, wqkvb, 3072), (wgateT, wgateb, 1024),
                              (woutT, woutb, 1024)):
            for r0 in range(0, 1024, 256):
                b.dma("gpsimd", dstw[r0:r0 + 256, 0:n], srcw[r0:r0 + 256, 0:n],
                      inc="cast0")
    cast0_tot = b.cnt.get("cast0", 0)
    TH = 1024
    TT = 512
    NTT = TH // TT
    NH = T // TH

    x0b = b.sb("A_x0b", [128, 8 * TH], BF16)
    vfull = b.sb("A_vfull", [128, 8 * TH], BF16)
    apre = b.sb("A_apre", [128, 8 * TH], BF16)
    qb_ = [b.sb(f"A_qb{i}", [128, TH], F32) for i in range(2)]
    kb_ = [b.sb(f"A_kb{i}", [128, TH], F32) for i in range(2)]
    kvb_ = [b.sb(f"A_kvb{i}", [128, TH], F32) for i in range(2)]
    sck_ = [b.sb(f"A_sck{i}", [128, TH], F32) for i in range(2)]
    sckv_ = [b.sb(f"A_sckv{i}", [128, TH], F32) for i in range(2)]
    mixs = kvb_[0]
    stg2 = sck_[0]
    tmps = b.sb("A_tmps", [128, TT], F32)
    tmps2 = b.sb("A_tmps2", [128, TT], F32)
    wstage = [b.sb(f"A_wstage{i}", [128, 1024], F32) for i in range(2)]
    wbf = [b.sb(f"A_wbf{i}", [128, 1024], BF16) for i in range(2)]
    carry = b.sb("A_carry", [128, 32], F32)
    dec16 = b.sb("A_dec16", [16, 1], F32)
    decsig = b.sb("A_decsig", [16, 1], F32)
    decrhs = b.sb("A_decrhs", [16, 8], F32)
    deccols = b.sb("A_deccols", [128, 8], F32)
    onesblk_f = b.sb("A_onesblkf", [128, 2], F32)
    onesblk_tf = b.sb("A_onesblk_tf", [2, 128], F32)
    onesblk = b.sb("A_onesblk", [128, 2], BF16)
    onesblk_t = b.sb("A_onesblk_t", [2, 128], BF16)
    prodb = b.sb("A_prodb", [128, 1024], BF16)
    den2b = b.sb("A_den2b", [2, TT], BF16)
    selm = b.sb("A_selm", [16, 128], F32)
    decmask = b.sb("A_decmask", [16, 8], F32)
    bgate_sb = b.sb("A_bgate", [128, 8], F32)
    onescol = b.sb("A_onescol", [128, 1], BF16)
    onesrow = b.sb("A_onesrow", [1, 128], BF16)
    epscol = b.sb("A_epscol", [128, 1], F32)
    den2 = b.sb("A_den2", [2, TT], F32)
    x0f = b.sb("A_x0f", [128, 16 * TT], F32)
    zbufq = b.sb("A_zbufq", [128, 16 * TT], F32)
    zbuf = zbufq
    x1buf = b.sb("A_x1buf", [128, 8 * TT], F32)
    sqtile = b.sb("A_sqtile", [128, TT], BF16)
    b._ln_zbf = b.sb("A_lnzbf", [128, TT], BF16)
    b._ln_rows_bf = b.sb("A_lnrbf", [1, 2 * TT], BF16)
    rows = b.sb("A_rows", [1, 4 * TT], F32)

    pmm = [b.ps(f"A_pmm{i}", [128, TT]) for i in range(2)]
    pden = b.ps("A_pden", [2, TT])
    pdbc = b.ps("A_pdbc", [128, TT])
    pstat1 = b.ps("A_pstat1", [1, TT])
    pstat2 = b.ps("A_pstat2", [1, TT])
    pbc1 = b.ps("A_pbc1", [128, TT])
    pbc2 = b.ps("A_pbc2", [128, TT])

    pmm_rot = Rot(pmm)
    wrot = Rot(list(range(2)))

    b.on("vector", lambda e, o=onesblk[:, :], i=onesblk_f[:, :]: e.tensor_copy(out=o, in_=i),
         wait=[("lc", 32)])
    b.on("vector", lambda e, o=onesblk_t[:, :], i=onesblk_tf[:, :]: e.tensor_copy(out=o, in_=i),
         wait=[("lc", 32)], inc="ve")
    for dst, key in [(onesblk_f, "c_onesblk"), (onesblk_tf, "c_onesblk_t"),
                     (selm, "c_sel"), (decmask, "c_decmask")]:
        b.dma("sync", dst[:, :], consts[key][:, :], inc="lc")
    b.dma("sync", bgate_sb[:, :], bgate_pp[:, :], inc="lc")
    ld_dec = b.dma("sync", dec16[:, 0:1], decay_p[:, :], inc="lc")
    b.on("vector", lambda e, o=onescol[:, :]: e.memset(o, 1.0))
    b.on("vector", lambda e, o=epscol[:, :]: e.memset(o, EPS))
    b.on("vector", lambda e, o=onesrow[:, :]: e.memset(o, 1.0), inc="ve")
    sct = b.on("scalar", lambda e, o=decsig[:, 0:1], i=dec16[:, 0:1]:
               e.activation(out=o, in_=i, func=AF.Sigmoid),
               wait=[("lc", ld_dec)], inc="sc")
    vet = b.on("vector", lambda e, o=decrhs[:, :], a=decmask[:, :], s=decsig[:, 0:1]:
               e.tensor_scalar(out=o, in0=a, scalar1=s, scalar2=None, op0=OP.mult),
               wait=[("sc", sct)], inc="ve")
    mmt = b.on("tensor", lambda e, p=pdbc[:, 0:8], l=selm[:, :], r=decrhs[:, :]:
               e.matmul(p, l, r, start=True, stop=True),
               wait=[("ve", vet)], inc="mm")
    b.on("vector", lambda e, o=deccols[:, :], i=pdbc[:, 0:8]: e.tensor_copy(out=o, in_=i),
         wait=[("mm", mmt)], inc="ve")

    def load_w_tiles(srcT, fbase, scr=None):
        i, _, tok = wrot.get()
        sem = f"lw{i}"
        if use_scr and scr is not None:
            srcv = scr.rearrange("(cc p) f -> p cc f", p=128)
            w0 = [tok] if tok else []
            if not b._cast0_waited:
                w0.append(("cast0", cast0_tot))
                b._cast0_waited = True
            lds = b.dma("sync", wbf[i][:, :].rearrange("p (cc j) -> p cc j", j=128),
                        srcv[:, :, fbase:fbase + 128],
                        wait=w0 if w0 else None, inc=sem)
            tok2 = (sem, lds)
            b._wconv[i] = None
            b._wdirect[i] = tok2
            return i, tok2
        srcv = srcT.rearrange("(cc p) f -> p cc f", p=128)
        lds = b.dma("sync", wstage[i][:, :].rearrange("p (cc j) -> p cc j", j=128),
                    srcv[:, :, fbase:fbase + 128],
                    wait=[("ve", b._wconv[i])] if b._wconv[i] else None, inc=sem)
        vtok = b.on("vector",
                    lambda e, o=wbf[i][:, :], s=wstage[i][:, :]: e.tensor_copy(out=o, in_=s),
                    wait=[(sem, lds)] + ([tok] if tok else []), inc="ve")
        b._wconv[i] = vtok
        return i, ("ve", vtok)

    b._wconv = [None, None]
    b._wdirect = [None, None]
    b._cast0_waited = False

    for h in range(NH):
        tbase = h * TH
        stgs = [zbuf, stg2]
        stg_tok = [("ve", b.cnt["ve"]), ("ve", b.cnt["ve"])]
        for cc in range(8):
            s = stgs[cc % 2]
            sem = f"lx{cc % 2}"
            ldc = b.dma("sync", s[:, 0:TH],
                        x0T[cc * 128:(cc + 1) * 128, tbase:tbase + TH],
                        wait=[stg_tok[cc % 2]], inc=sem)
            t = b.on("vector", lambda e, o=x0b[:, cc * TH:(cc + 1) * TH], src=s[:, 0:TH]:
                     e.tensor_copy(out=o, in_=src),
                     wait=[(sem, ldc)], inc="ve")
            stg_tok[cc % 2] = ("ve", t)
        x0b_ready = b.cnt["ve"]

        def part1(j):
            par = j % 2
            qb, kb, kvb = qb_[par], kb_[par], kvb_[par]
            bufs = {"q": qb, "k": kb, "v": kvb}
            for s_name, sbase in (("q", 0), ("k", 1024), ("v", 2048)):
                wi, wtok = load_w_tiles(wqkvT, sbase + j * 128, wqkvb)
                for tt in range(NTT):
                    pi, pt, ptok = pmm_rot.get()
                    for cc in range(8):
                        w = None
                        if cc == 0:
                            w = [wtok, ("ve", x0b_ready)] + ([ptok] if ptok else [])
                        b.on("tensor",
                             lambda e, p=pt[:, 0:TT], l=wbf[wi][:, cc * 128:(cc + 1) * 128],
                                    r=x0b[:, cc * TH + tt * TT: cc * TH + (tt + 1) * TT],
                                    st=(cc == 0), sp=(cc == 7):
                             e.matmul(p, l, r, start=st, stop=sp),
                             wait=w, inc="mm" if cc == 7 else None)
                    mm_done = b.cnt["mm"]
                    wrot.tok[wi] = ("mm", mm_done)
                    dst = bufs[s_name][:, tt * TT:(tt + 1) * TT]
                    if s_name == "v":
                        b.on("scalar",
                             lambda e, o=vfull[:, j * TH + tt * TT: j * TH + (tt + 1) * TT],
                                    i=pt[:, 0:TT]: e.copy(out=o, in_=i),
                             wait=[("mm", mm_done)], inc="sc")
                        vt = b.on("vector",
                                  lambda e, o=kvb[:, tt * TT:(tt + 1) * TT], i=pt[:, 0:TT],
                                         k_=kb[:, tt * TT:(tt + 1) * TT]:
                                  e.tensor_tensor(out=o, in0=i, in1=k_, op=OP.mult),
                                  wait=[("mm", mm_done), ("sc", b.cnt["sc"])], inc="ve")
                        pmm_rot.tok[pi] = ("ve", vt)
                    else:
                        v1 = b.on("vector",
                                  lambda e, o=tmps[:, 0:TT], i=pt[:, 0:TT]:
                                  e.tensor_scalar(out=o, in0=i, scalar1=0.0, scalar2=None,
                                                  op0=OP.min),
                                  wait=[("mm", mm_done)], inc="ve")
                        s1 = b.on("scalar",
                                  lambda e, o=tmps2[:, 0:TT], i=tmps[:, 0:TT]:
                                  e.activation(out=o, in_=i, func=AF.Exp),
                                  wait=[("ve", v1)], inc="sc")
                        v2 = b.on("vector",
                                  lambda e, o=dst, i=pt[:, 0:TT], t2=tmps2[:, 0:TT]:
                                  e.scalar_tensor_tensor(out=o, in0=i, scalar=0.0, in1=t2,
                                                         op0=OP.max, op1=OP.add),
                                  wait=[("sc", s1)], inc="ve")
                        pmm_rot.tok[pi] = ("ve", v2)

        def part2(j):
            par = j % 2
            qb, kb, kvb, sck, sckv = qb_[par], kb_[par], kvb_[par], sck_[par], sckv_[par]
            d_b = deccols[:, j:j + 1].broadcast_to((128, TH))
            ini_kv = 0.0 if h == 0 else carry[:, j:j + 1]
            ini_k = 0.0 if h == 0 else carry[:, 16 + j:17 + j]
            b.on("vector", lambda e, o=sckv[:, :], d0=d_b, d1=kvb[:, :], ini=ini_kv:
                 e.tensor_tensor_scan(out=o, data0=d0, data1=d1, initial=ini,
                                      op0=OP.mult, op1=OP.add), inc="ve")
            b.on("vector", lambda e, o=sck[:, :], d0=d_b, d1=kb[:, :], ini=ini_k:
                 e.tensor_tensor_scan(out=o, data0=d0, data1=d1, initial=ini,
                                      op0=OP.mult, op1=OP.add), inc="ve")
            if h < NH - 1:
                b.on("vector", lambda e, o=carry[:, j:j + 1], i=sckv[:, TH - 1:TH]:
                     e.tensor_copy(out=o, in_=i))
                b.on("vector", lambda e, o=carry[:, 16 + j:17 + j], i=sck[:, TH - 1:TH]:
                     e.tensor_copy(out=o, in_=i), inc="ve")
            b.on("vector", lambda e, o=prodb[:, :], a=qb[:, :], c=sck[:, :]:
                 e.tensor_tensor(out=o, in0=a, in1=c, op=OP.mult))
            b.on("vector", lambda e, o=qb[:, :], a=qb[:, :], c=sckv[:, :]:
                 e.tensor_tensor(out=o, in0=a, in1=c, op=OP.mult), inc="ve")
            ve_prod = b.cnt["ve"]
            for tt in range(NTT):
                mm1 = b.on("tensor", lambda e, p=pden[:, 0:TT], l=onesblk[:, :],
                                  r=prodb[:, tt * TT:(tt + 1) * TT]:
                           e.matmul(p, l, r, start=True, stop=True),
                           wait=[("ve", ve_prod)], inc="mm")
                vd = b.on("vector", lambda e, o=den2[:, 0:TT], i=pden[:, 0:TT]:
                          e.tensor_scalar(out=o, in0=i, scalar1=1e-6, scalar2=None, op0=OP.max),
                          wait=[("mm", mm1)], inc="ve")
                s1_ = b.on("scalar", lambda e, o=den2[:, 0:TT], i=den2[:, 0:TT]:
                           e.activation(out=o, in_=i, func=AF.Ln),
                           wait=[("ve", vd)], inc="sc")
                vneg = b.on("vector", lambda e, o=den2[:, 0:TT], i=den2[:, 0:TT]:
                            e.tensor_scalar(out=o, in0=i, scalar1=-1.0, scalar2=None, op0=OP.mult),
                            wait=[("sc", s1_)], inc="ve")
                vrec_s = b.on("scalar", lambda e, o=den2b[:, 0:TT], i=den2[:, 0:TT]:
                              e.activation(out=o, in_=i, func=AF.Exp),
                              wait=[("ve", vneg)], inc="sc")
                mm2 = b.on("tensor", lambda e, p=pdbc[:, 0:TT], l=onesblk_t[:, :],
                                  r=den2b[:, 0:TT]:
                           e.matmul(p, l, r, start=True, stop=True),
                           wait=[("sc", vrec_s)], inc="mm")
                b.on("vector", lambda e, o=apre[:, j * TH + tt * TT: j * TH + (tt + 1) * TT],
                            a=qb[:, tt * TT:(tt + 1) * TT], c=pdbc[:, 0:TT]:
                     e.tensor_tensor(out=o, in0=a, in1=c, op=OP.mult),
                     wait=[("mm", mm2)], inc="ve")

        for j in range(8):
            part1(j)
            if j > 0:
                part2(j - 1)
        part2(7)

        apre_ready = b.cnt["ve"]
        if getattr(b, "dbg", None) is not None and h == 0:
            for cc in range(8):
                b.dma("sync", b.dbg["apre"][cc * 128:(cc + 1) * 128, :],
                      apre[:, cc * TH:(cc + 1) * TH],
                      wait=[("ve", apre_ready)] if cc == 0 else None, inc="st")
            for cc in range(8):
                b.dma("sync", b.dbg["v"][cc * 128:(cc + 1) * 128, :],
                      vfull[:, cc * TH:(cc + 1) * TH], inc="st")
        mm_x0_free = b.cnt["mm"]
        for fc in range(8):
            wi, wtok = load_w_tiles(wgateT, fc * 128, wgateb)
            for tt in range(NTT):
                pi, pt, ptok = pmm_rot.get()
                for cc in range(8):
                    w = None
                    if cc == 0:
                        w = [wtok, ("ve", apre_ready)] + ([ptok] if ptok else [])
                    b.on("tensor",
                         lambda e, p=pt[:, 0:TT], l=wbf[wi][:, cc * 128:(cc + 1) * 128],
                                r=apre[:, cc * TH + tt * TT: cc * TH + (tt + 1) * TT],
                                st=(cc == 0), sp=(cc == 7):
                         e.matmul(p, l, r, start=st, stop=sp),
                         wait=w, inc="mm" if cc == 7 else None)
                mm_done = b.cnt["mm"]
                wrot.tok[wi] = ("mm", mm_done)
                sct = b.on("scalar",
                           lambda e, o=x0b[:, fc * TH + tt * TT: fc * TH + (tt + 1) * TT],
                                  i=pt[:, 0:TT], bias=bgate_sb[:, fc:fc + 1]:
                           e.activation(out=o, in_=i, func=AF.Sigmoid, bias=bias, scale=1.0),
                           wait=[("mm", mm_done)], inc="sc")
                pmm_rot.tok[pi] = ("sc", sct)
        g_ready = b.cnt["sc"]
        b.wait("vector", "sc", g_ready)
        for fc in range(8):
            b.on("vector", lambda e, o=mixs[:, :], a=apre[:, fc * TH:(fc + 1) * TH],
                        c=vfull[:, fc * TH:(fc + 1) * TH]:
                 e.tensor_tensor(out=o, in0=a, in1=c, op=OP.subtract))
            b.on("vector", lambda e, o=mixs[:, :], a=x0b[:, fc * TH:(fc + 1) * TH],
                        c=mixs[:, :]:
                 e.tensor_tensor(out=o, in0=a, in1=c, op=OP.mult))
            b.on("vector", lambda e, o=apre[:, fc * TH:(fc + 1) * TH], a=mixs[:, :],
                        c=vfull[:, fc * TH:(fc + 1) * TH]:
                 e.tensor_tensor(out=o, in0=a, in1=c, op=OP.add), inc="ve")
        gated_ready = b.cnt["ve"]
        if getattr(b, "dbg", None) is not None and h == 0:
            for cc in range(8):
                b.dma("sync", b.dbg["gated"][cc * 128:(cc + 1) * 128, :],
                      apre[:, cc * TH:(cc + 1) * TH],
                      wait=[("ve", gated_ready)] if cc == 0 else None, inc="st")

        ldr = b.dma("sync", x0f[:, 0:16 * TT].rearrange("p (cc th) -> p cc th", th=TH),
                    x0T.rearrange("(cc p) t -> p cc t", p=128)[:, :, tbase:tbase + TH],
                    wait=[("ve", b.cnt["ve"])], inc="lxf")
        for dc in range(8):
            wi, wtok = load_w_tiles(woutT, dc * 128, woutb)
            for tt in range(NTT):
                pi, pt, ptok = pmm_rot.get()
                for cc in range(8):
                    w = None
                    if cc == 0:
                        w = [wtok, ("ve", gated_ready)] + ([ptok] if ptok else [])
                    b.on("tensor",
                         lambda e, p=pt[:, 0:TT], l=wbf[wi][:, cc * 128:(cc + 1) * 128],
                                r=apre[:, cc * TH + tt * TT: cc * TH + (tt + 1) * TT],
                                st=(cc == 0), sp=(cc == 7):
                         e.matmul(p, l, r, start=st, stop=sp),
                         wait=w, inc="mm" if cc == 7 else None)
                mm_done = b.cnt["mm"]
                wrot.tok[wi] = ("mm", mm_done)
                vt = b.on("vector",
                          lambda e, o=zbufq[:, (tt * 8 + dc) * TT:(tt * 8 + dc + 1) * TT],
                                 a=pt[:, 0:TT],
                                 c=x0f[:, (dc * NTT + tt) * TT:(dc * NTT + tt + 1) * TT]:
                          e.tensor_tensor(out=o, in0=a, in1=c, op=OP.add),
                          wait=[("mm", mm_done), ("lxf", ldr), ("sc", b.cnt["sc"])], inc="ve")
                pmm_rot.tok[pi] = ("ve", vt)
        for tt in range(NTT):
            ve_out = emit_post_ln(b, zbufq, TT,
                                  sqtile, rows, pstat1, pstat2, pbc1, pbc2,
                                  onescol, onesrow, epscol,
                                  outf=lambda cc: x1buf[:, cc * TT:(cc + 1) * TT],
                                  zoff=tt * 8 * TT)
            stt = b.dma("sync",
                        x1T.rearrange("(cc p) t -> p cc t", p=128)[:, :,
                            tbase + tt * TT: tbase + (tt + 1) * TT],
                        x1buf[:, 0:8 * TT].rearrange("p (cc t) -> p cc t", t=TT),
                        wait=[("ve", ve_out)], inc="st")
            b.wait("vector", "st", stt)
    b.wait("sync", "st", b.cnt["st"])


def l0_attn_host_inputs(x0_b, Wqkv, Wgate, Wout, bgate, decay_param):
    d = {
        "x0T": None if x0_b is None else np.ascontiguousarray(x0_b.T),
        "wqkvT": np.ascontiguousarray(Wqkv.T),
        "wgateT": np.ascontiguousarray(Wgate.T),
        "woutT": np.ascontiguousarray(Wout.T),
        "bgate_pp": pp_col(bgate),
        "decay_p": np.ascontiguousarray(decay_param.reshape(16, 1)),
    }
    d.update(host_consts_l0())
    return d


def build_l0_attn_launch(dbg=False):
    b = B()
    x0T = b.dram_in("x0T", [D, T])
    wqkvT = b.dram_in("wqkvT", [D, 3 * D])
    wgateT = b.dram_in("wgateT", [D, D])
    woutT = b.dram_in("woutT", [D, D])
    bgate_pp = b.dram_in("bgate_pp", [128, 8])
    decay_p = b.dram_in("decay_p", [16, 1])
    consts = {k: b.dram_in(k, v.shape) for k, v in host_consts_l0().items()}
    x1T = b.dram_out("x1T", [D, T])
    if dbg:
        b.dbg = {k: b.dram_out("dbg_" + k, [D, 1024], BF16) for k in ("apre", "v", "gated")}
    else:
        b.dbg = None
    emit_l0_attn(b, x0T, x1T, wqkvT, wgateT, woutT, bgate_pp, decay_p, consts)
    return b.finalize()


def host_consts_ffn():
    import ml_dtypes
    esel = np.zeros((8, 8 * 128), ml_dtypes.bfloat16)
    for e in range(8):
        esel[e, e * 128:(e + 1) * 128] = 1.0
    return {"c_esel": esel}


def emit_l0_ffn(b, x1T, x2T, wg, w1, b1pp, w2, b2, w1b, w2b, consts):
    b.presem()
    TT = 512
    NTT = 8

    x1f_ = [b.sb(f"F_x1f{i}", [128, 8 * TT], F32) for i in range(2)]
    x1bt_ = [b.sb(f"F_x1bt{i}", [128, 8 * TT], BF16) for i in range(2)]
    y_acc = b.sb("F_yacc", [128, 8 * TT], F32)
    zbuf = b.sb("F_zbuf", [128, 8 * TT], F32)
    x2buf = b.sb("F_x2buf", [128, 8 * TT], F32)
    hbuf = b.sb("F_hbuf", [128, 2 * 16 * TT], BF16)
    hrel = b.sb("F_hrel", [128, TT], BF16)
    wsb1 = [b.sb(f"F_wsb1_{i}", [128, 1024], BF16) for i in range(3)]
    wsb2 = b.sb("F_wsb2", [128, 16 * 1024], BF16)
    wgsb = b.sb("F_wgsb", [128, 64], F32)
    wgb = b.sb("F_wgb", [128, 64], BF16)
    b1sb = b.sb("F_b1sb", [128, 128], F32)
    b2sb = b.sb("F_b2sb", [8, 1024], F32)
    b2b = b.sb("F_b2b", [8, 1024], BF16)
    eselm = b.sb("F_eselm", [8, 8 * 128], BF16)
    wT8b = b.sb("F_wT8b", [8, T], BF16)
    eselmb = eselm
    wbc = b.sb("F_wbc", [128, 8 * TT], BF16)
    probs = b.sb("F_probs", [128, 8], F32)
    pmsk = b.sb("F_pmsk", [128, 8], F32)
    wtok = b.sb("F_wtok", [128, 8], F32)
    mxc = b.sb("F_mxc", [128, 1], F32)
    sumc = b.sb("F_sumc", [128, 1], F32)
    recc = b.sb("F_recc", [128, 1], F32)
    mx1c = b.sb("F_mx1c", [128, 1], F32)
    mx2c = b.sb("F_mx2c", [128, 1], F32)
    ones128f = b.sb("F_ones128f", [128, 128], F32)
    identf = b.sb("F_identf", [128, 128], F32)
    onescol = b.sb("F_onescol", [128, 1], BF16)
    onesrow = b.sb("F_onesrow", [1, 128], BF16)
    epscol = b.sb("F_epscol", [128, 1], F32)
    sqtile = b.sb("F_sqtile", [128, TT], BF16)
    b._ln_zbf = b.sb("F_lnzbf", [128, TT], BF16)
    b._ln_rows_bf = b.sb("F_lnrbf", [1, 2 * TT], BF16)
    rows = b.sb("F_rows", [1, 4 * TT], F32)

    ph = [b.ps(f"F_ph{i}", [128, TT]) for i in range(2)]
    py = [b.ps(f"F_py{i}", [128, TT]) for i in range(2)]
    pstat1 = b.ps("F_pstat1", [1, TT])
    pstat2 = b.ps("F_pstat2", [1, TT])
    pbc1 = b.ps("F_pbc1", [128, TT])
    pbc2 = b.ps("F_pbc2", [128, TT])
    ptw = pbc2

    ph_rot = Rot(ph)
    py_rot = Rot(py)
    w1_rot = Rot(list(range(3)))

    for cc in range(8):
        b.dma("sync", wgsb[:, cc * 8:(cc + 1) * 8], wg[cc * 128:(cc + 1) * 128, :], inc="lc")
    b.dma("sync", b1sb[:, :], b1pp[:, :], inc="lc")
    b.dma("sync", b2sb[:, :], b2[:, :], inc="lc")
    b.dma("sync", eselm[:, :], consts["c_esel"][:, :], inc="lc")
    lc_tot = b.cnt["lc"]
    b.on("vector", lambda e, o=onescol[:, :]: e.memset(o, 1.0))
    b.on("vector", lambda e, o=onesrow[:, :]: e.memset(o, 1.0))
    b.on("vector", lambda e, o=epscol[:, :]: e.memset(o, EPS))
    b.on("vector", lambda e, o=ones128f[:, :]: e.memset(o, 1.0), inc="ve")
    b.on("vector", lambda e, o=wgb[:, :], i=wgsb[:, :]: e.tensor_copy(out=o, in_=i),
         wait=[("lc", lc_tot)])
    b.on("vector", lambda e, o=b2b[:, :], i=b2sb[:, :]: e.tensor_copy(out=o, in_=i), inc="ve")
    ve_consts = b.cnt["ve"]
    gp_id = b.on("gpsimd", lambda e, o=identf[:, :], i=ones128f[:, :]:
                 e.affine_select(out=o, in_=i, pattern=[[1, 128]], compare_op=OP.is_equal,
                                 fill=0.0, base=0, channel_multiplier=-1),
                 wait=[("ve", ve_consts)], inc="gp")

    for e_i in range(8):
        for fcl in range(16):
            src = w1[e_i, :, :].rearrange("(cc p) (fcl j) -> fcl p cc j", p=128, j=128)
            b.dma("gpsimd", w1b[e_i, fcl, :, :], src[fcl, :, :, :], inc="cast")
        for fcl in range(16):
            b.dma("gpsimd", w2b[e_i, fcl, :, :], w2[e_i, fcl * 128:(fcl + 1) * 128, :],
                  inc="cast")
    cast_tot = b.cnt["cast"]

    x1_cons = [None, None]
    for tt in range(NTT):
        ts = tt * TT
        par = tt % 2
        x1f = x1f_[par]
        x1bt = x1bt_[par]
        hb_off = par * 16 * TT
        lx = b.dma("sync", x1f[:, 0:8 * TT].rearrange("p (cc t) -> p cc t", t=TT),
                   x1T.rearrange("(cc p) t -> p cc t", p=128)[:, :, ts:ts + TT],
                   wait=([x1_cons[par]] if x1_cons[par] else None), inc=f"lx{par}")
        cv = b.on("vector", lambda e, o=x1bt[:, :], i=x1f[:, :]: e.tensor_copy(out=o, in_=i),
                  wait=[(f"lx{par}", lx)], inc="ve")
        x1_ready = cv

        for tc in range(4):
            pli, plt, pltok = ph_rot.get()
            pl = plt[:, 0:8]
            mmw = [("ve", x1_ready), (f"lx{par}", lx), ("lc", lc_tot)]
            if pltok:
                mmw.append(pltok)
            for cc in range(8):
                b.on("tensor",
                     lambda e, p=pl, l=x1f[:, cc * TT + tc * 128: cc * TT + (tc + 1) * 128],
                            r=wgsb[:, cc * 8:(cc + 1) * 8],
                            st=(cc == 0), sp=(cc == 7): e.matmul(p, l, r, start=st, stop=sp),
                     wait=mmw if cc == 0 else None, inc="mm" if cc == 7 else None)
            mml = b.cnt["mm"]
            t = b.on("vector", lambda e, o=mxc[:, 0:1], i=pl:
                     e.reduce_max(out=o, in_=i, axis=AX, negate=True),
                     wait=[("mm", mml)], inc="ve")
            sct = b.on("scalar", lambda e, o=probs[:, :], i=pl, bi=mxc[:, 0:1]:
                       e.activation(out=o, in_=i, func=AF.Exp, bias=bi, scale=1.0),
                       wait=[("ve", t)], inc="sc")
            ph_rot.tok[pli] = ("sc", sct)
            t = b.on("vector", lambda e, o=sumc[:, 0:1], i=probs[:, :]:
                     e.reduce_sum(out=o, in_=i, axis=AX), wait=[("sc", sct)], inc="ve")
            t = b.on("vector", lambda e, o=recc[:, 0:1], i=sumc[:, 0:1]:
                     e.reciprocal(out=o, in_=i), wait=[("ve", t)], inc="ve")
            t = b.on("vector", lambda e, o=probs[:, :], i=probs[:, :], s=recc[:, 0:1]:
                     e.tensor_scalar(out=o, in0=i, scalar1=s, scalar2=None, op0=OP.mult),
                     wait=[("ve", t)], inc="ve")
            t = b.on("vector", lambda e, o=mx1c[:, 0:1], i=probs[:, :]:
                     e.reduce_max(out=o, in_=i, axis=AX), wait=[("ve", t)], inc="ve")
            t = b.on("vector", lambda e, o=pmsk[:, :], i=probs[:, :], s=mx1c[:, 0:1]:
                     e.tensor_scalar(out=o, in0=i, scalar1=s, scalar2=None, op0=OP.is_ge),
                     wait=[("ve", t)], inc="ve")
            t = b.on("vector", lambda e, o=pmsk[:, :], i=pmsk[:, :], p2=probs[:, :]:
                     e.scalar_tensor_tensor(out=o, in0=i, scalar=-1e30, in1=p2,
                                            op0=OP.mult, op1=OP.add),
                     wait=[("ve", t)], inc="ve")
            t = b.on("vector", lambda e, o=mx2c[:, 0:1], i=pmsk[:, :]:
                     e.reduce_max(out=o, in_=i, axis=AX), wait=[("ve", t)], inc="ve")
            t = b.on("vector", lambda e, o=pmsk[:, :], i=probs[:, :], s=mx2c[:, 0:1]:
                     e.tensor_scalar(out=o, in0=i, scalar1=s, scalar2=None, op0=OP.is_ge),
                     wait=[("ve", t)], inc="ve")
            vw = b.on("vector", lambda e, o=wtok[:, :], a=probs[:, :], c=pmsk[:, :]:
                      e.tensor_tensor(out=o, in0=a, in1=c, op=OP.mult),
                      wait=[("ve", t)], inc="ve")
            mmtr = b.on("tensor", lambda e, p=ptw[0:8, 0:128], i=wtok[:, :], idn=identf[:, :]:
                        e.transpose(p, i, idn),
                        wait=[("ve", vw), ("gp", gp_id), ("ve", b.cnt["ve"])], inc="mm")
            b.on("vector", lambda e, o=wT8b[0:8, ts + tc * 128: ts + (tc + 1) * 128],
                        i=ptw[0:8, 0:128]: e.tensor_copy(out=o, in_=i),
                 wait=[("mm", mmtr)], inc="ve")
        w_ready = b.cnt["ve"]

        for e_i in range(8):
            mmb = b.on("tensor",
                       lambda e, p=pbc1[:, 0:TT], l=eselmb[0:8, e_i * 128:(e_i + 1) * 128],
                              r=wT8b[0:8, ts:ts + TT]: e.matmul(p, l, r, start=True, stop=True),
                       wait=[("ve", w_ready), ("ve", b.cnt["ve"])], inc="mm")
            b.on("vector", lambda e, o=wbc[:, e_i * TT:(e_i + 1) * TT], i=pbc1[:, 0:TT]:
                 e.tensor_copy(out=o, in_=i), wait=[("mm", mmb)], inc="ve")

        y_mm_prev = None
        if not hasattr(b, "_hb_guard"):
            b._hb_guard = {0: None, 1: None}
        for e_i in range(8):
            for fcl in range(16):
                wi, _, wtk = w1_rot.get()
                sem = f"lw{wi}"
                ldw = b.dma("sync", wsb1[wi][:, :], w1b[e_i, fcl, :, :],
                            wait=[wtk, ("cast", cast_tot)] if wtk or True else None, inc=sem)
                pi, pt, ptok = ph_rot.get()
                for cc in range(8):
                    w = None
                    if cc == 0:
                        w = [(sem, ldw), ("ve", x1_ready)] + ([ptok] if ptok else [])
                    b.on("tensor",
                         lambda e, p=pt[:, 0:TT], l=wsb1[wi][:, cc * 128:(cc + 1) * 128],
                                r=x1bt[:, cc * TT:(cc + 1) * TT],
                                st=(cc == 0), sp=(cc == 7): e.matmul(p, l, r, start=st, stop=sp),
                         wait=w, inc="mm" if cc == 7 else None)
                mmh = b.cnt["mm"]
                w1_rot.tok[wi] = ("mm", mmh)
                sct = b.on("scalar",
                           lambda e, o=hrel[:, 0:TT], i=pt[:, 0:TT],
                                  bi=b1sb[:, e_i * 16 + fcl: e_i * 16 + fcl + 1]:
                           e.activation(out=o, in_=i, func=AF.Relu, bias=bi, scale=1.0),
                           wait=[("mm", mmh), ("ve", b.cnt["ve"])], inc="sc")
                hbg = b._hb_guard[(e_i % 2)]
                vh = b.on("vector",
                          lambda e, o=hbuf[:, (e_i % 2) * 16 * TT + fcl * TT:
                                              (e_i % 2) * 16 * TT + (fcl + 1) * TT],
                                 a=hrel[:, 0:TT],
                                 c=wbc[:, e_i * TT:(e_i + 1) * TT]:
                          e.tensor_tensor(out=o, in0=a, in1=c, op=OP.mult),
                          wait=[("sc", sct)] + ([hbg] if hbg else []), inc="ve")
                ph_rot.tok[pi] = ("sc", sct)
            h_ready = b.cnt["ve"]
            lw2 = None
            for fcl in range(16):
                lw2 = b.dma("sync", wsb2[:, fcl * 1024:(fcl + 1) * 1024], w2b[e_i, fcl, :, :],
                            wait=([y_mm_prev] if (fcl == 0 and y_mm_prev) else None) if fcl == 0 else None,
                            inc="lw2")
            for dc in range(8):
                pi, pt, ptok = py_rot.get()
                first = True
                if e_i == 0:
                    b.on("tensor",
                         lambda e, p=pt[:, 0:TT], l=b2b[0:8, dc * 128:(dc + 1) * 128],
                                r=wT8b[0:8, ts:ts + TT]: e.matmul(p, l, r, start=True, stop=False),
                         wait=[("lw2", lw2), ("ve", h_ready), ("ve", w_ready)] + ([ptok] if ptok else []),
                         )
                    first = False
                for fcl in range(16):
                    w = None
                    if first and fcl == 0:
                        w = [("lw2", lw2), ("ve", h_ready)] + ([ptok] if ptok else [])
                    b.on("tensor",
                         lambda e, p=pt[:, 0:TT],
                                l=wsb2[:, fcl * 1024 + dc * 128: fcl * 1024 + (dc + 1) * 128],
                                r=hbuf[:, (e_i % 2) * 16 * TT + fcl * TT:
                                          (e_i % 2) * 16 * TT + (fcl + 1) * TT],
                                st=(first and fcl == 0), sp=(fcl == 15):
                         e.matmul(p, l, r, start=st, stop=sp),
                         wait=w, inc="mm" if fcl == 15 else None)
                mmy = b.cnt["mm"]
                if e_i == 0:
                    vy = b.on("vector", lambda e, o=y_acc[:, dc * TT:(dc + 1) * TT], i=pt[:, 0:TT]:
                              e.tensor_copy(out=o, in_=i), wait=[("mm", mmy)], inc="ve")
                else:
                    vy = b.on("vector", lambda e, o=y_acc[:, dc * TT:(dc + 1) * TT],
                                     a=y_acc[:, dc * TT:(dc + 1) * TT], i=pt[:, 0:TT]:
                              e.tensor_tensor(out=o, in0=a, in1=i, op=OP.add),
                              wait=[("mm", mmy)], inc="ve")
                py_rot.tok[pi] = ("ve", vy)
            y_mm_prev = ("mm", b.cnt["mm"])
            b._hb_guard[e_i % 2] = ("mm", b.cnt["mm"])

        if getattr(b, "dbgF", None) is not None and tt == 0:
            for dc in range(8):
                b.dma("sync", b.dbgF["y"][dc * 128:(dc + 1) * 128, :],
                      y_acc[:, dc * TT:(dc + 1) * TT],
                      wait=[("ve", b.cnt["ve"])] if dc == 0 else None, inc="st")
        for dc in range(8):
            b.on("vector", lambda e, o=zbuf[:, dc * TT:(dc + 1) * TT],
                        a=y_acc[:, dc * TT:(dc + 1) * TT], c=x1f[:, dc * TT:(dc + 1) * TT]:
                 e.tensor_tensor(out=o, in0=a, in1=c, op=OP.add),
                 wait=[("sc", b.cnt["sc"])] if dc == 0 else None, inc="ve" if dc == 7 else None)
        x1_cons[par] = ("ve", b.cnt["ve"])
        ve_out = emit_post_ln(b, zbuf, TT, sqtile, rows, pstat1, pstat2, pbc1, pbc2,
                              onescol, onesrow, epscol,
                              outf=lambda cc: x2buf[:, cc * TT:(cc + 1) * TT])
        stt = b.dma("sync",
                    x2T.rearrange("(cc p) t -> p cc t", p=128)[:, :, ts:ts + TT],
                    x2buf[:, 0:8 * TT].rearrange("p (cc t) -> p cc t", t=TT),
                    wait=[("ve", ve_out)], inc="st")
        b.wait("vector", "st", stt)
    if getattr(b, "dbgF", None) is not None:
        b.dma("sync", b.dbgF["w"][:, :], wT8b[:, :], inc="st")
    b.wait("sync", "st", b.cnt["st"])


def l0_ffn_host_inputs(Wg, W1, b1, W2, b2):
    b1pp = np.concatenate([pp_col(b1[e]) for e in range(8)], axis=1)
    d = {"wg": np.ascontiguousarray(Wg), "w1": np.ascontiguousarray(W1),
         "b1pp": np.ascontiguousarray(b1pp), "w2": np.ascontiguousarray(W2),
         "b2": np.ascontiguousarray(b2)}
    d.update(host_consts_ffn())
    return d


def build_l0_ffn_launch(dbg=False):
    b = B()
    x1T = b.dram_in("x1T", [D, T])
    wg = b.dram_in("wg", [D, 8])
    w1 = b.dram_in("w1", [8, D, 2048])
    b1pp = b.dram_in("b1pp", [128, 128])
    w2 = b.dram_in("w2", [8, 2048, D])
    b2 = b.dram_in("b2", [8, D])
    consts = {k: b.dram_in(k, v.shape, BF16) for k, v in host_consts_ffn().items()}
    w1b = b.dram_tmp("w1b", [8, 16, 128, 1024], BF16)
    wqkvb = b.dram_tmp("wqkvb", [D, 3 * D], BF16)
    wgateb = b.dram_tmp("wgateb", [D, D], BF16)
    woutb = b.dram_tmp("woutb", [D, D], BF16)
    w2b = b.dram_tmp("w2b", [8, 16, 128, 1024], BF16)
    x2T = b.dram_out("x2T", [D, T])
    if dbg:
        b.dbgF = {"w": b.dram_out("dbgF_w", [8, T], BF16),
                  "y": b.dram_out("dbgF_y", [D, 512], F32)}
    else:
        b.dbgF = None
    emit_l0_ffn(b, x1T, x2T, wg, w1, b1pp, w2, b2, w1b, w2b, consts)
    return b.finalize()


def emit_l1_attn(b, x2T, x3T, wqkv, bqkv_pp, wproj, bproj_pp, attn_dram):
    b.presem()
    TT = 512
    NTT = 8
    b.identb = b.sb("G_identb", [128, 128], BF16)
    _onesb = b.sb("G_onesb", [128, 128], BF16)
    b.on("vector", lambda e, o=_onesb[:, :]: e.memset(o, 1.0), inc="ve")
    b.on("gpsimd", lambda e, o=b.identb[:, :], i=_onesb[:, :]:
         e.affine_select(out=o, in_=i, pattern=[[1, 128]], compare_op=OP.is_equal,
                         fill=0.0, base=0, channel_multiplier=-1),
         wait=[("ve", b.cnt["ve"])], inc="gp")
    b._identb_gp = b.cnt["gp"]

    x2b = b.sb("G_x2b", [128, 8 * T], BF16)
    qTb = b.sb("G_qTb", [128, T], BF16)
    kTb = b.sb("G_kTb", [128, T], BF16)
    vtmp = b.sb("G_vtmp", [128, T], BF16)
    vtok = b.sb("G_vtok", [128, 2 * 32 * 65], BF16)
    ptb = [b.sb(f"G_ptb{i}", [128, TT], BF16) for i in range(4)]
    obuf = b.sb("G_obuf", [64, TT], F32)
    rec = b.sb("G_rec", [1, TT], F32)
    recb = b.sb("G_recb", [1, TT], BF16)
    atile = b.sb("G_atile", [128, TT], BF16)
    attsb = b.sb("G_attsb", [128, 8 * TT], BF16)
    wstage = [b.sb(f"G_wstage{i}", [128, 1024], F32) for i in range(2)]
    wbf = [b.sb(f"G_wbf{i}", [128, 1024], BF16) for i in range(2)]
    bqkv_sb = b.sb("G_bqkv", [128, 24], F32)
    bqs_sb = b.sb("G_bqs", [128, 8], F32)
    bproj_sb = b.sb("G_bproj", [128, 8], F32)
    stg = [b.sb(f"G_stg{i}", [128, 2048], F32) for i in range(2)]
    x2f = b.sb("G_x2f", [128, 8 * TT], F32)
    zbuf = b.sb("G_zbuf", [128, 8 * TT], F32)
    x3buf = b.sb("G_x3buf", [128, 8 * TT], F32)
    onescol = b.sb("G_onescol", [128, 1], BF16)
    onesrow = b.sb("G_onesrow", [1, 128], BF16)
    epscol = b.sb("G_epscol", [128, 1], F32)
    sqtile = b.sb("G_sqtile", [128, TT], BF16)
    b._ln_zbf = b.sb("G_lnzbf", [128, TT], BF16)
    b._ln_rows_bf = b.sb("G_lnrbf", [1, 2 * TT], BF16)
    rows = b.sb("G_rows", [1, 4 * TT], F32)

    pS = [b.ps(f"G_pS{i}", [128, TT]) for i in range(2)]
    b.push_scope()
    pS2 = [b.ps(f"G_pS2_{i}", [128, TT]) for i in range(2)]
    pO_ = [b.ps(f"G_pO{i}", [65, TT]) for i in range(2)]
    pB = b.ps("G_pB", [64, TT])
    pTR = b.ps("G_pTR", [128, 128], BF16)

    pS_rot = Rot(pS)
    wrot = Rot(list(range(2)))
    b._wconvG = [None, None]
    b._sa_tok = {}

    pSatt_rot = Rot(pS + pS2)
    b.dma("sync", bqkv_sb[:, :], bqkv_pp[:, :], inc="lc")
    b.dma("sync", bproj_sb[:, :], bproj_pp[:, :], inc="lc")
    lc_tot = b.cnt["lc"]
    b.on("vector", lambda e, o=onescol[:, :]: e.memset(o, 1.0))
    b.on("vector", lambda e, o=onesrow[:, :]: e.memset(o, 1.0))
    b.on("vector", lambda e, o=epscol[:, :]: e.memset(o, EPS))
    b.on("vector", lambda e, o=bqs_sb[:, :], i=bqkv_sb[:, 0:8]:
         e.tensor_scalar(out=o, in0=i, scalar1=0.125, scalar2=None, op0=OP.mult),
         wait=[("lc", lc_tot)], inc="ve")
    ve_consts = b.cnt["ve"]

    stg_tok = [("ve", b.cnt["ve"]), ("ve", b.cnt["ve"])]
    for cc in range(8):
        for half in range(2):
            i = (cc * 2 + half) % 2
            sem = f"lx{i}"
            ldc = b.dma("sync", stg[i][:, :],
                        x2T[cc * 128:(cc + 1) * 128, half * 2048:(half + 1) * 2048],
                        wait=[stg_tok[i]], inc=sem)
            t = b.on("vector",
                     lambda e, o=x2b[:, cc * T + half * 2048: cc * T + (half + 1) * 2048],
                            s=stg[i][:, :]: e.tensor_copy(out=o, in_=s),
                     wait=[(sem, ldc)], inc="ve")
            stg_tok[i] = ("ve", t)
    x2_ready = b.cnt["ve"]

    def load_w_tiles(srcT, fbase):
        i, _, tok = wrot.get()
        sem = f"lw{i}"
        srcv = srcT.rearrange("(cc p) f -> p cc f", p=128)
        lds = b.dma("sync", wstage[i][:, :].rearrange("p (cc j) -> p cc j", j=128),
                    srcv[:, :, fbase:fbase + 128],
                    wait=[("ve", b._wconvG[i])] if b._wconvG[i] else None, inc=sem)
        vtok_ = b.on("vector",
                     lambda e, o=wbf[i][:, :], s=wstage[i][:, :]: e.tensor_copy(out=o, in_=s),
                     wait=[(sem, lds)] + ([tok] if tok else []), inc="ve")
        b._wconvG[i] = vtok_
        return i, vtok_

    for hg in range(8):
        vtok_ms = b.on("vector", lambda e, o=vtok[:, :]: e.memset(o, 1.0),
                       wait=[("mm", b.cnt["mm"])], inc="ve")
        for s_i, s_name in ((0, "q"), (1, "k"), (2, "v")):
            wi, wtok = load_w_tiles(wqkv, s_i * 1024 + hg * 128)
            for tt in range(NTT):
                pi, pt, ptok = pSatt_rot.get()
                for cc in range(8):
                    w = None
                    if cc == 0:
                        w = [("ve", wtok), ("ve", x2_ready)] + ([ptok] if ptok else [])
                    b.on("tensor",
                         lambda e, p=pt[:, 0:TT], l=wbf[wi][:, cc * 128:(cc + 1) * 128],
                                r=x2b[:, cc * T + tt * TT: cc * T + (tt + 1) * TT],
                                st=(cc == 0), sp=(cc == 7): e.matmul(p, l, r, start=st, stop=sp),
                         wait=w, inc="mm" if cc == 7 else None)
                mm_done = b.cnt["mm"]
                wrot.tok[wi] = ("mm", mm_done)
                if s_name == "q":
                    vet = b.on("vector",
                               lambda e, o=qTb[:, tt * TT:(tt + 1) * TT], i=pt[:, 0:TT],
                                      bi=bqs_sb[:, hg:hg + 1].broadcast_to((128, TT)):
                               e.scalar_tensor_tensor(out=o, in0=i, scalar=0.125, in1=bi,
                                                      op0=OP.mult, op1=OP.add),
                               wait=[("mm", mm_done), ("ve", ve_consts)], inc="ve")
                else:
                    dst = kTb if s_name == "k" else vtmp
                    bias = bqkv_sb[:, 8 + hg:9 + hg] if s_name == "k" else bqkv_sb[:, 16 + hg:17 + hg]
                    vet = b.on("vector",
                               lambda e, o=dst[:, tt * TT:(tt + 1) * TT], i=pt[:, 0:TT], bi=bias:
                               e.tensor_scalar(out=o, in0=i, scalar1=bi, scalar2=None, op0=OP.add),
                               wait=[("mm", mm_done), ("ve", ve_consts)], inc="ve")
                pSatt_rot.tok[pi] = ("ve", vet)
        qkv_done = b.cnt["ve"]
        tr_prev = [None, None]
        for h2 in range(2):
            for kc in range(32):
                half = kc % 2
                mmt = b.on("tensor",
                           lambda e, p=pTR[:, half * 64:half * 64 + 64],
                                  i=vtmp[h2 * 64:(h2 + 1) * 64, kc * 128:(kc + 1) * 128],
                                  idn=b.identb[h2 * 64:h2 * 64 + 64, h2 * 64:h2 * 64 + 64]: e.transpose(p, i, idn),
                           wait=[("ve", qkv_done), ("ve", vtok_ms), ("gp", b._identb_gp)] + ([tr_prev[half]] if tr_prev[half] else []),
                           inc="mm")
                vtr = b.on("vector",
                           lambda e, o=vtok[:, (h2 * 32 + kc) * 65:(h2 * 32 + kc) * 65 + 64],
                                  i=pTR[:, half * 64:half * 64 + 64]: e.tensor_copy(out=o, in_=i),
                           wait=[("mm", mmt)], inc="ve")
                tr_prev[half] = ("ve", vtr)
        vtok_ready = b.cnt["ve"]

        if not isinstance(getattr(b, "_pO_tok", None), dict):
            b._pO_tok = {}
        for qb in range(NTT):
            def emit_S(h2, kc):
                hb = h2 * 64
                pi, pt, ptok = pSatt_rot.get()
                w = [("ve", vtok_ready)] if kc == 0 else []
                if ptok:
                    w.append(ptok)
                b.on("tensor",
                     lambda e, p=pt[:, 0:TT],
                            l=kTb[hb:hb + 64, kc * 128:(kc + 1) * 128],
                            r=qTb[hb:hb + 64, qb * TT:(qb + 1) * TT]:
                     e.matmul(p, l, r, start=True, stop=True),
                     wait=w if w else None, inc="mm")
                return pi, pt, b.cnt["mm"]

            def emit_exp(h2, kc, pi, pt, mmS):
                pj = (kc % 2) * 2 + h2
                sct = b.on("scalar",
                           lambda e, o=ptb[pj][:, 0:TT], i=pt[:, 0:TT]:
                           e.activation(out=o, in_=i, func=AF.Exp),
                           wait=[("mm", mmS)], inc="sc")
                pSatt_rot.tok[pi] = ("sc", sct)
                return sct

            def emit_O(h2, kc, sct):
                pj = (kc % 2) * 2 + h2
                w2 = [("sc", sct)]
                if kc == 0 and b._pO_tok.get(h2):
                    w2.append(b._pO_tok[h2])
                b.on("tensor",
                     lambda e, p=pO_[h2][:, 0:TT],
                            l=vtok[:, (h2 * 32 + kc) * 65:(h2 * 32 + kc + 1) * 65],
                            r=ptb[pj][:, 0:TT],
                            st=(kc == 0), sp=(kc == 31): e.matmul(p, l, r, start=st, stop=sp),
                     wait=w2, inc="mm" if kc == 31 else None)
                return b.cnt["mm"]

            scts = {}
            pis = {}
            for h2 in range(2):
                pis[(h2, 0)] = emit_S(h2, 0)
                scts[(h2, 0)] = emit_exp(h2, 0, *pis[(h2, 0)])
            for kc in range(1, 32):
                for h2 in range(2):
                    pis[(h2, kc)] = emit_S(h2, kc)
                for h2 in range(2):
                    emit_O(h2, kc - 1, scts[(h2, kc - 1)])
                for h2 in range(2):
                    scts[(h2, kc)] = emit_exp(h2, kc, *pis[(h2, kc)])
            mmOs = {}
            for h2 in range(2):
                mmOs[h2] = emit_O(h2, 31, scts[(h2, 31)])
            for h2 in range(2):
                hb = h2 * 64
                pO = pO_[h2]
                mmO = mmOs[1]
                vr = b.on("vector", lambda e, o=rec[0:1, 0:TT], i=pO[64:65, 0:TT]:
                          e.reciprocal(out=o, in_=i), wait=[("mm", mmO)], inc="ve")
                vrb = b.on("gpsimd", lambda e, o=recb[0:1, 0:TT], i=rec[0:1, 0:TT]:
                           e.tensor_copy(out=o, in_=i), wait=[("ve", vr)], inc="gp")
                mmB = b.on("tensor", lambda e, p=pB[:, 0:TT], l=onesrow[0:1, 0:64],
                                  r=recb[0:1, 0:TT]: e.matmul(p, l, r, start=True, stop=True),
                           wait=[("gp", vrb)], inc="mm")
                vo = b.on("vector", lambda e, o=obuf[:, 0:TT], i=pO[0:64, 0:TT]:
                          e.tensor_copy(out=o, in_=i),
                          wait=[("mm", mmO), ("ve", b.cnt["ve"])], inc="ve")
                sa = f"sa{h2}"
                satok = b._sa_tok.get(sa)
                va = b.on("vector",
                          lambda e, o=atile[hb:hb + 64, 0:TT], a=obuf[:, 0:TT], c=pB[:, 0:TT]:
                          e.tensor_tensor(out=o, in0=a, in1=c, op=OP.mult),
                          wait=[("mm", mmB), ("ve", vo)] + ([satok] if satok else []), inc="ve")
                b._pO_tok[h2] = ("ve", vo)
                sat = b.dma("sync", attn_dram[hg * 128 + hb: hg * 128 + hb + 64,
                                              qb * TT:(qb + 1) * TT],
                            atile[hb:hb + 64, 0:TT], wait=[("ve", va)], inc=sa)
                b._sa_tok[sa] = (sa, sat)


    b.pop_scope()
    b.barrier()
    pS_rot = Rot(pS)
    b.push_scope()
    pstat1 = b.ps("G_pstat1", [1, TT])
    pstat2 = b.ps("G_pstat2", [1, TT])
    pbc1 = b.ps("G_pbc1", [128, TT])
    pbc2 = b.ps("G_pbc2", [128, TT])
    for tt in range(NTT):
        lds = b.dma("sync", attsb.rearrange("p (cc t) -> p cc t", t=TT),
                    attn_dram.rearrange("(cc p) t -> p cc t", p=128)[:, :, tt * TT:(tt + 1) * TT],
                    wait=[("ve", b.cnt["ve"]), ("sa0", b.cnt.get("sa0", 0)),
                          ("sa1", b.cnt.get("sa1", 0))], inc="la")
        ldx = b.dma("sync", x2f.rearrange("p (cc t) -> p cc t", t=TT),
                    x2T.rearrange("(cc p) t -> p cc t", p=128)[:, :, tt * TT:(tt + 1) * TT],
                    wait=[("ve", b.cnt["ve"])], inc="lxf")
        for dc in range(8):
            wi, wtok = load_w_tiles(wproj, dc * 128)
            pi, pt, ptok = pS_rot.get()
            for cc in range(8):
                w = None
                if cc == 0:
                    w = [("ve", wtok), ("la", lds)] + ([ptok] if ptok else [])
                b.on("tensor",
                     lambda e, p=pt[:, 0:TT], l=wbf[wi][:, cc * 128:(cc + 1) * 128],
                            r=attsb[:, cc * TT:(cc + 1) * TT],
                            st=(cc == 0), sp=(cc == 7): e.matmul(p, l, r, start=st, stop=sp),
                     wait=w, inc="mm" if cc == 7 else None)
            mm_done = b.cnt["mm"]
            wrot.tok[wi] = ("mm", mm_done)
            vt = b.on("vector",
                      lambda e, o=zbuf[:, dc * TT:(dc + 1) * TT], i=pt[:, 0:TT],
                             bi=bproj_sb[:, dc:dc + 1], c=x2f[:, dc * TT:(dc + 1) * TT]:
                      e.scalar_tensor_tensor(out=o, in0=i, scalar=bi, in1=c,
                                             op0=OP.add, op1=OP.add),
                      wait=[("mm", mm_done), ("lxf", ldx), ("sc", b.cnt["sc"])], inc="ve")
            pS_rot.tok[pi] = ("ve", vt)
        ve_out = emit_post_ln(b, zbuf, TT, sqtile, rows, pstat1, pstat2, pbc1, pbc2,
                              onescol, onesrow, epscol,
                              outf=lambda cc: x3buf[:, cc * TT:(cc + 1) * TT])
        stt = b.dma("sync",
                    x3T.rearrange("(cc p) t -> p cc t", p=128)[:, :, tt * TT:(tt + 1) * TT],
                    x3buf[:, 0:8 * TT].rearrange("p (cc t) -> p cc t", t=TT),
                    wait=[("ve", ve_out)], inc="st")
        b.wait("vector", "st", stt)
    b.pop_scope()
    b.wait("sync", "st", b.cnt["st"])


def l1_attn_host_inputs(W_qkv, b_qkv, W_proj, b_proj):
    return {"wqkv": np.ascontiguousarray(W_qkv),
            "bqkv_pp": pp_col(b_qkv),
            "wproj": np.ascontiguousarray(W_proj),
            "bproj_pp": pp_col(b_proj)}


def build_l1_attn_launch():
    b = B()
    x2T = b.dram_in("x2T", [D, T])
    wqkv = b.dram_in("wqkv", [D, 3 * D])
    bqkv_pp = b.dram_in("bqkv_pp", [128, 24])
    wproj = b.dram_in("wproj", [D, D])
    bproj_pp = b.dram_in("bproj_pp", [128, 8])
    attn_dram = b.dram_tmp("attn_dram", [D, T], BF16)
    x3T = b.dram_out("x3T", [D, T])
    emit_l1_attn(b, x2T, x3T, wqkv, bqkv_pp, wproj, bproj_pp, attn_dram)
    return b.finalize()


def butterfly_host_arrays(up_w, dn_w):
    DH = 4096

    def stage_block_mats(w, s):
        S = 2 ** s
        M = np.zeros((32, 128, 128), np.float32)
        for c in range(DH):
            blk, i = divmod(c, 128)
            if (c // S) % 2 == 0:
                k = (c // (2 * S)) * S + (c % S)
                M[blk, i, i] = w[s, k, 0, 0]
                M[blk, i, i + S] = w[s, k, 0, 1]
            else:
                cp = c - S
                k = (cp // (2 * S)) * S + (cp % S)
                M[blk, i, i] = w[s, k, 1, 1]
                M[blk, i, i - S] = w[s, k, 1, 0]
        return M

    def coeffs(w, s, transposed):
        S = 2 ** s
        A = np.zeros(DH, np.float32); Bv = np.zeros(DH, np.float32)
        for c in range(DH):
            if (c // S) % 2 == 0:
                k = (c // (2 * S)) * S + (c % S)
                A[c] = w[s, k, 0, 0]
                Bv[c] = w[s, k, 0, 1] if not transposed else w[s, k, 1, 0]
            else:
                cp = c - S
                k = (cp // (2 * S)) * S + (cp % S)
                A[c] = w[s, k, 1, 1]
                Bv[c] = w[s, k, 1, 0] if not transposed else w[s, k, 0, 1]
        return pp_col(A), pp_col(Bv)

    ups = np.stack([stage_block_mats(up_w, s) for s in range(6)])
    ups6t = np.ascontiguousarray(stage_block_mats(up_w, 6).transpose(0, 2, 1))
    dns = [stage_block_mats(dn_w, s) for s in range(7)]
    dns0 = dns[0]
    dnst = np.stack([np.ascontiguousarray(dns[s].transpose(0, 2, 1)) for s in range(1, 7)])
    cup = np.zeros((128, 320), np.float32)
    cdn = np.zeros((128, 320), np.float32)
    for s in range(7, 12):
        A, Bv = coeffs(up_w, s, False)
        cup[:, (s - 7) * 64:(s - 7) * 64 + 32] = A
        cup[:, (s - 7) * 64 + 32:(s - 7) * 64 + 64] = Bv
        A, Bv = coeffs(dn_w, s, True)
        cdn[:, (s - 7) * 64:(s - 7) * 64 + 32] = A
        cdn[:, (s - 7) * 64 + 32:(s - 7) * 64 + 64] = Bv
    return {"ups": ups, "ups6t": ups6t, "dns0": dns0, "dnst": dnst,
            "cup": cup, "cdn": cdn}


def emit_l1_ffn(b, x3T, x4T, vaug, bexp_pp, wcon, ups, ups6t, dns0, dnst, cup, cdn,
                upbias_pp, bcon_pp, dbias_pp, wexp_scr, wcon_scr):
    b.presem()
    TT = 512
    NTT = 8
    DH = 4096

    pool1 = b.sb("H_pool1", [128, 16384], F32)
    pool2 = b.sb("H_pool2", [128, 16384], F32)
    bufA = pool1
    bufB = pool2
    pool1b = pool1[:, :].bitcast(BF16)
    pool2b = pool2[:, :].bitcast(BF16)
    def wcon_full_sl(a, c):
        return pool1b[:, a:c]
    def h3_sl(a, c):
        return pool2b[:, a:c]
    zbuf = pool2[:, 8192:12288]
    x4buf = pool2[:, 12288:16384]
    chainM = b.sb("H_chainM", [128, 4096], F32)
    tb = [b.sb(f"H_tb{i}", [128, 128], BF16) for i in range(2)]
    trs = [b.sb(f"H_trs{i}", [128, 128], BF16) for i in range(2)]
    cupsb = b.sb("H_cup", [128, 320], F32)
    cdnsb = b.sb("H_cdn", [128, 320], F32)
    upbias_sb = b.sb("H_upb", [128, 32], F32)
    gbias = b.sb("H_gbias", [128, 32], F32)
    beffc = b.sb("H_beffc", [128, 32], F32)
    bcol = [b.sb(f"H_bcol{i}", [128, 32], F32) for i in range(2)]
    bcolt = b.sb("H_bcolt", [128, 1], F32)
    ybias = b.sb("H_ybias", [128, 8], F32)
    bcon_sb = b.sb("H_bcon", [128, 8], F32)
    dbias_sb = b.sb("H_dbias", [128, 8], F32)
    x3f_ = [b.sb(f"H_x3f{i}", [128, 8 * TT], F32) for i in range(2)]
    x3bt_ = [b.sb(f"H_x3bt{i}", [128, 8 * TT], BF16) for i in range(2)]
    ttile = x3f_[1][:, 0:TT]
    ttile2 = x3f_[1][:, TT:2 * TT]
    scscr = [x3f_[1][:, 2 * TT:3 * TT], x3f_[1][:, 3 * TT:4 * TT]]
    b._stg_slot = [None, None]
    wexp_sl = [b.sb(f"H_wexp{i}", [128, 1024], BF16) for i in range(2)]
    onesb = b.sb("H_onesb", [128, 128], BF16)
    identb = b.sb("H_identb", [128, 128], BF16)
    onescol = b.sb("H_onescol", [128, 1], BF16)
    onesrow = b.sb("H_onesrow", [1, 128], BF16)
    epscol = b.sb("H_epscol", [128, 1], F32)
    sqtile = chainM[:, 2048:2304].bitcast(BF16)
    b._ln_zbf = chainM[:, 2304:2560].bitcast(BF16)
    b._ln_rows_bf = chainM[0:1, 2560:3072].bitcast(BF16)
    rows = chainM[0:1, 0:2048]

    ph = [b.ps(f"H_ph{i}", [128, TT]) for i in range(2)]
    py = [b.ps(f"H_py{i}", [128, TT]) for i in range(2)]
    b.push_scope()
    pTRb = b.ps("H_pTRb", [128, 128], BF16)

    ph_rot = Rot(ph)
    py_rot = Rot(py)

    for dst, src in [(cupsb, cup), (cdnsb, cdn), (upbias_sb, upbias_pp),
                     (bcon_sb, bcon_pp), (dbias_sb, dbias_pp)]:
        b.dma("sync", dst[:, :], src[:, :], inc="lc")
    lc_tot = b.cnt["lc"]
    b.on("vector", lambda e, o=onescol[:, :]: e.memset(o, 1.0))
    b.on("vector", lambda e, o=onesrow[:, :]: e.memset(o, 1.0))
    b.on("vector", lambda e, o=epscol[:, :]: e.memset(o, EPS))
    b.on("vector", lambda e, o=onesb[:, :]: e.memset(o, 1.0), inc="ve")
    b.on("vector", lambda e, o=ybias[:, :], a=bcon_sb[:, :], c=dbias_sb[:, :]:
         e.tensor_tensor(out=o, in0=a, in1=c, op=OP.add), wait=[("lc", lc_tot)], inc="ve")
    gp_id = b.on("gpsimd", lambda e, o=identb[:, :], i=onesb[:, :]:
                 e.affine_select(out=o, in_=i, pattern=[[1, 128]], compare_op=OP.is_equal,
                                 fill=0.0, base=0, channel_multiplier=-1),
                 wait=[("ve", b.cnt["ve"])], inc="gp")

    def build_chain(dst_name, seed_arr, mats, ascending):
        ld0 = None
        for bi in range(32):
            ld0 = b.dma("sync", chainM[:, bi * 128:(bi + 1) * 128], seed_arr[bi, :, :],
                        wait=[("ve", b.cnt["ve"]), ("mm", b.cnt["mm"])] if bi == 0 else None,
                        inc="lm0")
        ve_prev = ("lm0", ld0)
        for si, mat in enumerate(mats):
            half = si % 2
            base = half * 4096
            sem = f"ls{half}"
            lds = None
            for bi in range(32):
                w0 = None
                if bi == 0:
                    w0 = [("mm", b._chain_halfmm[half])] if b._chain_halfmm[half] else None
                    if si == 0:
                        w0 = (w0 or []) + [("ve", b.cnt["ve"]), ("gp", b.cnt["gp"]),
                                           ("mm", b.cnt["mm"])]
                lds = b.dma("sync", bufA[:, base + bi * 128:base + (bi + 1) * 128],
                            mat[bi, :, :], wait=w0, inc=sem)
            for bi in range(32):
                pi, pt, ptok = ph_rot.get()
                w = [(sem, lds), ve_prev] if bi == 0 else ([ptok] if ptok else None)
                if bi == 0 and ptok:
                    w.append(ptok)
                mmt = b.on("tensor",
                           lambda e, p=pt[:, 0:128],
                                  l=bufA[:, base + bi * 128:base + (bi + 1) * 128],
                                  r=chainM[:, bi * 128:(bi + 1) * 128]:
                           e.matmul(p, l, r, start=True, stop=True),
                           wait=w, inc="mm")
                vct = b.on("scalar",
                           lambda e, o=chainM[:, bi * 128:(bi + 1) * 128], i=pt[:, 0:128]:
                           e.copy(out=o, in_=i), wait=[("mm", mmt)], inc="sc")
                ph_rot.tok[pi] = ("sc", vct)
            b._chain_halfmm[half] = b.cnt["mm"]
            ve_prev = ("sc", b.cnt["sc"])
        return b.cnt["sc"]
    b._chain_halfmm = [None, None]

    def stages_pass(src_getcol, W, bufs, cur, stage_list, coefsb, tok0):
        for s in stage_list:
            Sc = 2 ** (s - 7)
            base = (s - 7) * 64
            ve_bar = b.cnt["ve"]; gp_bar = b.cnt["gp"]; sc_bar = b.cnt["sc"]
            nsc = 0
            for cb in range(32):
                pb = cb ^ Sc
                eng = "vector" if cb % 5 != 4 else "gpsimd"
                acol = coefsb[:, base + cb:base + cb + 1]
                bcol_ = coefsb[:, base + 32 + cb:base + 33 + cb]
                w0 = [("ve", ve_bar), ("gp", gp_bar), ("sc", sc_bar)] + ([tok0] if tok0 else [])
                if eng == "vector":
                    slot = nsc % 2
                    nsc += 1
                    sg = b._stg_slot[slot]
                    sct = b.on("scalar",
                               lambda e, o=scscr[slot], i=bufs[cur](cb, W), a=acol:
                               e.activation(out=o, in_=i, func=AF.Identity, scale=a),
                               wait=w0 + ([sg] if sg else []), inc="sc")
                    vt1 = b.on("vector",
                               lambda e, o=bufs[1 - cur](cb, W), i=bufs[cur](pb, W), bc=bcol_,
                                      t=scscr[slot]:
                               e.scalar_tensor_tensor(out=o, in0=i, scalar=bc, in1=t,
                                                      op0=OP.mult, op1=OP.add),
                               wait=[("sc", sct)], inc="ve")
                    b._stg_slot[slot] = ("ve", vt1)
                else:
                    t1 = b.on(eng, lambda e, o=ttile2, i=bufs[cur](cb, W),
                                     a=acol.broadcast_to((128, W)):
                              e.tensor_tensor(out=o, in0=i, in1=a, op=OP.mult),
                              wait=w0, inc="gp")
                    t2 = b.on(eng, lambda e, o=bufs[1 - cur](cb, W), i=bufs[cur](pb, W),
                                     bc=bcol_.broadcast_to((128, W)):
                              e.tensor_tensor(out=o, in0=i, in1=bc, op=OP.mult),
                              wait=[("gp", t1)], inc="gp")
                    b.on(eng, lambda e, o=bufs[1 - cur](cb, W), a=bufs[1 - cur](cb, W),
                              t=ttile2:
                         e.tensor_tensor(out=o, in0=a, in1=t, op=OP.add),
                         wait=[("gp", t2)], inc="gp")
            tok0 = None
            cur = 1 - cur
        b.on("vector", None, wait=[("gp", b.cnt["gp"])])
        b.on("gpsimd", None, wait=[("ve", b.cnt["ve"])])
        return cur

    bufAc = lambda cb, W: bufA[:, cb * W:(cb + 1) * W]
    bufBc = lambda cb, W: bufB[:, cb * W:(cb + 1) * W]
    bufs = {0: bufAc, 1: bufBc}

    ve_chain_up = build_chain("up", ups6t, [ups[s] for s in range(5, -1, -1)], None)
    for dh in range(2):
        W = 512
        lv = None
        for bi in range(32):
            lv = b.dma("sync", bufA[:, bi * W:(bi + 1) * W],
                       vaug[bi * 128:(bi + 1) * 128, dh * 512:(dh + 1) * 512],
                       wait=[("ve", b.cnt["ve"]), ("gp", b.cnt["gp"]),
                             ("mm", b.cnt["mm"]), ("sw0", b.cnt.get("sw0", 0)),
                             ("sw1", b.cnt.get("sw1", 0))] if bi == 0 else None,
                       inc="lv")
        for bi in range(32):
            pi, pt, ptok = ph_rot.get()
            mmt = b.on("tensor",
                       lambda e, p=pt[:, 0:W], l=chainM[:, bi * 128:(bi + 1) * 128],
                              r=bufA[:, bi * W:(bi + 1) * W]:
                       e.matmul(p, l, r, start=True, stop=True),
                       wait=[("lv", lv), ("sc", ve_chain_up)] + ([ptok] if ptok else []),
                       inc="mm")
            vt = b.on("scalar", lambda e, o=bufB[:, bi * W:(bi + 1) * W], i=pt[:, 0:W]:
                      e.copy(out=o, in_=i), wait=[("mm", mmt)], inc="sc")
            ph_rot.tok[pi] = ("sc", vt)
        cur = stages_pass(None, W, bufs, 1, range(7, 12), cupsb, None)
        if not hasattr(b, "_tb_tok"):
            b._tb_tok = [None, None]
            b._trs_tok = [None, None]
            b._ptr_free = None
        for bi in range(32):
            for dq in range(4):
                i2 = (bi * 4 + dq) % 2
                vt = b.on("vector",
                          lambda e, o=tb[i2][:, :],
                                 i=bufs[cur](bi, W)[:, dq * 128:(dq + 1) * 128]:
                          e.tensor_copy(out=o, in_=i),
                          wait=[b._tb_tok[i2]] if b._tb_tok[i2] else None, inc="ve")
                mmt = b.on("tensor", lambda e, p=pTRb[:, :], i=tb[i2][:, :], idn=identb[:, :]:
                           e.transpose(p, i, idn),
                           wait=[("ve", vt), ("gp", gp_id)] +
                                ([b._ptr_free] if b._ptr_free else []), inc="mm")
                b._tb_tok[i2] = ("mm", mmt)
                vt2 = b.on("scalar", lambda e, o=trs[i2][:, :], i=pTRb[:, :]:
                           e.copy(out=o, in_=i),
                           wait=[("mm", mmt)] + ([b._trs_tok[i2]] if b._trs_tok[i2] else []),
                           inc="sc")
                b._ptr_free = ("sc", vt2)
                sem = f"sw{i2}"
                stok = b.dma("sync",
                             wexp_scr[bi, :, (dh * 4 + dq) * 128:(dh * 4 + dq + 1) * 128],
                             trs[i2][:, :], wait=[("sc", vt2)], inc=sem)
                b._trs_tok[i2] = (sem, stok)
    bcur = 0
    lvb = b.dma("sync", bcol[0][:, :], bexp_pp[:, :],
                wait=[("ve", b.cnt["ve"])], inc="lvb")
    mm_prev = None
    for bi in range(32):
        mmt = b.on("tensor",
                   lambda e, p=ph[0][:, 0:1], l=chainM[:, bi * 128:(bi + 1) * 128],
                          r=bcol[0][:, bi:bi + 1]:
                   e.matmul(p, l, r, start=True, stop=True),
                   wait=[("lvb", lvb), ("ve", b.cnt["ve"])] if bi == 0 else None, inc="mm")
        b.on("vector", lambda e, o=bcol[1][:, bi:bi + 1], i=ph[0][:, 0:1]:
             e.tensor_copy(out=o, in_=i), wait=[("mm", mmt)], inc="ve")
    bcur = 1
    for s in range(7, 12):
        Sc = 2 ** (s - 7)
        base = (s - 7) * 64
        bar = b.on("vector", lambda e, o=bcolt[:, 0:1]: e.memset(o, 0.0), inc="ve")
        for cb in range(32):
            pb = cb ^ Sc
            t1 = b.on("vector", lambda e, o=bcolt[:, 0:1], i=bcol[bcur][:, cb:cb + 1],
                             a=cupsb[:, base + cb:base + cb + 1]:
                      e.tensor_scalar(out=o, in0=i, scalar1=a, scalar2=None, op0=OP.mult),
                      wait=[("ve", b.cnt["ve"])], inc="ve")
            b.on("vector", lambda e, o=bcol[1 - bcur][:, cb:cb + 1], i=bcol[bcur][:, pb:pb + 1],
                        bc=cupsb[:, base + 32 + cb:base + 33 + cb], t=bcolt[:, 0:1]:
                 e.scalar_tensor_tensor(out=o, in0=i, scalar=bc, in1=t,
                                        op0=OP.mult, op1=OP.add),
                 wait=[("ve", t1)], inc="ve")
        bcur = 1 - bcur
    b.on("vector", lambda e, o=gbias[:, :], a=bcol[bcur][:, :], c=upbias_sb[:, :]:
         e.tensor_tensor(out=o, in0=a, in1=c, op=OP.add),
         wait=[("ve", b.cnt["ve"]), ("lc", lc_tot)], inc="ve")
    gbias_tok = b.cnt["ve"]

    wsb16 = [b.sb(f"H_wsb16_{i}", [128, 512], BF16) for i in range(2)]
    b._w16_tok = [None, None]
    ve_chain_dn = build_chain("dn", dns0, [dnst[s] for s in range(6)], None)
    for dh in range(2):
        W = 512
        lv = None
        for bi in range(32):
            lv = b.dma("sync", bufA[:, bi * W:(bi + 1) * W],
                       wcon[bi * 128:(bi + 1) * 128, dh * 512:(dh + 1) * 512],
                       wait=[("ve", b.cnt["ve"]), ("gp", b.cnt["gp"]),
                             ("mm", b.cnt["mm"]), ("sw0", b.cnt.get("sw0", 0)),
                             ("sw1", b.cnt.get("sw1", 0))] if bi == 0 else None,
                       inc="lv")
        b.on("vector", None, wait=[("lv", lv)])
        b.on("gpsimd", None, wait=[("lv", lv)])
        cur = stages_pass(None, W, bufs, 0, range(11, 6, -1), cdnsb, None)
        for bi in range(32):
            pi, pt, ptok = ph_rot.get()
            mmt = b.on("tensor",
                       lambda e, p=pt[:, 0:W], l=chainM[:, bi * 128:(bi + 1) * 128],
                              r=bufs[cur](bi, W):
                       e.matmul(p, l, r, start=True, stop=True),
                       wait=[("ve", b.cnt["ve"]), ("gp", b.cnt["gp"]),
                             ("sc", ve_chain_dn)] + ([ptok] if ptok else []),
                       inc="mm")
            vt = b.on("vector", lambda e, o=bufs[1 - cur](bi, W).bitcast(BF16)[:, 0:W],
                             i=pt[:, 0:W]:
                      e.tensor_copy(out=o, in_=i), wait=[("mm", mmt)], inc="ve")
            ph_rot.tok[pi] = ("ve", vt)
            b.dma("sync", wcon_scr[bi, :, dh * 512:(dh + 1) * 512],
                  bufs[1 - cur](bi, W).bitcast(BF16)[:, 0:W],
                  wait=[("ve", vt)], inc="sw0")
    sw_done = b.cnt["sw0"]

    b.pop_scope()
    b.barrier()
    b.push_scope()
    pstat1 = b.ps("H_pstat1", [1, TT])
    pstat2 = b.ps("H_pstat2", [1, TT])
    pbc1 = b.ps("H_pbc1", [128, TT])
    pbc2 = b.ps("H_pbc2", [128, TT])
    lwc = None
    for hcl in range(32):
        lwc = b.dma("sync", wcon_full_sl(hcl * 1024, (hcl + 1) * 1024), wcon_scr[hcl, :, :],
                    wait=[("sw0", sw_done), ("mm", b.cnt["mm"]), ("ve", b.cnt["ve"])]
                    if hcl == 0 else None, inc="lwc")
    x3_cons = [None, None]
    for tt in range(NTT):
        ts = tt * TT
        par = tt % 2
        x3f = x3f_[par]
        x3bt = x3bt_[par]
        w0 = [x3_cons[par]] if x3_cons[par] else []
        if tt < 2:
            w0 += [("ve", b.cnt["ve"]), ("gp", b.cnt["gp"]), ("sc", b.cnt["sc"])]
        lx = b.dma("sync", x3f[:, 0:8 * TT].rearrange("p (cc t) -> p cc t", t=TT),
                   x3T.rearrange("(cc p) t -> p cc t", p=128)[:, :, ts:ts + TT],
                   wait=w0 if w0 else None, inc=f"lx{par}")
        cvx = b.on("vector", lambda e, o=x3bt[:, :], i=x3f[:, :]: e.tensor_copy(out=o, in_=i),
                   wait=[(f"lx{par}", lx)], inc="ve")
        wslot_tok = [None, None]
        for hc in range(32):
            i2 = hc % 2
            sem = f"lw{i2}"
            ldw = b.dma("sync", wexp_sl[i2][:, :], wexp_scr[hc, :, :],
                        wait=[wslot_tok[i2]] if wslot_tok[i2] else
                             ([("sw0", b.cnt["sw0"]), ("sw1", b.cnt["sw1"])] if hc < 2 and tt == 0 else None),
                        inc=sem)
            pi, pt, ptok = ph_rot.get()
            for cc in range(8):
                w = None
                if cc == 0:
                    w = [(sem, ldw), ("ve", cvx)] + ([ptok] if ptok else [])
                b.on("tensor",
                     lambda e, p=pt[:, 0:TT], l=wexp_sl[i2][:, cc * 128:(cc + 1) * 128],
                            r=x3bt[:, cc * TT:(cc + 1) * TT],
                            st=(cc == 0), sp=(cc == 7): e.matmul(p, l, r, start=st, stop=sp),
                     wait=w, inc="mm" if cc == 7 else None)
            mmh = b.cnt["mm"]
            wslot_tok[i2] = ("mm", mmh)
            sct = b.on("scalar",
                       lambda e, o=h3_sl(hc * TT, (hc + 1) * TT), i=pt[:, 0:TT],
                              bi=gbias[:, hc:hc + 1]:
                       e.activation(out=o, in_=i, func=AF.Gelu, bias=bi, scale=1.0),
                       wait=[("mm", mmh), ("ve", gbias_tok)], inc="sc")
            ph_rot.tok[pi] = ("sc", sct)
        h3_sc = b.cnt["sc"]
        for dc in range(8):
            pi, pt, ptok = py_rot.get()
            for hcl in range(32):
                w = None
                if hcl == 0:
                    w = [("lwc", lwc), ("sc", h3_sc)] + ([ptok] if ptok else [])
                b.on("tensor",
                     lambda e, p=pt[:, 0:TT],
                            l=wcon_full_sl(hcl * 1024 + dc * 128, hcl * 1024 + (dc + 1) * 128),
                            r=h3_sl(hcl * TT, (hcl + 1) * TT),
                            st=(hcl == 0), sp=(hcl == 31): e.matmul(p, l, r, start=st, stop=sp),
                     wait=w, inc="mm" if hcl == 31 else None)
            mmy = b.cnt["mm"]
            vt = b.on("vector",
                      lambda e, o=zbuf[:, dc * TT:(dc + 1) * TT], i=pt[:, 0:TT],
                             bi=ybias[:, dc:dc + 1], c=x3f[:, dc * TT:(dc + 1) * TT]:
                      e.scalar_tensor_tensor(out=o, in0=i, scalar=bi, in1=c,
                                             op0=OP.add, op1=OP.add),
                      wait=[("mm", mmy), ("sc", b.cnt["sc"])], inc="ve")
            py_rot.tok[pi] = ("ve", vt)
        x3_cons[par] = ("ve", b.cnt["ve"])
        ve_out = emit_post_ln(b, zbuf, TT, sqtile, rows, pstat1, pstat2, pbc1, pbc2,
                              onescol, onesrow, epscol,
                              outf=lambda cc: x4buf[:, cc * TT:(cc + 1) * TT])
        stt = b.dma("sync",
                    x4T.rearrange("(cc p) t -> p cc t", p=128)[:, :, ts:ts + TT],
                    x4buf[:, 0:8 * TT].rearrange("p (cc t) -> p cc t", t=TT),
                    wait=[("ve", ve_out)], inc="st")
        b.wait("vector", "st", stt)
    b.pop_scope()
    b.wait("sync", "st", b.cnt["st"])


def l1_ffn_host_inputs(W_exp, b_exp, up_w, upb, dn_w, W_con, b_con, d_bias):
    d = {"vaug": np.ascontiguousarray(W_exp.T.astype(np.float32)),
         "bexp_pp": pp_col(b_exp),
         "wcon": np.ascontiguousarray(W_con),
         "upbias_pp": pp_col(upb), "bcon_pp": pp_col(b_con), "dbias_pp": pp_col(d_bias)}
    d.update(butterfly_host_arrays(up_w, dn_w))
    return d


def build_l1_ffn_launch():
    b = B()
    x3T = b.dram_in("x3T", [D, T])
    vaug = b.dram_in("vaug", [4096, 1024])
    bexp_pp = b.dram_in("bexp_pp", [128, 32])
    wcon = b.dram_in("wcon", [4096, 1024])
    ups = b.dram_in("ups", [6, 32, 128, 128])
    ups6t = b.dram_in("ups6t", [32, 128, 128])
    dns0 = b.dram_in("dns0", [32, 128, 128])
    dnst = b.dram_in("dnst", [6, 32, 128, 128])
    cup = b.dram_in("cup", [128, 320])
    cdn = b.dram_in("cdn", [128, 320])
    upbias_pp = b.dram_in("upbias_pp", [128, 32])
    bcon_pp = b.dram_in("bcon_pp", [128, 8])
    dbias_pp = b.dram_in("dbias_pp", [128, 8])
    wexp_scr = b.dram_tmp("wexp_scr", [32, 128, 1024], BF16)
    wcon_scr = b.dram_tmp("wcon_scr", [32, 128, 1024], BF16)
    x4T = b.dram_out("x4T", [D, T])
    emit_l1_ffn(b, x3T, x4T, vaug, bexp_pp, wcon,
                ups, ups6t, dns0, dnst, cup, cdn,
                upbias_pp, bcon_pp, dbias_pp, wexp_scr, wcon_scr)
    return b.finalize()


_CACHE = {}


def _programs():
    if "p" not in _CACHE:
        _CACHE["p"] = (build_l0_attn_launch(), build_l0_ffn_launch(),
                       build_l1_attn_launch(), build_l1_ffn_launch())
    return _CACHE["p"]


def kernel_unfused(**inputs):
    x0 = np.asarray(inputs["L0_attn_x"], dtype=np.float32)
    NCORE = 8
    nc1, nc2, nc3, nc4 = _programs()

    h1 = l0_attn_host_inputs(None, inputs["L0_attn_Wqkv"], inputs["L0_attn_Wgate"],
                             inputs["L0_attn_Wout"], inputs["L0_attn_bgate"],
                             inputs["L0_attn_decay_param"])
    del h1["x0T"]
    im1 = []
    for c in range(NCORE):
        d = dict(h1)
        d["x0T"] = np.ascontiguousarray(x0[c].T)
        im1.append(d)
    r1 = run_bass_kernel_spmd(nc1, im1, core_ids=list(range(NCORE)))

    h2 = l0_ffn_host_inputs(inputs["L0_ffn_Wg"], inputs["L0_ffn_W1"], inputs["L0_ffn_b1"],
                            inputs["L0_ffn_W2"], inputs["L0_ffn_b2"])
    im2 = []
    for c in range(NCORE):
        d = dict(h2)
        d["x1T"] = np.ascontiguousarray(r1.results[c]["x1T"])
        im2.append(d)
    r2 = run_bass_kernel_spmd(nc2, im2, core_ids=list(range(NCORE)))

    h3 = l1_attn_host_inputs(inputs["L1_attn_W_qkv"], inputs["L1_attn_b_qkv"],
                             inputs["L1_attn_W_proj"], inputs["L1_attn_b_proj"])
    im3 = []
    for c in range(NCORE):
        d = dict(h3)
        d["x2T"] = np.ascontiguousarray(r2.results[c]["x2T"])
        im3.append(d)
    r3 = run_bass_kernel_spmd(nc3, im3, core_ids=list(range(NCORE)))

    h4 = l1_ffn_host_inputs(inputs["L1_ffn_W_exp"], inputs["L1_ffn_b_exp"],
                            inputs["L1_ffn_up_weights"], inputs["L1_ffn_up_bias"],
                            inputs["L1_ffn_down_weights"], inputs["L1_ffn_W_con"],
                            inputs["L1_ffn_b_con"], inputs["L1_ffn_down_bias"])
    im4 = []
    for c in range(NCORE):
        d = dict(h4)
        d["x3T"] = np.ascontiguousarray(r3.results[c]["x3T"])
        im4.append(d)
    r4 = run_bass_kernel_spmd(nc4, im4, core_ids=list(range(NCORE)))

    out = np.stack([np.ascontiguousarray(r4.results[c]["x4T"].T) for c in range(NCORE)])
    return out.astype(np.float32, copy=False)


def build_fused():
    b = B()
    x0T = b.dram_in("x0T", [D, T])
    wqkvT = b.dram_in("wqkvT", [D, 3 * D])
    wgateT = b.dram_in("wgateT", [D, D])
    woutT = b.dram_in("woutT", [D, D])
    bgate_pp = b.dram_in("bgate_pp", [128, 8])
    decay_p = b.dram_in("decay_p", [16, 1])
    consts0 = {k: b.dram_in(k, v.shape) for k, v in host_consts_l0().items()}
    wg = b.dram_in("wg", [D, 8])
    w1 = b.dram_in("w1", [8, D, 2048])
    b1pp = b.dram_in("b1pp", [128, 128])
    w2 = b.dram_in("w2", [8, 2048, D])
    b2 = b.dram_in("b2", [8, D])
    constsF = {k: b.dram_in(k, v.shape, BF16) for k, v in host_consts_ffn().items()}
    wqkv1 = b.dram_in("wqkv1", [D, 3 * D])
    bqkv_pp = b.dram_in("bqkv_pp", [128, 24])
    wproj = b.dram_in("wproj", [D, D])
    bproj_pp = b.dram_in("bproj_pp", [128, 8])
    vaug = b.dram_in("vaug", [4096, 1024])
    bexp_pp = b.dram_in("bexp_pp", [128, 32])
    wcon = b.dram_in("wcon", [4096, 1024])
    ups = b.dram_in("ups", [6, 32, 128, 128])
    ups6t = b.dram_in("ups6t", [32, 128, 128])
    dns0 = b.dram_in("dns0", [32, 128, 128])
    dnst = b.dram_in("dnst", [6, 32, 128, 128])
    cup = b.dram_in("cup", [128, 320])
    cdn = b.dram_in("cdn", [128, 320])
    upbias_pp = b.dram_in("upbias_pp", [128, 32])
    bcon_pp = b.dram_in("bcon_pp", [128, 8])
    dbias_pp = b.dram_in("dbias_pp", [128, 8])
    x4T = b.dram_out("x4T", [D, T])
    x1T = b.dram_tmp("x1T", [D, T])
    x2T = b.dram_tmp("x2T", [D, T])
    x3T = b.dram_tmp("x3T", [D, T])
    w1b = b.dram_tmp("w1b", [8, 16, 128, 1024], BF16)
    wqkvb = b.dram_tmp("wqkvb", [D, 3 * D], BF16)
    wgateb = b.dram_tmp("wgateb", [D, D], BF16)
    woutb = b.dram_tmp("woutb", [D, D], BF16)
    w2b = b.dram_tmp("w2b", [8, 16, 128, 1024], BF16)
    attn_dram = b.dram_tmp("attn_dram", [D, T], BF16)
    wexp_scr = b.dram_tmp("wexp_scr", [32, 128, 1024], BF16)
    wcon_scr = b.dram_tmp("wcon_scr", [32, 128, 1024], BF16)

    b.begin_phase()
    emit_l0_attn(b, x0T, x1T, wqkvT, wgateT, woutT, bgate_pp, decay_p, consts0,
                 wqkvb, wgateb, woutb)
    b.end_phase()
    b.barrier()
    b.begin_phase()
    emit_l0_ffn(b, x1T, x2T, wg, w1, b1pp, w2, b2, w1b, w2b, constsF)
    b.end_phase()
    b.barrier()
    b.begin_phase()
    emit_l1_attn(b, x2T, x3T, wqkv1, bqkv_pp, wproj, bproj_pp, attn_dram)
    b.end_phase()
    b.barrier()
    b.begin_phase()
    emit_l1_ffn(b, x3T, x4T, vaug, bexp_pp, wcon,
                ups, ups6t, dns0, dnst, cup, cdn,
                upbias_pp, bcon_pp, dbias_pp, wexp_scr, wcon_scr)
    b.end_phase()
    return b.finalize()


def kernel(**inputs):
    x0 = np.asarray(inputs["L0_attn_x"], dtype=np.float32)
    NCORE = 8
    if "fused" not in _CACHE:
        _CACHE["fused"] = build_fused()
    nc = _CACHE["fused"]
    h1 = l0_attn_host_inputs(None, inputs["L0_attn_Wqkv"], inputs["L0_attn_Wgate"],
                             inputs["L0_attn_Wout"], inputs["L0_attn_bgate"],
                             inputs["L0_attn_decay_param"])
    del h1["x0T"]
    h2 = l0_ffn_host_inputs(inputs["L0_ffn_Wg"], inputs["L0_ffn_W1"], inputs["L0_ffn_b1"],
                            inputs["L0_ffn_W2"], inputs["L0_ffn_b2"])
    h3 = l1_attn_host_inputs(inputs["L1_attn_W_qkv"], inputs["L1_attn_b_qkv"],
                             inputs["L1_attn_W_proj"], inputs["L1_attn_b_proj"])
    h3 = {("wqkv1" if k == "wqkv" else k): v for k, v in h3.items()}
    h4 = l1_ffn_host_inputs(inputs["L1_ffn_W_exp"], inputs["L1_ffn_b_exp"],
                            inputs["L1_ffn_up_weights"], inputs["L1_ffn_up_bias"],
                            inputs["L1_ffn_down_weights"], inputs["L1_ffn_W_con"],
                            inputs["L1_ffn_b_con"], inputs["L1_ffn_down_bias"])
    shared = {}
    shared.update(h1); shared.update(h2); shared.update(h3); shared.update(h4)
    im = []
    for c in range(NCORE):
        d = dict(shared)
        d["x0T"] = np.ascontiguousarray(x0[c].T)
        im.append(d)
    res = run_bass_kernel_spmd(nc, im, core_ids=list(range(NCORE)))
    out = np.stack([np.ascontiguousarray(res.results[c]["x4T"].T) for c in range(NCORE)])
    return out.astype(np.float32, copy=False)


# version 29
# speedup vs baseline: 1.1677x; 1.1677x over previous
import numpy as np
from contextlib import ExitStack
import concourse.bass as bass
import concourse.mybir as mybir
from concourse.bass_utils import run_bass_kernel_spmd
from concourse.alu_op_type import AluOpType as OP

F32 = mybir.dt.float32
BF16 = mybir.dt.bfloat16
AF = mybir.ActivationFunctionType
AX = mybir.AxisListType.X

D = 1024
T = 4096
EPS = 1e-5


class B:
    ENGINES = ("sync", "tensor", "vector", "scalar", "gpsimd")

    def __init__(self):
        self.nc = bass.Bass("TRN2", target_bir_lowering=False)
        self.qs = {e: [] for e in self.ENGINES}
        self.ctx = ExitStack()
        self.phase_ctx = None
        self.semh = {}
        self.cnt = {}

    def push_scope(self):
        if not hasattr(self, "_scopes"):
            self._scopes = []
        self._scopes.append(ExitStack())

    def pop_scope(self):
        self._scopes.pop().close()

    def begin_phase(self):
        self.phase_ctx = ExitStack()

    def end_phase(self):
        if self.phase_ctx is not None:
            self.phase_ctx.close()
            self.phase_ctx = None

    def barrier(self):
        snap = [(s, v) for s, v in self.cnt.items() if v > 0]
        for e in self.ENGINES:
            self.on(e, None, wait=snap)

    def sem(self, name):
        if name not in self.semh:
            self.semh[name] = self.ctx.enter_context(self.nc.semaphore(name))
            self.cnt[name] = 0
        return self.semh[name]

    def _ctx(self):
        if getattr(self, "_scopes", None):
            return self._scopes[-1]
        return self.phase_ctx if self.phase_ctx is not None else self.ctx

    def sb(self, name, shape, dt):
        return self._ctx().enter_context(self.nc.sbuf_tensor(name, list(shape), dt))

    def ps(self, name, shape, dt=F32):
        return self._ctx().enter_context(self.nc.psum_tensor(name, list(shape), dt))

    def dram_in(self, name, shape, dt=F32):
        return self.nc.dram_tensor(name, list(shape), dt, kind="ExternalInput").ap()

    def dram_out(self, name, shape, dt=F32):
        return self.nc.dram_tensor(name, list(shape), dt, kind="ExternalOutput").ap()

    def dram_tmp(self, name, shape, dt=F32):
        return self.nc.dram_tensor(name, list(shape), dt, kind="Internal").ap()

    def presem(self):
        for s in ("mm", "ve", "sc", "gp", "st", "lc"):
            self.sem(s)

    def on(self, eng, fn, wait=None, inc=None, inc_n=1):
        waits = []
        if wait:
            for w in wait:
                if w is None:
                    continue
                if isinstance(w, str):
                    waits.append((self.sem(w), self.cnt[w]))
                else:
                    waits.append((self.sem(w[0]), w[1]))
        ret = None
        if inc is not None:
            semh = self.sem(inc)
            self.cnt[inc] += inc_n
            ret = self.cnt[inc]

            def f(e, fn=fn, waits=waits, semh=semh, n=inc_n):
                for s, v in waits:
                    e.wait_ge(s, v)
                fn(e).then_inc(semh, n)
        else:
            def f(e, fn=fn, waits=waits):
                for s, v in waits:
                    e.wait_ge(s, v)
                if fn is not None:
                    fn(e)
        self.qs[eng].append(f)
        return ret

    def wait(self, eng, sem, val=None):
        self.on(eng, None, wait=[sem if val is None else (sem, val)])

    def dma(self, eng, out, in_, inc="ld", wait=None):
        return self.on(eng, lambda e, o=out, i=in_: e.dma_start(out=o, in_=i),
                       wait=wait, inc=inc, inc_n=16)

    def finalize(self):
        with self.nc.Block() as block:
            for e in self.ENGINES:
                q = self.qs[e]
                if not q:
                    continue

                def runner(eng, q=q):
                    for f in q:
                        f(eng)
                getattr(block, e)(runner)
        self.ctx.close()
        return self.nc


class Rot:

    def __init__(self, slots):
        self.slots = list(slots)
        self.tok = [None] * len(self.slots)
        self.i = 0

    def get(self):
        i = self.i
        self.i = (self.i + 1) % len(self.slots)
        return i, self.slots[i], self.tok[i]


def pp_col(v):
    v = np.asarray(v)
    return np.ascontiguousarray(v.reshape(-1, 128).T.astype(np.float32, copy=False))


def host_consts_l0():
    onesblk = np.zeros((128, 2), np.float32)
    onesblk[:64, 0] = 1.0
    onesblk[64:, 1] = 1.0
    onesblk_t = np.ascontiguousarray(onesblk.T)
    sel = np.zeros((16, 128), np.float32)
    for e in range(16):
        sel[e, (e % 2) * 64:(e % 2) * 64 + 64] = 1.0
    decmask = np.zeros((16, 8), np.float32)
    for e in range(16):
        decmask[e, e // 2] = 1.0
    return {"c_onesblk": onesblk, "c_onesblk_t": onesblk_t,
            "c_sel": sel, "c_decmask": decmask}


def emit_post_ln(b, zbuf, TT, sqtile, rows, pstat1, pstat2, pbc1, pbc2,
                 onescol, onesrow, epscol, outf=None, outb=None, zoff=0):
    zc = lambda cc: zbuf[:, zoff + cc * TT:zoff + (cc + 1) * TT]
    zready = ("ve", b.cnt["ve"])
    scready = ("sc", b.cnt["sc"])
    zbf = b._ln_zbf
    m_ = rows[0:1, 0 * TT:1 * TT]
    msq = rows[0:1, 1 * TT:2 * TT]
    var = rows[0:1, 1 * TT:2 * TT]
    tmp = rows[0:1, 2 * TT:3 * TT]
    rb2 = b._ln_rows_bf
    mm_prev = None
    for cc in range(8):
        sct = b.on("scalar",
                   lambda e, o=zbf[:, 0:TT], i=zc(cc): e.copy(out=o, in_=i),
                   wait=([zready, scready] if cc == 0 else [mm_prev]), inc="sc")
        b.on("tensor",
             lambda e, p=pstat1[0:1, 0:TT], l=onescol[:, 0:1], r=zbf[:, 0:TT],
                    st=(cc == 0), sp=(cc == 7): e.matmul(p, l, r, start=st, stop=sp),
             wait=[("sc", sct)], inc="mm")
        sct2 = b.on("scalar",
                    lambda e, o=sqtile[:, 0:TT], i=zc(cc):
                    e.activation(out=o, in_=i, func=AF.Square),
                    wait=[scready] if cc == 0 else None, inc="sc")
        tok = b.on("tensor",
                   lambda e, p=pstat2[0:1, 0:TT], l=onescol[:, 0:1], r=sqtile[:, 0:TT],
                          st=(cc == 0), sp=(cc == 7): e.matmul(p, l, r, start=st, stop=sp),
                   wait=[("sc", sct2)], inc="mm")
        mm_prev = ("mm", tok)
    mm_sq = b.cnt["mm"]
    b.on("vector", lambda e, o=m_, i=pstat1[0:1, 0:TT]:
         e.tensor_scalar(out=o, in0=i, scalar1=1.0 / 1024, scalar2=None, op0=OP.mult),
         wait=[("mm", mm_sq)])
    b.on("vector", lambda e, o=msq, i=pstat2[0:1, 0:TT]:
         e.tensor_scalar(out=o, in0=i, scalar1=1.0 / 1024, scalar2=None, op0=OP.mult),
         inc="ve")
    ve_ms = b.cnt["ve"]
    b.on("vector", lambda e, o=tmp, a=m_, c=m_: e.tensor_tensor(out=o, in0=a, in1=c, op=OP.mult),
         wait=[("ve", ve_ms)])
    b.on("vector", lambda e, o=var, a=msq, c=tmp:
         e.tensor_tensor(out=o, in0=a, in1=c, op=OP.subtract), inc="ve")
    ve_var = b.cnt["ve"]
    sct = b.on("scalar", lambda e, o=tmp, i=var, bi=epscol[0:1, 0:1]:
               e.activation(out=o, in_=i, func=AF.Ln, bias=bi, scale=1.0),
               wait=[("ve", ve_var)], inc="sc")
    vth = b.on("vector", lambda e, o=tmp, i=tmp:
               e.tensor_scalar(out=o, in0=i, scalar1=-0.5, scalar2=None, op0=OP.mult),
               wait=[("sc", sct)], inc="ve")
    sct = b.on("scalar", lambda e, o=rb2[0:1, 0:TT], i=tmp:
               e.activation(out=o, in_=i, func=AF.Exp),
               wait=[("ve", vth)], inc="sc")
    b.on("vector", lambda e, o=rb2[0:1, TT:2 * TT], a=m_, c=rb2[0:1, 0:TT]:
         e.tensor_tensor(out=o, in0=a, in1=c, op=OP.mult),
         wait=[("sc", sct)], inc="ve")
    ve_rows = b.cnt["ve"]
    b.on("tensor", lambda e, p=pbc1[:, 0:TT], l=onesrow[0:1, :], r=rb2[0:1, 0:TT]:
         e.matmul(p, l, r, start=True, stop=True), wait=[("ve", ve_rows)])
    b.on("tensor", lambda e, p=pbc2[:, 0:TT], l=onesrow[0:1, :], r=rb2[0:1, TT:2 * TT]:
         e.matmul(p, l, r, start=True, stop=True), inc="mm")
    mm_bc = b.cnt["mm"]
    for cc in range(8):
        b.on("vector",
             lambda e, o=sqtile[:, 0:TT], a=zc(cc), c=pbc1[:, 0:TT]:
             e.tensor_tensor(out=o, in0=a, in1=c, op=OP.mult),
             wait=[("mm", mm_bc), scready] if cc == 0 else None)
        if outf is not None:
            b.on("vector",
                 lambda e, o=outf(cc), a=sqtile[:, 0:TT], c=pbc2[:, 0:TT]:
                 e.tensor_tensor(out=o, in0=a, in1=c, op=OP.subtract), inc="ve")
        if outb is not None:
            b.on("vector",
                 lambda e, o=outb(cc), a=sqtile[:, 0:TT], c=pbc2[:, 0:TT]:
                 e.tensor_tensor(out=o, in0=a, in1=c, op=OP.subtract), inc="ve")
    return b.cnt["ve"]


def emit_l0_attn(b, x0T, x1T, wqkvT, wgateT, woutT, bgate_pp, decay_p, consts,
                 wqkvb=None, wgateb=None, woutb=None):
    b.presem()
    use_scr = wqkvb is not None
    if use_scr:
        for srcw, dstw, n in ((wqkvT, wqkvb, 3072), (wgateT, wgateb, 1024),
                              (woutT, woutb, 1024)):
            for r0 in range(0, 1024, 256):
                b.dma("gpsimd", dstw[r0:r0 + 256, 0:n], srcw[r0:r0 + 256, 0:n],
                      inc="cast0")
    cast0_tot = b.cnt.get("cast0", 0)
    TH = 1024
    TT = 512
    NTT = TH // TT
    NH = T // TH

    x0b = b.sb("A_x0b", [128, 8 * TH], BF16)
    vfull = b.sb("A_vfull", [128, 8 * TH], BF16)
    apre = b.sb("A_apre", [128, 8 * TH], BF16)
    qb_ = [b.sb(f"A_qb{i}", [128, TH], F32) for i in range(2)]
    kb_ = [b.sb(f"A_kb{i}", [128, TH], F32) for i in range(2)]
    kvb_ = [b.sb(f"A_kvb{i}", [128, TH], F32) for i in range(2)]
    sck_ = [b.sb(f"A_sck{i}", [128, TH], F32) for i in range(2)]
    sckv_ = [b.sb(f"A_sckv{i}", [128, TH], F32) for i in range(2)]
    mixs = kvb_[0]
    stg2 = sck_[0]
    tmps = b.sb("A_tmps", [128, TT], F32)
    tmps2 = b.sb("A_tmps2", [128, TT], F32)
    wstage = [b.sb(f"A_wstage{i}", [128, 1024], F32) for i in range(2)]
    wbf = [b.sb(f"A_wbf{i}", [128, 1024], BF16) for i in range(2)]
    carry = b.sb("A_carry", [128, 32], F32)
    dec16 = b.sb("A_dec16", [16, 1], F32)
    decsig = b.sb("A_decsig", [16, 1], F32)
    decrhs = b.sb("A_decrhs", [16, 8], F32)
    deccols = b.sb("A_deccols", [128, 8], F32)
    onesblk_f = b.sb("A_onesblkf", [128, 2], F32)
    onesblk_tf = b.sb("A_onesblk_tf", [2, 128], F32)
    onesblk = b.sb("A_onesblk", [128, 2], BF16)
    onesblk_t = b.sb("A_onesblk_t", [2, 128], BF16)
    prodb = b.sb("A_prodb", [128, 1024], BF16)
    den2b = b.sb("A_den2b", [2, TT], BF16)
    selm = b.sb("A_selm", [16, 128], F32)
    decmask = b.sb("A_decmask", [16, 8], F32)
    bgate_sb = b.sb("A_bgate", [128, 8], F32)
    onescol = b.sb("A_onescol", [128, 1], BF16)
    onesrow = b.sb("A_onesrow", [1, 128], BF16)
    epscol = b.sb("A_epscol", [128, 1], F32)
    den2 = b.sb("A_den2", [2, TT], F32)
    x0f = b.sb("A_x0f", [128, 16 * TT], F32)
    zbufq = b.sb("A_zbufq", [128, 16 * TT], F32)
    zbuf = zbufq
    x1buf = b.sb("A_x1buf", [128, 8 * TT], F32)
    sqtile = b.sb("A_sqtile", [128, TT], BF16)
    b._ln_zbf = b.sb("A_lnzbf", [128, TT], BF16)
    b._ln_rows_bf = b.sb("A_lnrbf", [1, 2 * TT], BF16)
    rows = b.sb("A_rows", [1, 4 * TT], F32)

    pmm = [b.ps(f"A_pmm{i}", [128, TT]) for i in range(2)]
    pden = b.ps("A_pden", [2, TT])
    pdbc = b.ps("A_pdbc", [128, TT])
    pstat1 = b.ps("A_pstat1", [1, TT])
    pstat2 = b.ps("A_pstat2", [1, TT])
    pbc1 = b.ps("A_pbc1", [128, TT])
    pbc2 = b.ps("A_pbc2", [128, TT])

    pmm_rot = Rot(pmm)
    wrot = Rot(list(range(2)))

    b.on("vector", lambda e, o=onesblk[:, :], i=onesblk_f[:, :]: e.tensor_copy(out=o, in_=i),
         wait=[("lc", 32)])
    b.on("vector", lambda e, o=onesblk_t[:, :], i=onesblk_tf[:, :]: e.tensor_copy(out=o, in_=i),
         wait=[("lc", 32)], inc="ve")
    for dst, key in [(onesblk_f, "c_onesblk"), (onesblk_tf, "c_onesblk_t"),
                     (selm, "c_sel"), (decmask, "c_decmask")]:
        b.dma("sync", dst[:, :], consts[key][:, :], inc="lc")
    b.dma("sync", bgate_sb[:, :], bgate_pp[:, :], inc="lc")
    ld_dec = b.dma("sync", dec16[:, 0:1], decay_p[:, :], inc="lc")
    b.on("vector", lambda e, o=onescol[:, :]: e.memset(o, 1.0))
    b.on("vector", lambda e, o=epscol[:, :]: e.memset(o, EPS))
    b.on("vector", lambda e, o=onesrow[:, :]: e.memset(o, 1.0), inc="ve")
    sct = b.on("scalar", lambda e, o=decsig[:, 0:1], i=dec16[:, 0:1]:
               e.activation(out=o, in_=i, func=AF.Sigmoid),
               wait=[("lc", ld_dec)], inc="sc")
    vet = b.on("vector", lambda e, o=decrhs[:, :], a=decmask[:, :], s=decsig[:, 0:1]:
               e.tensor_scalar(out=o, in0=a, scalar1=s, scalar2=None, op0=OP.mult),
               wait=[("sc", sct)], inc="ve")
    mmt = b.on("tensor", lambda e, p=pdbc[:, 0:8], l=selm[:, :], r=decrhs[:, :]:
               e.matmul(p, l, r, start=True, stop=True),
               wait=[("ve", vet)], inc="mm")
    b.on("vector", lambda e, o=deccols[:, :], i=pdbc[:, 0:8]: e.tensor_copy(out=o, in_=i),
         wait=[("mm", mmt)], inc="ve")

    def load_w_tiles(srcT, fbase, scr=None):
        i, _, tok = wrot.get()
        sem = f"lw{i}"
        if use_scr and scr is not None:
            srcv = scr.rearrange("(cc p) f -> p cc f", p=128)
            w0 = [tok] if tok else []
            if not b._cast0_waited:
                w0.append(("cast0", cast0_tot))
                b._cast0_waited = True
            lds = b.dma("sync", wbf[i][:, :].rearrange("p (cc j) -> p cc j", j=128),
                        srcv[:, :, fbase:fbase + 128],
                        wait=w0 if w0 else None, inc=sem)
            tok2 = (sem, lds)
            b._wconv[i] = None
            b._wdirect[i] = tok2
            return i, tok2
        srcv = srcT.rearrange("(cc p) f -> p cc f", p=128)
        lds = b.dma("sync", wstage[i][:, :].rearrange("p (cc j) -> p cc j", j=128),
                    srcv[:, :, fbase:fbase + 128],
                    wait=[("ve", b._wconv[i])] if b._wconv[i] else None, inc=sem)
        vtok = b.on("vector",
                    lambda e, o=wbf[i][:, :], s=wstage[i][:, :]: e.tensor_copy(out=o, in_=s),
                    wait=[(sem, lds)] + ([tok] if tok else []), inc="ve")
        b._wconv[i] = vtok
        return i, ("ve", vtok)

    b._wconv = [None, None]
    b._wdirect = [None, None]
    b._cast0_waited = False

    for h in range(NH):
        tbase = h * TH
        stgs = [zbuf, stg2]
        stg_tok = [("ve", b.cnt["ve"]), ("ve", b.cnt["ve"])]
        for cc in range(8):
            s = stgs[cc % 2]
            sem = f"lx{cc % 2}"
            ldc = b.dma("sync", s[:, 0:TH],
                        x0T[cc * 128:(cc + 1) * 128, tbase:tbase + TH],
                        wait=[stg_tok[cc % 2]], inc=sem)
            t = b.on("vector", lambda e, o=x0b[:, cc * TH:(cc + 1) * TH], src=s[:, 0:TH]:
                     e.tensor_copy(out=o, in_=src),
                     wait=[(sem, ldc)], inc="ve")
            stg_tok[cc % 2] = ("ve", t)
        x0b_ready = b.cnt["ve"]

        def part1(j):
            par = j % 2
            qb, kb, kvb = qb_[par], kb_[par], kvb_[par]
            bufs = {"q": qb, "k": kb, "v": kvb}
            for s_name, sbase in (("q", 0), ("k", 1024), ("v", 2048)):
                wi, wtok = load_w_tiles(wqkvT, sbase + j * 128, wqkvb)
                for tt in range(NTT):
                    pi, pt, ptok = pmm_rot.get()
                    for cc in range(8):
                        w = None
                        if cc == 0:
                            w = [wtok, ("ve", x0b_ready)] + ([ptok] if ptok else [])
                        b.on("tensor",
                             lambda e, p=pt[:, 0:TT], l=wbf[wi][:, cc * 128:(cc + 1) * 128],
                                    r=x0b[:, cc * TH + tt * TT: cc * TH + (tt + 1) * TT],
                                    st=(cc == 0), sp=(cc == 7):
                             e.matmul(p, l, r, start=st, stop=sp),
                             wait=w, inc="mm" if cc == 7 else None)
                    mm_done = b.cnt["mm"]
                    wrot.tok[wi] = ("mm", mm_done)
                    dst = bufs[s_name][:, tt * TT:(tt + 1) * TT]
                    if s_name == "v":
                        b.on("scalar",
                             lambda e, o=vfull[:, j * TH + tt * TT: j * TH + (tt + 1) * TT],
                                    i=pt[:, 0:TT]: e.copy(out=o, in_=i),
                             wait=[("mm", mm_done)], inc="sc")
                        vt = b.on("vector",
                                  lambda e, o=kvb[:, tt * TT:(tt + 1) * TT], i=pt[:, 0:TT],
                                         k_=kb[:, tt * TT:(tt + 1) * TT]:
                                  e.tensor_tensor(out=o, in0=i, in1=k_, op=OP.mult),
                                  wait=[("mm", mm_done), ("sc", b.cnt["sc"])], inc="ve")
                        pmm_rot.tok[pi] = ("ve", vt)
                    else:
                        v1 = b.on("vector",
                                  lambda e, o=tmps[:, 0:TT], i=pt[:, 0:TT]:
                                  e.tensor_scalar(out=o, in0=i, scalar1=0.0, scalar2=None,
                                                  op0=OP.min),
                                  wait=[("mm", mm_done)], inc="ve")
                        s1 = b.on("scalar",
                                  lambda e, o=tmps2[:, 0:TT], i=tmps[:, 0:TT]:
                                  e.activation(out=o, in_=i, func=AF.Exp),
                                  wait=[("ve", v1)], inc="sc")
                        v2 = b.on("vector",
                                  lambda e, o=dst, i=pt[:, 0:TT], t2=tmps2[:, 0:TT]:
                                  e.scalar_tensor_tensor(out=o, in0=i, scalar=0.0, in1=t2,
                                                         op0=OP.max, op1=OP.add),
                                  wait=[("sc", s1)], inc="ve")
                        pmm_rot.tok[pi] = ("ve", v2)

        def part2(j):
            par = j % 2
            qb, kb, kvb, sck, sckv = qb_[par], kb_[par], kvb_[par], sck_[par], sckv_[par]
            d_b = deccols[:, j:j + 1].broadcast_to((128, TH))
            ini_kv = 0.0 if h == 0 else carry[:, j:j + 1]
            ini_k = 0.0 if h == 0 else carry[:, 16 + j:17 + j]
            b.on("vector", lambda e, o=sckv[:, :], d0=d_b, d1=kvb[:, :], ini=ini_kv:
                 e.tensor_tensor_scan(out=o, data0=d0, data1=d1, initial=ini,
                                      op0=OP.mult, op1=OP.add), inc="ve")
            b.on("vector", lambda e, o=sck[:, :], d0=d_b, d1=kb[:, :], ini=ini_k:
                 e.tensor_tensor_scan(out=o, data0=d0, data1=d1, initial=ini,
                                      op0=OP.mult, op1=OP.add), inc="ve")
            if h < NH - 1:
                b.on("vector", lambda e, o=carry[:, j:j + 1], i=sckv[:, TH - 1:TH]:
                     e.tensor_copy(out=o, in_=i))
                b.on("vector", lambda e, o=carry[:, 16 + j:17 + j], i=sck[:, TH - 1:TH]:
                     e.tensor_copy(out=o, in_=i), inc="ve")
            b.on("vector", lambda e, o=prodb[:, :], a=qb[:, :], c=sck[:, :]:
                 e.tensor_tensor(out=o, in0=a, in1=c, op=OP.mult))
            b.on("vector", lambda e, o=qb[:, :], a=qb[:, :], c=sckv[:, :]:
                 e.tensor_tensor(out=o, in0=a, in1=c, op=OP.mult), inc="ve")
            ve_prod = b.cnt["ve"]
            for tt in range(NTT):
                mm1 = b.on("tensor", lambda e, p=pden[:, 0:TT], l=onesblk[:, :],
                                  r=prodb[:, tt * TT:(tt + 1) * TT]:
                           e.matmul(p, l, r, start=True, stop=True),
                           wait=[("ve", ve_prod)], inc="mm")
                vd = b.on("vector", lambda e, o=den2[:, 0:TT], i=pden[:, 0:TT]:
                          e.tensor_scalar(out=o, in0=i, scalar1=1e-6, scalar2=None, op0=OP.max),
                          wait=[("mm", mm1)], inc="ve")
                s1_ = b.on("scalar", lambda e, o=den2[:, 0:TT], i=den2[:, 0:TT]:
                           e.activation(out=o, in_=i, func=AF.Ln),
                           wait=[("ve", vd)], inc="sc")
                vneg = b.on("vector", lambda e, o=den2[:, 0:TT], i=den2[:, 0:TT]:
                            e.tensor_scalar(out=o, in0=i, scalar1=-1.0, scalar2=None, op0=OP.mult),
                            wait=[("sc", s1_)], inc="ve")
                vrec_s = b.on("scalar", lambda e, o=den2b[:, 0:TT], i=den2[:, 0:TT]:
                              e.activation(out=o, in_=i, func=AF.Exp),
                              wait=[("ve", vneg)], inc="sc")
                mm2 = b.on("tensor", lambda e, p=pdbc[:, 0:TT], l=onesblk_t[:, :],
                                  r=den2b[:, 0:TT]:
                           e.matmul(p, l, r, start=True, stop=True),
                           wait=[("sc", vrec_s)], inc="mm")
                b.on("vector", lambda e, o=apre[:, j * TH + tt * TT: j * TH + (tt + 1) * TT],
                            a=qb[:, tt * TT:(tt + 1) * TT], c=pdbc[:, 0:TT]:
                     e.tensor_tensor(out=o, in0=a, in1=c, op=OP.mult),
                     wait=[("mm", mm2)], inc="ve")

        for j in range(8):
            part1(j)
            if j > 0:
                part2(j - 1)
        part2(7)

        apre_ready = b.cnt["ve"]
        if getattr(b, "dbg", None) is not None and h == 0:
            for cc in range(8):
                b.dma("sync", b.dbg["apre"][cc * 128:(cc + 1) * 128, :],
                      apre[:, cc * TH:(cc + 1) * TH],
                      wait=[("ve", apre_ready)] if cc == 0 else None, inc="st")
            for cc in range(8):
                b.dma("sync", b.dbg["v"][cc * 128:(cc + 1) * 128, :],
                      vfull[:, cc * TH:(cc + 1) * TH], inc="st")
        mm_x0_free = b.cnt["mm"]
        for fc in range(8):
            wi, wtok = load_w_tiles(wgateT, fc * 128, wgateb)
            for tt in range(NTT):
                pi, pt, ptok = pmm_rot.get()
                for cc in range(8):
                    w = None
                    if cc == 0:
                        w = [wtok, ("ve", apre_ready)] + ([ptok] if ptok else [])
                    b.on("tensor",
                         lambda e, p=pt[:, 0:TT], l=wbf[wi][:, cc * 128:(cc + 1) * 128],
                                r=apre[:, cc * TH + tt * TT: cc * TH + (tt + 1) * TT],
                                st=(cc == 0), sp=(cc == 7):
                         e.matmul(p, l, r, start=st, stop=sp),
                         wait=w, inc="mm" if cc == 7 else None)
                mm_done = b.cnt["mm"]
                wrot.tok[wi] = ("mm", mm_done)
                sct = b.on("scalar",
                           lambda e, o=x0b[:, fc * TH + tt * TT: fc * TH + (tt + 1) * TT],
                                  i=pt[:, 0:TT], bias=bgate_sb[:, fc:fc + 1]:
                           e.activation(out=o, in_=i, func=AF.Sigmoid, bias=bias, scale=1.0),
                           wait=[("mm", mm_done)], inc="sc")
                pmm_rot.tok[pi] = ("sc", sct)
        g_ready = b.cnt["sc"]
        b.wait("vector", "sc", g_ready)
        for fc in range(8):
            b.on("vector", lambda e, o=mixs[:, :], a=apre[:, fc * TH:(fc + 1) * TH],
                        c=vfull[:, fc * TH:(fc + 1) * TH]:
                 e.tensor_tensor(out=o, in0=a, in1=c, op=OP.subtract))
            b.on("vector", lambda e, o=mixs[:, :], a=x0b[:, fc * TH:(fc + 1) * TH],
                        c=mixs[:, :]:
                 e.tensor_tensor(out=o, in0=a, in1=c, op=OP.mult))
            b.on("vector", lambda e, o=apre[:, fc * TH:(fc + 1) * TH], a=mixs[:, :],
                        c=vfull[:, fc * TH:(fc + 1) * TH]:
                 e.tensor_tensor(out=o, in0=a, in1=c, op=OP.add), inc="ve")
        gated_ready = b.cnt["ve"]
        if getattr(b, "dbg", None) is not None and h == 0:
            for cc in range(8):
                b.dma("sync", b.dbg["gated"][cc * 128:(cc + 1) * 128, :],
                      apre[:, cc * TH:(cc + 1) * TH],
                      wait=[("ve", gated_ready)] if cc == 0 else None, inc="st")

        ldr = None
        for cc in range(8):
            for tt in range(NTT):
                ldr = b.dma("sync", x0f[:, (cc * NTT + tt) * TT:(cc * NTT + tt + 1) * TT],
                            x0T[cc * 128:(cc + 1) * 128, tbase + tt * TT: tbase + (tt + 1) * TT],
                            wait=[("ve", b.cnt["ve"])] if (cc == 0 and tt == 0) else None,
                            inc="lxf")
        for dc in range(8):
            wi, wtok = load_w_tiles(woutT, dc * 128, woutb)
            for tt in range(NTT):
                pi, pt, ptok = pmm_rot.get()
                for cc in range(8):
                    w = None
                    if cc == 0:
                        w = [wtok, ("ve", gated_ready)] + ([ptok] if ptok else [])
                    b.on("tensor",
                         lambda e, p=pt[:, 0:TT], l=wbf[wi][:, cc * 128:(cc + 1) * 128],
                                r=apre[:, cc * TH + tt * TT: cc * TH + (tt + 1) * TT],
                                st=(cc == 0), sp=(cc == 7):
                         e.matmul(p, l, r, start=st, stop=sp),
                         wait=w, inc="mm" if cc == 7 else None)
                mm_done = b.cnt["mm"]
                wrot.tok[wi] = ("mm", mm_done)
                vt = b.on("vector",
                          lambda e, o=zbufq[:, (tt * 8 + dc) * TT:(tt * 8 + dc + 1) * TT],
                                 a=pt[:, 0:TT],
                                 c=x0f[:, (dc * NTT + tt) * TT:(dc * NTT + tt + 1) * TT]:
                          e.tensor_tensor(out=o, in0=a, in1=c, op=OP.add),
                          wait=[("mm", mm_done), ("lxf", ldr), ("sc", b.cnt["sc"])], inc="ve")
                pmm_rot.tok[pi] = ("ve", vt)
        for tt in range(NTT):
            ve_out = emit_post_ln(b, zbufq, TT,
                                  sqtile, rows, pstat1, pstat2, pbc1, pbc2,
                                  onescol, onesrow, epscol,
                                  outf=lambda cc: x1buf[:, cc * TT:(cc + 1) * TT],
                                  zoff=tt * 8 * TT)
            stt = None
            for cc in range(8):
                stt = b.dma("sync", x1T[cc * 128:(cc + 1) * 128,
                                        tbase + tt * TT: tbase + (tt + 1) * TT],
                            x1buf[:, cc * TT:(cc + 1) * TT],
                            wait=[("ve", ve_out)] if cc == 0 else None, inc="st")
            b.wait("vector", "st", stt)
    b.wait("sync", "st", b.cnt["st"])


def l0_attn_host_inputs(x0_b, Wqkv, Wgate, Wout, bgate, decay_param):
    d = {
        "x0T": None if x0_b is None else np.ascontiguousarray(x0_b.T),
        "wqkvT": np.ascontiguousarray(Wqkv.T),
        "wgateT": np.ascontiguousarray(Wgate.T),
        "woutT": np.ascontiguousarray(Wout.T),
        "bgate_pp": pp_col(bgate),
        "decay_p": np.ascontiguousarray(decay_param.reshape(16, 1)),
    }
    d.update(host_consts_l0())
    return d


def build_l0_attn_launch(dbg=False):
    b = B()
    x0T = b.dram_in("x0T", [D, T])
    wqkvT = b.dram_in("wqkvT", [D, 3 * D])
    wgateT = b.dram_in("wgateT", [D, D])
    woutT = b.dram_in("woutT", [D, D])
    bgate_pp = b.dram_in("bgate_pp", [128, 8])
    decay_p = b.dram_in("decay_p", [16, 1])
    consts = {k: b.dram_in(k, v.shape) for k, v in host_consts_l0().items()}
    x1T = b.dram_out("x1T", [D, T])
    if dbg:
        b.dbg = {k: b.dram_out("dbg_" + k, [D, 1024], BF16) for k in ("apre", "v", "gated")}
    else:
        b.dbg = None
    emit_l0_attn(b, x0T, x1T, wqkvT, wgateT, woutT, bgate_pp, decay_p, consts)
    return b.finalize()


def host_consts_ffn():
    import ml_dtypes
    esel = np.zeros((8, 8 * 128), ml_dtypes.bfloat16)
    for e in range(8):
        esel[e, e * 128:(e + 1) * 128] = 1.0
    return {"c_esel": esel}


def emit_l0_ffn(b, x1T, x2T, wg, w1, b1pp, w2, b2, w1b, w2b, consts):
    b.presem()
    TT = 512
    NTT = 8

    x1f_ = [b.sb(f"F_x1f{i}", [128, 8 * TT], F32) for i in range(2)]
    x1bt_ = [b.sb(f"F_x1bt{i}", [128, 8 * TT], BF16) for i in range(2)]
    y_acc = b.sb("F_yacc", [128, 8 * TT], F32)
    zbuf = b.sb("F_zbuf", [128, 8 * TT], F32)
    x2buf = b.sb("F_x2buf", [128, 8 * TT], F32)
    hbuf = b.sb("F_hbuf", [128, 2 * 16 * TT], BF16)
    hrel = b.sb("F_hrel", [128, TT], BF16)
    wsb1 = [b.sb(f"F_wsb1_{i}", [128, 1024], BF16) for i in range(3)]
    wsb2 = b.sb("F_wsb2", [128, 16 * 1024], BF16)
    wgsb = b.sb("F_wgsb", [128, 64], F32)
    wgb = b.sb("F_wgb", [128, 64], BF16)
    b1sb = b.sb("F_b1sb", [128, 128], F32)
    b2sb = b.sb("F_b2sb", [8, 1024], F32)
    b2b = b.sb("F_b2b", [8, 1024], BF16)
    eselm = b.sb("F_eselm", [8, 8 * 128], BF16)
    wT8b = b.sb("F_wT8b", [8, T], BF16)
    eselmb = eselm
    wbc = b.sb("F_wbc", [128, 8 * TT], BF16)
    probs = b.sb("F_probs", [128, 8], F32)
    pmsk = b.sb("F_pmsk", [128, 8], F32)
    wtok = b.sb("F_wtok", [128, 8], F32)
    mxc = b.sb("F_mxc", [128, 1], F32)
    sumc = b.sb("F_sumc", [128, 1], F32)
    recc = b.sb("F_recc", [128, 1], F32)
    mx1c = b.sb("F_mx1c", [128, 1], F32)
    mx2c = b.sb("F_mx2c", [128, 1], F32)
    ones128f = b.sb("F_ones128f", [128, 128], F32)
    identf = b.sb("F_identf", [128, 128], F32)
    onescol = b.sb("F_onescol", [128, 1], BF16)
    onesrow = b.sb("F_onesrow", [1, 128], BF16)
    epscol = b.sb("F_epscol", [128, 1], F32)
    sqtile = b.sb("F_sqtile", [128, TT], BF16)
    b._ln_zbf = b.sb("F_lnzbf", [128, TT], BF16)
    b._ln_rows_bf = b.sb("F_lnrbf", [1, 2 * TT], BF16)
    rows = b.sb("F_rows", [1, 4 * TT], F32)

    ph = [b.ps(f"F_ph{i}", [128, TT]) for i in range(2)]
    py = [b.ps(f"F_py{i}", [128, TT]) for i in range(2)]
    pstat1 = b.ps("F_pstat1", [1, TT])
    pstat2 = b.ps("F_pstat2", [1, TT])
    pbc1 = b.ps("F_pbc1", [128, TT])
    pbc2 = b.ps("F_pbc2", [128, TT])
    ptw = pbc2

    ph_rot = Rot(ph)
    py_rot = Rot(py)
    w1_rot = Rot(list(range(3)))

    for cc in range(8):
        b.dma("sync", wgsb[:, cc * 8:(cc + 1) * 8], wg[cc * 128:(cc + 1) * 128, :], inc="lc")
    b.dma("sync", b1sb[:, :], b1pp[:, :], inc="lc")
    b.dma("sync", b2sb[:, :], b2[:, :], inc="lc")
    b.dma("sync", eselm[:, :], consts["c_esel"][:, :], inc="lc")
    lc_tot = b.cnt["lc"]
    b.on("vector", lambda e, o=onescol[:, :]: e.memset(o, 1.0))
    b.on("vector", lambda e, o=onesrow[:, :]: e.memset(o, 1.0))
    b.on("vector", lambda e, o=epscol[:, :]: e.memset(o, EPS))
    b.on("vector", lambda e, o=ones128f[:, :]: e.memset(o, 1.0), inc="ve")
    b.on("vector", lambda e, o=wgb[:, :], i=wgsb[:, :]: e.tensor_copy(out=o, in_=i),
         wait=[("lc", lc_tot)])
    b.on("vector", lambda e, o=b2b[:, :], i=b2sb[:, :]: e.tensor_copy(out=o, in_=i), inc="ve")
    ve_consts = b.cnt["ve"]
    gp_id = b.on("gpsimd", lambda e, o=identf[:, :], i=ones128f[:, :]:
                 e.affine_select(out=o, in_=i, pattern=[[1, 128]], compare_op=OP.is_equal,
                                 fill=0.0, base=0, channel_multiplier=-1),
                 wait=[("ve", ve_consts)], inc="gp")

    for e_i in range(8):
        for fcl in range(16):
            src = w1[e_i, :, :].rearrange("(cc p) (fcl j) -> fcl p cc j", p=128, j=128)
            b.dma("gpsimd", w1b[e_i, fcl, :, :], src[fcl, :, :, :], inc="cast")
        for fcl in range(16):
            b.dma("gpsimd", w2b[e_i, fcl, :, :], w2[e_i, fcl * 128:(fcl + 1) * 128, :],
                  inc="cast")
    cast_tot = b.cnt["cast"]

    x1_cons = [None, None]
    for tt in range(NTT):
        ts = tt * TT
        par = tt % 2
        x1f = x1f_[par]
        x1bt = x1bt_[par]
        hb_off = par * 16 * TT
        lx = None
        for cc in range(8):
            w0 = None
            if cc == 0:
                w0 = [x1_cons[par]] if x1_cons[par] else []
            lx = b.dma("sync", x1f[:, cc * TT:(cc + 1) * TT],
                       x1T[cc * 128:(cc + 1) * 128, ts:ts + TT],
                       wait=w0, inc=f"lx{par}")
        cv = b.on("vector", lambda e, o=x1bt[:, :], i=x1f[:, :]: e.tensor_copy(out=o, in_=i),
                  wait=[(f"lx{par}", lx)], inc="ve")
        x1_ready = cv

        for tc in range(4):
            pli, plt, pltok = ph_rot.get()
            pl = plt[:, 0:8]
            mmw = [("ve", x1_ready), (f"lx{par}", lx), ("lc", lc_tot)]
            if pltok:
                mmw.append(pltok)
            for cc in range(8):
                b.on("tensor",
                     lambda e, p=pl, l=x1f[:, cc * TT + tc * 128: cc * TT + (tc + 1) * 128],
                            r=wgsb[:, cc * 8:(cc + 1) * 8],
                            st=(cc == 0), sp=(cc == 7): e.matmul(p, l, r, start=st, stop=sp),
                     wait=mmw if cc == 0 else None, inc="mm" if cc == 7 else None)
            mml = b.cnt["mm"]
            t = b.on("vector", lambda e, o=mxc[:, 0:1], i=pl:
                     e.reduce_max(out=o, in_=i, axis=AX, negate=True),
                     wait=[("mm", mml)], inc="ve")
            sct = b.on("scalar", lambda e, o=probs[:, :], i=pl, bi=mxc[:, 0:1]:
                       e.activation(out=o, in_=i, func=AF.Exp, bias=bi, scale=1.0),
                       wait=[("ve", t)], inc="sc")
            ph_rot.tok[pli] = ("sc", sct)
            t = b.on("vector", lambda e, o=sumc[:, 0:1], i=probs[:, :]:
                     e.reduce_sum(out=o, in_=i, axis=AX), wait=[("sc", sct)], inc="ve")
            t = b.on("vector", lambda e, o=recc[:, 0:1], i=sumc[:, 0:1]:
                     e.reciprocal(out=o, in_=i), wait=[("ve", t)], inc="ve")
            t = b.on("vector", lambda e, o=probs[:, :], i=probs[:, :], s=recc[:, 0:1]:
                     e.tensor_scalar(out=o, in0=i, scalar1=s, scalar2=None, op0=OP.mult),
                     wait=[("ve", t)], inc="ve")
            t = b.on("vector", lambda e, o=mx1c[:, 0:1], i=probs[:, :]:
                     e.reduce_max(out=o, in_=i, axis=AX), wait=[("ve", t)], inc="ve")
            t = b.on("vector", lambda e, o=pmsk[:, :], i=probs[:, :], s=mx1c[:, 0:1]:
                     e.tensor_scalar(out=o, in0=i, scalar1=s, scalar2=None, op0=OP.is_ge),
                     wait=[("ve", t)], inc="ve")
            t = b.on("vector", lambda e, o=pmsk[:, :], i=pmsk[:, :], p2=probs[:, :]:
                     e.scalar_tensor_tensor(out=o, in0=i, scalar=-1e30, in1=p2,
                                            op0=OP.mult, op1=OP.add),
                     wait=[("ve", t)], inc="ve")
            t = b.on("vector", lambda e, o=mx2c[:, 0:1], i=pmsk[:, :]:
                     e.reduce_max(out=o, in_=i, axis=AX), wait=[("ve", t)], inc="ve")
            t = b.on("vector", lambda e, o=pmsk[:, :], i=probs[:, :], s=mx2c[:, 0:1]:
                     e.tensor_scalar(out=o, in0=i, scalar1=s, scalar2=None, op0=OP.is_ge),
                     wait=[("ve", t)], inc="ve")
            vw = b.on("vector", lambda e, o=wtok[:, :], a=probs[:, :], c=pmsk[:, :]:
                      e.tensor_tensor(out=o, in0=a, in1=c, op=OP.mult),
                      wait=[("ve", t)], inc="ve")
            mmtr = b.on("tensor", lambda e, p=ptw[0:8, 0:128], i=wtok[:, :], idn=identf[:, :]:
                        e.transpose(p, i, idn),
                        wait=[("ve", vw), ("gp", gp_id), ("ve", b.cnt["ve"])], inc="mm")
            b.on("vector", lambda e, o=wT8b[0:8, ts + tc * 128: ts + (tc + 1) * 128],
                        i=ptw[0:8, 0:128]: e.tensor_copy(out=o, in_=i),
                 wait=[("mm", mmtr)], inc="ve")
        w_ready = b.cnt["ve"]

        for e_i in range(8):
            mmb = b.on("tensor",
                       lambda e, p=pbc1[:, 0:TT], l=eselmb[0:8, e_i * 128:(e_i + 1) * 128],
                              r=wT8b[0:8, ts:ts + TT]: e.matmul(p, l, r, start=True, stop=True),
                       wait=[("ve", w_ready), ("ve", b.cnt["ve"])], inc="mm")
            b.on("vector", lambda e, o=wbc[:, e_i * TT:(e_i + 1) * TT], i=pbc1[:, 0:TT]:
                 e.tensor_copy(out=o, in_=i), wait=[("mm", mmb)], inc="ve")

        y_mm_prev = None
        if not hasattr(b, "_hb_guard"):
            b._hb_guard = {0: None, 1: None}
        for e_i in range(8):
            for fcl in range(16):
                wi, _, wtk = w1_rot.get()
                sem = f"lw{wi}"
                ldw = b.dma("sync", wsb1[wi][:, :], w1b[e_i, fcl, :, :],
                            wait=[wtk, ("cast", cast_tot)] if wtk or True else None, inc=sem)
                pi, pt, ptok = ph_rot.get()
                for cc in range(8):
                    w = None
                    if cc == 0:
                        w = [(sem, ldw), ("ve", x1_ready)] + ([ptok] if ptok else [])
                    b.on("tensor",
                         lambda e, p=pt[:, 0:TT], l=wsb1[wi][:, cc * 128:(cc + 1) * 128],
                                r=x1bt[:, cc * TT:(cc + 1) * TT],
                                st=(cc == 0), sp=(cc == 7): e.matmul(p, l, r, start=st, stop=sp),
                         wait=w, inc="mm" if cc == 7 else None)
                mmh = b.cnt["mm"]
                w1_rot.tok[wi] = ("mm", mmh)
                sct = b.on("scalar",
                           lambda e, o=hrel[:, 0:TT], i=pt[:, 0:TT],
                                  bi=b1sb[:, e_i * 16 + fcl: e_i * 16 + fcl + 1]:
                           e.activation(out=o, in_=i, func=AF.Relu, bias=bi, scale=1.0),
                           wait=[("mm", mmh), ("ve", b.cnt["ve"])], inc="sc")
                hbg = b._hb_guard[(e_i % 2)]
                vh = b.on("vector",
                          lambda e, o=hbuf[:, (e_i % 2) * 16 * TT + fcl * TT:
                                              (e_i % 2) * 16 * TT + (fcl + 1) * TT],
                                 a=hrel[:, 0:TT],
                                 c=wbc[:, e_i * TT:(e_i + 1) * TT]:
                          e.tensor_tensor(out=o, in0=a, in1=c, op=OP.mult),
                          wait=[("sc", sct)] + ([hbg] if hbg else []), inc="ve")
                ph_rot.tok[pi] = ("sc", sct)
            h_ready = b.cnt["ve"]
            lw2 = None
            for fcl in range(16):
                lw2 = b.dma("sync", wsb2[:, fcl * 1024:(fcl + 1) * 1024], w2b[e_i, fcl, :, :],
                            wait=([y_mm_prev] if (fcl == 0 and y_mm_prev) else None) if fcl == 0 else None,
                            inc="lw2")
            for dc in range(8):
                pi, pt, ptok = py_rot.get()
                first = True
                if e_i == 0:
                    b.on("tensor",
                         lambda e, p=pt[:, 0:TT], l=b2b[0:8, dc * 128:(dc + 1) * 128],
                                r=wT8b[0:8, ts:ts + TT]: e.matmul(p, l, r, start=True, stop=False),
                         wait=[("lw2", lw2), ("ve", h_ready), ("ve", w_ready)] + ([ptok] if ptok else []),
                         )
                    first = False
                for fcl in range(16):
                    w = None
                    if first and fcl == 0:
                        w = [("lw2", lw2), ("ve", h_ready)] + ([ptok] if ptok else [])
                    b.on("tensor",
                         lambda e, p=pt[:, 0:TT],
                                l=wsb2[:, fcl * 1024 + dc * 128: fcl * 1024 + (dc + 1) * 128],
                                r=hbuf[:, (e_i % 2) * 16 * TT + fcl * TT:
                                          (e_i % 2) * 16 * TT + (fcl + 1) * TT],
                                st=(first and fcl == 0), sp=(fcl == 15):
                         e.matmul(p, l, r, start=st, stop=sp),
                         wait=w, inc="mm" if fcl == 15 else None)
                mmy = b.cnt["mm"]
                if e_i == 0:
                    vy = b.on("vector", lambda e, o=y_acc[:, dc * TT:(dc + 1) * TT], i=pt[:, 0:TT]:
                              e.tensor_copy(out=o, in_=i), wait=[("mm", mmy)], inc="ve")
                else:
                    vy = b.on("vector", lambda e, o=y_acc[:, dc * TT:(dc + 1) * TT],
                                     a=y_acc[:, dc * TT:(dc + 1) * TT], i=pt[:, 0:TT]:
                              e.tensor_tensor(out=o, in0=a, in1=i, op=OP.add),
                              wait=[("mm", mmy)], inc="ve")
                py_rot.tok[pi] = ("ve", vy)
            y_mm_prev = ("mm", b.cnt["mm"])
            b._hb_guard[e_i % 2] = ("mm", b.cnt["mm"])

        if getattr(b, "dbgF", None) is not None and tt == 0:
            for dc in range(8):
                b.dma("sync", b.dbgF["y"][dc * 128:(dc + 1) * 128, :],
                      y_acc[:, dc * TT:(dc + 1) * TT],
                      wait=[("ve", b.cnt["ve"])] if dc == 0 else None, inc="st")
        for dc in range(8):
            b.on("vector", lambda e, o=zbuf[:, dc * TT:(dc + 1) * TT],
                        a=y_acc[:, dc * TT:(dc + 1) * TT], c=x1f[:, dc * TT:(dc + 1) * TT]:
                 e.tensor_tensor(out=o, in0=a, in1=c, op=OP.add),
                 wait=[("sc", b.cnt["sc"])] if dc == 0 else None, inc="ve" if dc == 7 else None)
        x1_cons[par] = ("ve", b.cnt["ve"])
        ve_out = emit_post_ln(b, zbuf, TT, sqtile, rows, pstat1, pstat2, pbc1, pbc2,
                              onescol, onesrow, epscol,
                              outf=lambda cc: x2buf[:, cc * TT:(cc + 1) * TT])
        stt = None
        for cc in range(8):
            stt = b.dma("sync", x2T[cc * 128:(cc + 1) * 128, ts:ts + TT],
                        x2buf[:, cc * TT:(cc + 1) * TT],
                        wait=[("ve", ve_out)] if cc == 0 else None, inc="st")
        b.wait("vector", "st", stt)
    if getattr(b, "dbgF", None) is not None:
        b.dma("sync", b.dbgF["w"][:, :], wT8b[:, :], inc="st")
    b.wait("sync", "st", b.cnt["st"])


def l0_ffn_host_inputs(Wg, W1, b1, W2, b2):
    b1pp = np.concatenate([pp_col(b1[e]) for e in range(8)], axis=1)
    d = {"wg": np.ascontiguousarray(Wg), "w1": np.ascontiguousarray(W1),
         "b1pp": np.ascontiguousarray(b1pp), "w2": np.ascontiguousarray(W2),
         "b2": np.ascontiguousarray(b2)}
    d.update(host_consts_ffn())
    return d


def build_l0_ffn_launch(dbg=False):
    b = B()
    x1T = b.dram_in("x1T", [D, T])
    wg = b.dram_in("wg", [D, 8])
    w1 = b.dram_in("w1", [8, D, 2048])
    b1pp = b.dram_in("b1pp", [128, 128])
    w2 = b.dram_in("w2", [8, 2048, D])
    b2 = b.dram_in("b2", [8, D])
    consts = {k: b.dram_in(k, v.shape, BF16) for k, v in host_consts_ffn().items()}
    w1b = b.dram_tmp("w1b", [8, 16, 128, 1024], BF16)
    wqkvb = b.dram_tmp("wqkvb", [D, 3 * D], BF16)
    wgateb = b.dram_tmp("wgateb", [D, D], BF16)
    woutb = b.dram_tmp("woutb", [D, D], BF16)
    w2b = b.dram_tmp("w2b", [8, 16, 128, 1024], BF16)
    x2T = b.dram_out("x2T", [D, T])
    if dbg:
        b.dbgF = {"w": b.dram_out("dbgF_w", [8, T], BF16),
                  "y": b.dram_out("dbgF_y", [D, 512], F32)}
    else:
        b.dbgF = None
    emit_l0_ffn(b, x1T, x2T, wg, w1, b1pp, w2, b2, w1b, w2b, consts)
    return b.finalize()


def emit_l1_attn(b, x2T, x3T, wqkv, bqkv_pp, wproj, bproj_pp, attn_dram):
    b.presem()
    TT = 512
    NTT = 8
    b.identb = b.sb("G_identb", [128, 128], BF16)
    _onesb = b.sb("G_onesb", [128, 128], BF16)
    b.on("vector", lambda e, o=_onesb[:, :]: e.memset(o, 1.0), inc="ve")
    b.on("gpsimd", lambda e, o=b.identb[:, :], i=_onesb[:, :]:
         e.affine_select(out=o, in_=i, pattern=[[1, 128]], compare_op=OP.is_equal,
                         fill=0.0, base=0, channel_multiplier=-1),
         wait=[("ve", b.cnt["ve"])], inc="gp")
    b._identb_gp = b.cnt["gp"]

    x2b = b.sb("G_x2b", [128, 8 * T], BF16)
    qTb = b.sb("G_qTb", [128, T], BF16)
    kTb = b.sb("G_kTb", [128, T], BF16)
    vtmp = b.sb("G_vtmp", [128, T], BF16)
    vtok = b.sb("G_vtok", [128, 2 * 32 * 65], BF16)
    ptb = [b.sb(f"G_ptb{i}", [128, TT], BF16) for i in range(4)]
    obuf = b.sb("G_obuf", [64, TT], F32)
    rec = b.sb("G_rec", [1, TT], F32)
    recb = b.sb("G_recb", [1, TT], BF16)
    atile = b.sb("G_atile", [128, TT], BF16)
    attsb = b.sb("G_attsb", [128, 8 * TT], BF16)
    wstage = [b.sb(f"G_wstage{i}", [128, 1024], F32) for i in range(2)]
    wbf = [b.sb(f"G_wbf{i}", [128, 1024], BF16) for i in range(2)]
    bqkv_sb = b.sb("G_bqkv", [128, 24], F32)
    bqs_sb = b.sb("G_bqs", [128, 8], F32)
    bproj_sb = b.sb("G_bproj", [128, 8], F32)
    stg = [b.sb(f"G_stg{i}", [128, 2048], F32) for i in range(2)]
    x2f = b.sb("G_x2f", [128, 8 * TT], F32)
    zbuf = b.sb("G_zbuf", [128, 8 * TT], F32)
    x3buf = b.sb("G_x3buf", [128, 8 * TT], F32)
    onescol = b.sb("G_onescol", [128, 1], BF16)
    onesrow = b.sb("G_onesrow", [1, 128], BF16)
    epscol = b.sb("G_epscol", [128, 1], F32)
    sqtile = b.sb("G_sqtile", [128, TT], BF16)
    b._ln_zbf = b.sb("G_lnzbf", [128, TT], BF16)
    b._ln_rows_bf = b.sb("G_lnrbf", [1, 2 * TT], BF16)
    rows = b.sb("G_rows", [1, 4 * TT], F32)

    pS = [b.ps(f"G_pS{i}", [128, TT]) for i in range(2)]
    b.push_scope()
    pS2 = [b.ps(f"G_pS2_{i}", [128, TT]) for i in range(2)]
    pO_ = [b.ps(f"G_pO{i}", [65, TT]) for i in range(2)]
    pB = b.ps("G_pB", [64, TT])
    pTR = b.ps("G_pTR", [128, 128], BF16)

    pS_rot = Rot(pS)
    wrot = Rot(list(range(2)))
    b._wconvG = [None, None]
    b._sa_tok = {}

    pSatt_rot = Rot(pS + pS2)
    b.dma("sync", bqkv_sb[:, :], bqkv_pp[:, :], inc="lc")
    b.dma("sync", bproj_sb[:, :], bproj_pp[:, :], inc="lc")
    lc_tot = b.cnt["lc"]
    b.on("vector", lambda e, o=onescol[:, :]: e.memset(o, 1.0))
    b.on("vector", lambda e, o=onesrow[:, :]: e.memset(o, 1.0))
    b.on("vector", lambda e, o=epscol[:, :]: e.memset(o, EPS))
    b.on("vector", lambda e, o=bqs_sb[:, :], i=bqkv_sb[:, 0:8]:
         e.tensor_scalar(out=o, in0=i, scalar1=0.125, scalar2=None, op0=OP.mult),
         wait=[("lc", lc_tot)], inc="ve")
    ve_consts = b.cnt["ve"]

    stg_tok = [("ve", b.cnt["ve"]), ("ve", b.cnt["ve"])]
    for cc in range(8):
        for half in range(2):
            i = (cc * 2 + half) % 2
            sem = f"lx{i}"
            ldc = b.dma("sync", stg[i][:, :],
                        x2T[cc * 128:(cc + 1) * 128, half * 2048:(half + 1) * 2048],
                        wait=[stg_tok[i]], inc=sem)
            t = b.on("vector",
                     lambda e, o=x2b[:, cc * T + half * 2048: cc * T + (half + 1) * 2048],
                            s=stg[i][:, :]: e.tensor_copy(out=o, in_=s),
                     wait=[(sem, ldc)], inc="ve")
            stg_tok[i] = ("ve", t)
    x2_ready = b.cnt["ve"]

    def load_w_tiles(srcT, fbase):
        i, _, tok = wrot.get()
        sem = f"lw{i}"
        srcv = srcT.rearrange("(cc p) f -> p cc f", p=128)
        lds = b.dma("sync", wstage[i][:, :].rearrange("p (cc j) -> p cc j", j=128),
                    srcv[:, :, fbase:fbase + 128],
                    wait=[("ve", b._wconvG[i])] if b._wconvG[i] else None, inc=sem)
        vtok_ = b.on("vector",
                     lambda e, o=wbf[i][:, :], s=wstage[i][:, :]: e.tensor_copy(out=o, in_=s),
                     wait=[(sem, lds)] + ([tok] if tok else []), inc="ve")
        b._wconvG[i] = vtok_
        return i, vtok_

    for hg in range(8):
        vtok_ms = b.on("vector", lambda e, o=vtok[:, :]: e.memset(o, 1.0),
                       wait=[("mm", b.cnt["mm"])], inc="ve")
        for s_i, s_name in ((0, "q"), (1, "k"), (2, "v")):
            wi, wtok = load_w_tiles(wqkv, s_i * 1024 + hg * 128)
            for tt in range(NTT):
                pi, pt, ptok = pSatt_rot.get()
                for cc in range(8):
                    w = None
                    if cc == 0:
                        w = [("ve", wtok), ("ve", x2_ready)] + ([ptok] if ptok else [])
                    b.on("tensor",
                         lambda e, p=pt[:, 0:TT], l=wbf[wi][:, cc * 128:(cc + 1) * 128],
                                r=x2b[:, cc * T + tt * TT: cc * T + (tt + 1) * TT],
                                st=(cc == 0), sp=(cc == 7): e.matmul(p, l, r, start=st, stop=sp),
                         wait=w, inc="mm" if cc == 7 else None)
                mm_done = b.cnt["mm"]
                wrot.tok[wi] = ("mm", mm_done)
                if s_name == "q":
                    vet = b.on("vector",
                               lambda e, o=qTb[:, tt * TT:(tt + 1) * TT], i=pt[:, 0:TT],
                                      bi=bqs_sb[:, hg:hg + 1].broadcast_to((128, TT)):
                               e.scalar_tensor_tensor(out=o, in0=i, scalar=0.125, in1=bi,
                                                      op0=OP.mult, op1=OP.add),
                               wait=[("mm", mm_done), ("ve", ve_consts)], inc="ve")
                else:
                    dst = kTb if s_name == "k" else vtmp
                    bias = bqkv_sb[:, 8 + hg:9 + hg] if s_name == "k" else bqkv_sb[:, 16 + hg:17 + hg]
                    vet = b.on("vector",
                               lambda e, o=dst[:, tt * TT:(tt + 1) * TT], i=pt[:, 0:TT], bi=bias:
                               e.tensor_scalar(out=o, in0=i, scalar1=bi, scalar2=None, op0=OP.add),
                               wait=[("mm", mm_done), ("ve", ve_consts)], inc="ve")
                pSatt_rot.tok[pi] = ("ve", vet)
        qkv_done = b.cnt["ve"]
        tr_prev = [None, None]
        for h2 in range(2):
            for kc in range(32):
                half = kc % 2
                mmt = b.on("tensor",
                           lambda e, p=pTR[:, half * 64:half * 64 + 64],
                                  i=vtmp[h2 * 64:(h2 + 1) * 64, kc * 128:(kc + 1) * 128],
                                  idn=b.identb[h2 * 64:h2 * 64 + 64, h2 * 64:h2 * 64 + 64]: e.transpose(p, i, idn),
                           wait=[("ve", qkv_done), ("ve", vtok_ms), ("gp", b._identb_gp)] + ([tr_prev[half]] if tr_prev[half] else []),
                           inc="mm")
                vtr = b.on("vector",
                           lambda e, o=vtok[:, (h2 * 32 + kc) * 65:(h2 * 32 + kc) * 65 + 64],
                                  i=pTR[:, half * 64:half * 64 + 64]: e.tensor_copy(out=o, in_=i),
                           wait=[("mm", mmt)], inc="ve")
                tr_prev[half] = ("ve", vtr)
        vtok_ready = b.cnt["ve"]

        if not isinstance(getattr(b, "_pO_tok", None), dict):
            b._pO_tok = {}
        for qb in range(NTT):
            def emit_S(h2, kc):
                hb = h2 * 64
                pi, pt, ptok = pSatt_rot.get()
                w = [("ve", vtok_ready)] if kc == 0 else []
                if ptok:
                    w.append(ptok)
                b.on("tensor",
                     lambda e, p=pt[:, 0:TT],
                            l=kTb[hb:hb + 64, kc * 128:(kc + 1) * 128],
                            r=qTb[hb:hb + 64, qb * TT:(qb + 1) * TT]:
                     e.matmul(p, l, r, start=True, stop=True),
                     wait=w if w else None, inc="mm")
                return pi, pt, b.cnt["mm"]

            def emit_exp(h2, kc, pi, pt, mmS):
                pj = (kc % 2) * 2 + h2
                sct = b.on("scalar",
                           lambda e, o=ptb[pj][:, 0:TT], i=pt[:, 0:TT]:
                           e.activation(out=o, in_=i, func=AF.Exp),
                           wait=[("mm", mmS)], inc="sc")
                pSatt_rot.tok[pi] = ("sc", sct)
                return sct

            def emit_O(h2, kc, sct):
                pj = (kc % 2) * 2 + h2
                w2 = [("sc", sct)]
                if kc == 0 and b._pO_tok.get(h2):
                    w2.append(b._pO_tok[h2])
                b.on("tensor",
                     lambda e, p=pO_[h2][:, 0:TT],
                            l=vtok[:, (h2 * 32 + kc) * 65:(h2 * 32 + kc + 1) * 65],
                            r=ptb[pj][:, 0:TT],
                            st=(kc == 0), sp=(kc == 31): e.matmul(p, l, r, start=st, stop=sp),
                     wait=w2, inc="mm" if kc == 31 else None)
                return b.cnt["mm"]

            scts = {}
            pis = {}
            for h2 in range(2):
                pis[(h2, 0)] = emit_S(h2, 0)
                scts[(h2, 0)] = emit_exp(h2, 0, *pis[(h2, 0)])
            for kc in range(1, 32):
                for h2 in range(2):
                    pis[(h2, kc)] = emit_S(h2, kc)
                for h2 in range(2):
                    emit_O(h2, kc - 1, scts[(h2, kc - 1)])
                for h2 in range(2):
                    scts[(h2, kc)] = emit_exp(h2, kc, *pis[(h2, kc)])
            mmOs = {}
            for h2 in range(2):
                mmOs[h2] = emit_O(h2, 31, scts[(h2, 31)])
            for h2 in range(2):
                hb = h2 * 64
                pO = pO_[h2]
                mmO = mmOs[1]
                def _vrecip(e, o=recb[0:1, 0:TT], i=pO[64:65, 0:TT], _nc=b.nc):
                    with _nc.allow_low_precision(reason="bf16 denom for broadcast matmul"):
                        return e.reciprocal(out=o, in_=i)
                vr = b.on("vector", _vrecip, wait=[("mm", mmO)], inc="ve")
                mmB = b.on("tensor", lambda e, p=pB[:, 0:TT], l=onesrow[0:1, 0:64],
                                  r=recb[0:1, 0:TT]: e.matmul(p, l, r, start=True, stop=True),
                           wait=[("ve", vr)], inc="mm")
                vo = b.on("vector", lambda e, o=obuf[:, 0:TT], i=pO[0:64, 0:TT]:
                          e.tensor_copy(out=o, in_=i),
                          wait=[("mm", mmO), ("ve", b.cnt["ve"])], inc="ve")
                sa = f"sa{h2}"
                satok = b._sa_tok.get(sa)
                va = b.on("vector",
                          lambda e, o=atile[hb:hb + 64, 0:TT], a=obuf[:, 0:TT], c=pB[:, 0:TT]:
                          e.tensor_tensor(out=o, in0=a, in1=c, op=OP.mult),
                          wait=[("mm", mmB), ("ve", vo)] + ([satok] if satok else []), inc="ve")
                b._pO_tok[h2] = ("ve", vo)
                sat = b.dma("sync", attn_dram[hg * 128 + hb: hg * 128 + hb + 64,
                                              qb * TT:(qb + 1) * TT],
                            atile[hb:hb + 64, 0:TT], wait=[("ve", va)], inc=sa)
                b._sa_tok[sa] = (sa, sat)


    b.pop_scope()
    b.barrier()
    pS_rot = Rot(pS)
    b.push_scope()
    pstat1 = b.ps("G_pstat1", [1, TT])
    pstat2 = b.ps("G_pstat2", [1, TT])
    pbc1 = b.ps("G_pbc1", [128, TT])
    pbc2 = b.ps("G_pbc2", [128, TT])
    for tt in range(NTT):
        lds = None
        for cc in range(8):
            w0 = None
            if cc == 0:
                w0 = [("ve", b.cnt["ve"]), ("sa0", b.cnt.get("sa0", 0)),
                      ("sa1", b.cnt.get("sa1", 0))]
            lds = b.dma("sync", attsb[:, cc * TT:(cc + 1) * TT],
                        attn_dram[cc * 128:(cc + 1) * 128, tt * TT:(tt + 1) * TT],
                        wait=w0, inc="la")
        ldx = None
        for cc in range(8):
            ldx = b.dma("sync", x2f[:, cc * TT:(cc + 1) * TT],
                        x2T[cc * 128:(cc + 1) * 128, tt * TT:(tt + 1) * TT],
                        wait=[("ve", b.cnt["ve"])] if cc == 0 else None, inc="lxf")
        for dc in range(8):
            wi, wtok = load_w_tiles(wproj, dc * 128)
            pi, pt, ptok = pS_rot.get()
            for cc in range(8):
                w = None
                if cc == 0:
                    w = [("ve", wtok), ("la", lds)] + ([ptok] if ptok else [])
                b.on("tensor",
                     lambda e, p=pt[:, 0:TT], l=wbf[wi][:, cc * 128:(cc + 1) * 128],
                            r=attsb[:, cc * TT:(cc + 1) * TT],
                            st=(cc == 0), sp=(cc == 7): e.matmul(p, l, r, start=st, stop=sp),
                     wait=w, inc="mm" if cc == 7 else None)
            mm_done = b.cnt["mm"]
            wrot.tok[wi] = ("mm", mm_done)
            vt = b.on("vector",
                      lambda e, o=zbuf[:, dc * TT:(dc + 1) * TT], i=pt[:, 0:TT],
                             bi=bproj_sb[:, dc:dc + 1], c=x2f[:, dc * TT:(dc + 1) * TT]:
                      e.scalar_tensor_tensor(out=o, in0=i, scalar=bi, in1=c,
                                             op0=OP.add, op1=OP.add),
                      wait=[("mm", mm_done), ("lxf", ldx), ("sc", b.cnt["sc"])], inc="ve")
            pS_rot.tok[pi] = ("ve", vt)
        ve_out = emit_post_ln(b, zbuf, TT, sqtile, rows, pstat1, pstat2, pbc1, pbc2,
                              onescol, onesrow, epscol,
                              outf=lambda cc: x3buf[:, cc * TT:(cc + 1) * TT])
        stt = None
        for cc in range(8):
            stt = b.dma("sync", x3T[cc * 128:(cc + 1) * 128, tt * TT:(tt + 1) * TT],
                        x3buf[:, cc * TT:(cc + 1) * TT],
                        wait=[("ve", ve_out)] if cc == 0 else None, inc="st")
        b.wait("vector", "st", stt)
    b.pop_scope()
    b.wait("sync", "st", b.cnt["st"])


def l1_attn_host_inputs(W_qkv, b_qkv, W_proj, b_proj):
    return {"wqkv": np.ascontiguousarray(W_qkv),
            "bqkv_pp": pp_col(b_qkv),
            "wproj": np.ascontiguousarray(W_proj),
            "bproj_pp": pp_col(b_proj)}


def build_l1_attn_launch():
    b = B()
    x2T = b.dram_in("x2T", [D, T])
    wqkv = b.dram_in("wqkv", [D, 3 * D])
    bqkv_pp = b.dram_in("bqkv_pp", [128, 24])
    wproj = b.dram_in("wproj", [D, D])
    bproj_pp = b.dram_in("bproj_pp", [128, 8])
    attn_dram = b.dram_tmp("attn_dram", [D, T], BF16)
    x3T = b.dram_out("x3T", [D, T])
    emit_l1_attn(b, x2T, x3T, wqkv, bqkv_pp, wproj, bproj_pp, attn_dram)
    return b.finalize()


def butterfly_host_arrays(up_w, dn_w):
    DH = 4096

    def stage_block_mats(w, s):
        S = 2 ** s
        M = np.zeros((32, 128, 128), np.float32)
        for c in range(DH):
            blk, i = divmod(c, 128)
            if (c // S) % 2 == 0:
                k = (c // (2 * S)) * S + (c % S)
                M[blk, i, i] = w[s, k, 0, 0]
                M[blk, i, i + S] = w[s, k, 0, 1]
            else:
                cp = c - S
                k = (cp // (2 * S)) * S + (cp % S)
                M[blk, i, i] = w[s, k, 1, 1]
                M[blk, i, i - S] = w[s, k, 1, 0]
        return M

    def coeffs(w, s, transposed):
        S = 2 ** s
        A = np.zeros(DH, np.float32); Bv = np.zeros(DH, np.float32)
        for c in range(DH):
            if (c // S) % 2 == 0:
                k = (c // (2 * S)) * S + (c % S)
                A[c] = w[s, k, 0, 0]
                Bv[c] = w[s, k, 0, 1] if not transposed else w[s, k, 1, 0]
            else:
                cp = c - S
                k = (cp // (2 * S)) * S + (cp % S)
                A[c] = w[s, k, 1, 1]
                Bv[c] = w[s, k, 1, 0] if not transposed else w[s, k, 0, 1]
        return pp_col(A), pp_col(Bv)

    ups = np.stack([stage_block_mats(up_w, s) for s in range(6)])
    ups6t = np.ascontiguousarray(stage_block_mats(up_w, 6).transpose(0, 2, 1))
    dns = [stage_block_mats(dn_w, s) for s in range(7)]
    dns0 = dns[0]
    dnst = np.stack([np.ascontiguousarray(dns[s].transpose(0, 2, 1)) for s in range(1, 7)])
    cup = np.zeros((128, 320), np.float32)
    cdn = np.zeros((128, 320), np.float32)
    for s in range(7, 12):
        A, Bv = coeffs(up_w, s, False)
        cup[:, (s - 7) * 64:(s - 7) * 64 + 32] = A
        cup[:, (s - 7) * 64 + 32:(s - 7) * 64 + 64] = Bv
        A, Bv = coeffs(dn_w, s, True)
        cdn[:, (s - 7) * 64:(s - 7) * 64 + 32] = A
        cdn[:, (s - 7) * 64 + 32:(s - 7) * 64 + 64] = Bv
    return {"ups": ups, "ups6t": ups6t, "dns0": dns0, "dnst": dnst,
            "cup": cup, "cdn": cdn}


def emit_l1_ffn(b, x3T, x4T, vaug, bexp_pp, wcon, ups, ups6t, dns0, dnst, cup, cdn,
                upbias_pp, bcon_pp, dbias_pp, wexp_scr, wcon_scr):
    b.presem()
    TT = 512
    NTT = 8
    DH = 4096

    pool1 = b.sb("H_pool1", [128, 16384], F32)
    pool2 = b.sb("H_pool2", [128, 16384], F32)
    bufA = pool1
    bufB = pool2
    pool1b = pool1[:, :].bitcast(BF16)
    pool2b = pool2[:, :].bitcast(BF16)
    def wcon_full_sl(a, c):
        return pool1b[:, a:c]
    def h3_sl(a, c):
        return pool2b[:, a:c]
    zbuf = pool2[:, 8192:12288]
    x4buf = pool2[:, 12288:16384]
    chainM = b.sb("H_chainM", [128, 4096], F32)
    tb = [b.sb(f"H_tb{i}", [128, 128], BF16) for i in range(2)]
    trs = [b.sb(f"H_trs{i}", [128, 128], BF16) for i in range(2)]
    cupsb = b.sb("H_cup", [128, 320], F32)
    cdnsb = b.sb("H_cdn", [128, 320], F32)
    upbias_sb = b.sb("H_upb", [128, 32], F32)
    gbias = b.sb("H_gbias", [128, 32], F32)
    beffc = b.sb("H_beffc", [128, 32], F32)
    bcol = [b.sb(f"H_bcol{i}", [128, 32], F32) for i in range(2)]
    bcolt = b.sb("H_bcolt", [128, 1], F32)
    ybias = b.sb("H_ybias", [128, 8], F32)
    bcon_sb = b.sb("H_bcon", [128, 8], F32)
    dbias_sb = b.sb("H_dbias", [128, 8], F32)
    x3f_ = [b.sb(f"H_x3f{i}", [128, 8 * TT], F32) for i in range(2)]
    x3bt_ = [b.sb(f"H_x3bt{i}", [128, 8 * TT], BF16) for i in range(2)]
    ttile = x3f_[1][:, 0:TT]
    ttile2 = x3f_[1][:, TT:2 * TT]
    scscr = [x3f_[1][:, 2 * TT:3 * TT], x3f_[1][:, 3 * TT:4 * TT]]
    b._stg_slot = [None, None]
    wexp_sl = [b.sb(f"H_wexp{i}", [128, 1024], BF16) for i in range(2)]
    onesb = b.sb("H_onesb", [128, 128], BF16)
    identb = b.sb("H_identb", [128, 128], BF16)
    onescol = b.sb("H_onescol", [128, 1], BF16)
    onesrow = b.sb("H_onesrow", [1, 128], BF16)
    epscol = b.sb("H_epscol", [128, 1], F32)
    sqtile = chainM[:, 2048:2304].bitcast(BF16)
    b._ln_zbf = chainM[:, 2304:2560].bitcast(BF16)
    b._ln_rows_bf = chainM[0:1, 2560:3072].bitcast(BF16)
    rows = chainM[0:1, 0:2048]

    ph = [b.ps(f"H_ph{i}", [128, TT]) for i in range(2)]
    py = [b.ps(f"H_py{i}", [128, TT]) for i in range(2)]
    b.push_scope()
    pTRb = b.ps("H_pTRb", [128, 128], BF16)

    ph_rot = Rot(ph)
    py_rot = Rot(py)

    for dst, src in [(cupsb, cup), (cdnsb, cdn), (upbias_sb, upbias_pp),
                     (bcon_sb, bcon_pp), (dbias_sb, dbias_pp)]:
        b.dma("sync", dst[:, :], src[:, :], inc="lc")
    lc_tot = b.cnt["lc"]
    b.on("vector", lambda e, o=onescol[:, :]: e.memset(o, 1.0))
    b.on("vector", lambda e, o=onesrow[:, :]: e.memset(o, 1.0))
    b.on("vector", lambda e, o=epscol[:, :]: e.memset(o, EPS))
    b.on("vector", lambda e, o=onesb[:, :]: e.memset(o, 1.0), inc="ve")
    b.on("vector", lambda e, o=ybias[:, :], a=bcon_sb[:, :], c=dbias_sb[:, :]:
         e.tensor_tensor(out=o, in0=a, in1=c, op=OP.add), wait=[("lc", lc_tot)], inc="ve")
    gp_id = b.on("gpsimd", lambda e, o=identb[:, :], i=onesb[:, :]:
                 e.affine_select(out=o, in_=i, pattern=[[1, 128]], compare_op=OP.is_equal,
                                 fill=0.0, base=0, channel_multiplier=-1),
                 wait=[("ve", b.cnt["ve"])], inc="gp")

    def build_chain(dst_name, seed_arr, mats, ascending):
        ld0 = None
        for bi in range(32):
            ld0 = b.dma("sync", chainM[:, bi * 128:(bi + 1) * 128], seed_arr[bi, :, :],
                        wait=[("ve", b.cnt["ve"]), ("mm", b.cnt["mm"])] if bi == 0 else None,
                        inc="lm0")
        ve_prev = ("lm0", ld0)
        for si, mat in enumerate(mats):
            half = si % 2
            base = half * 4096
            sem = f"ls{half}"
            lds = None
            for bi in range(32):
                w0 = None
                if bi == 0:
                    w0 = [("mm", b._chain_halfmm[half])] if b._chain_halfmm[half] else None
                    if si == 0:
                        w0 = (w0 or []) + [("ve", b.cnt["ve"]), ("gp", b.cnt["gp"]),
                                           ("mm", b.cnt["mm"])]
                lds = b.dma("sync", bufA[:, base + bi * 128:base + (bi + 1) * 128],
                            mat[bi, :, :], wait=w0, inc=sem)
            for bi in range(32):
                pi, pt, ptok = ph_rot.get()
                w = [(sem, lds), ve_prev] if bi == 0 else ([ptok] if ptok else None)
                if bi == 0 and ptok:
                    w.append(ptok)
                mmt = b.on("tensor",
                           lambda e, p=pt[:, 0:128],
                                  l=bufA[:, base + bi * 128:base + (bi + 1) * 128],
                                  r=chainM[:, bi * 128:(bi + 1) * 128]:
                           e.matmul(p, l, r, start=True, stop=True),
                           wait=w, inc="mm")
                vct = b.on("scalar",
                           lambda e, o=chainM[:, bi * 128:(bi + 1) * 128], i=pt[:, 0:128]:
                           e.copy(out=o, in_=i), wait=[("mm", mmt)], inc="sc")
                ph_rot.tok[pi] = ("sc", vct)
            b._chain_halfmm[half] = b.cnt["mm"]
            ve_prev = ("sc", b.cnt["sc"])
        return b.cnt["sc"]
    b._chain_halfmm = [None, None]

    def stages_pass(src_getcol, W, bufs, cur, stage_list, coefsb, tok0):
        for s in stage_list:
            Sc = 2 ** (s - 7)
            base = (s - 7) * 64
            ve_bar = b.cnt["ve"]; gp_bar = b.cnt["gp"]; sc_bar = b.cnt["sc"]
            nsc = 0
            for cb in range(32):
                pb = cb ^ Sc
                eng = "vector" if cb % 5 != 4 else "gpsimd"
                acol = coefsb[:, base + cb:base + cb + 1]
                bcol_ = coefsb[:, base + 32 + cb:base + 33 + cb]
                w0 = [("ve", ve_bar), ("gp", gp_bar), ("sc", sc_bar)] + ([tok0] if tok0 else [])
                if eng == "vector":
                    slot = nsc % 2
                    nsc += 1
                    sg = b._stg_slot[slot]
                    sct = b.on("scalar",
                               lambda e, o=scscr[slot], i=bufs[cur](cb, W), a=acol:
                               e.activation(out=o, in_=i, func=AF.Identity, scale=a),
                               wait=w0 + ([sg] if sg else []), inc="sc")
                    vt1 = b.on("vector",
                               lambda e, o=bufs[1 - cur](cb, W), i=bufs[cur](pb, W), bc=bcol_,
                                      t=scscr[slot]:
                               e.scalar_tensor_tensor(out=o, in0=i, scalar=bc, in1=t,
                                                      op0=OP.mult, op1=OP.add),
                               wait=[("sc", sct)], inc="ve")
                    b._stg_slot[slot] = ("ve", vt1)
                else:
                    t1 = b.on(eng, lambda e, o=ttile2, i=bufs[cur](cb, W),
                                     a=acol.broadcast_to((128, W)):
                              e.tensor_tensor(out=o, in0=i, in1=a, op=OP.mult),
                              wait=w0, inc="gp")
                    t2 = b.on(eng, lambda e, o=bufs[1 - cur](cb, W), i=bufs[cur](pb, W),
                                     bc=bcol_.broadcast_to((128, W)):
                              e.tensor_tensor(out=o, in0=i, in1=bc, op=OP.mult),
                              wait=[("gp", t1)], inc="gp")
                    b.on(eng, lambda e, o=bufs[1 - cur](cb, W), a=bufs[1 - cur](cb, W),
                              t=ttile2:
                         e.tensor_tensor(out=o, in0=a, in1=t, op=OP.add),
                         wait=[("gp", t2)], inc="gp")
            tok0 = None
            cur = 1 - cur
        b.on("vector", None, wait=[("gp", b.cnt["gp"])])
        b.on("gpsimd", None, wait=[("ve", b.cnt["ve"])])
        return cur

    bufAc = lambda cb, W: bufA[:, cb * W:(cb + 1) * W]
    bufBc = lambda cb, W: bufB[:, cb * W:(cb + 1) * W]
    bufs = {0: bufAc, 1: bufBc}

    ve_chain_up = build_chain("up", ups6t, [ups[s] for s in range(5, -1, -1)], None)
    for dh in range(2):
        W = 512
        lv = None
        for bi in range(32):
            lv = b.dma("sync", bufA[:, bi * W:(bi + 1) * W],
                       vaug[bi * 128:(bi + 1) * 128, dh * 512:(dh + 1) * 512],
                       wait=[("ve", b.cnt["ve"]), ("gp", b.cnt["gp"]),
                             ("mm", b.cnt["mm"]), ("sw0", b.cnt.get("sw0", 0)),
                             ("sw1", b.cnt.get("sw1", 0))] if bi == 0 else None,
                       inc="lv")
        for bi in range(32):
            pi, pt, ptok = ph_rot.get()
            mmt = b.on("tensor",
                       lambda e, p=pt[:, 0:W], l=chainM[:, bi * 128:(bi + 1) * 128],
                              r=bufA[:, bi * W:(bi + 1) * W]:
                       e.matmul(p, l, r, start=True, stop=True),
                       wait=[("lv", lv), ("sc", ve_chain_up)] + ([ptok] if ptok else []),
                       inc="mm")
            vt = b.on("scalar", lambda e, o=bufB[:, bi * W:(bi + 1) * W], i=pt[:, 0:W]:
                      e.copy(out=o, in_=i), wait=[("mm", mmt)], inc="sc")
            ph_rot.tok[pi] = ("sc", vt)
        cur = stages_pass(None, W, bufs, 1, range(7, 12), cupsb, None)
        if not hasattr(b, "_tb_tok"):
            b._tb_tok = [None, None]
            b._trs_tok = [None, None]
            b._ptr_free = None
        for bi in range(32):
            for dq in range(4):
                i2 = (bi * 4 + dq) % 2
                vt = b.on("vector",
                          lambda e, o=tb[i2][:, :],
                                 i=bufs[cur](bi, W)[:, dq * 128:(dq + 1) * 128]:
                          e.tensor_copy(out=o, in_=i),
                          wait=[b._tb_tok[i2]] if b._tb_tok[i2] else None, inc="ve")
                mmt = b.on("tensor", lambda e, p=pTRb[:, :], i=tb[i2][:, :], idn=identb[:, :]:
                           e.transpose(p, i, idn),
                           wait=[("ve", vt), ("gp", gp_id)] +
                                ([b._ptr_free] if b._ptr_free else []), inc="mm")
                b._tb_tok[i2] = ("mm", mmt)
                vt2 = b.on("scalar", lambda e, o=trs[i2][:, :], i=pTRb[:, :]:
                           e.copy(out=o, in_=i),
                           wait=[("mm", mmt)] + ([b._trs_tok[i2]] if b._trs_tok[i2] else []),
                           inc="sc")
                b._ptr_free = ("sc", vt2)
                sem = f"sw{i2}"
                stok = b.dma("sync",
                             wexp_scr[bi, :, (dh * 4 + dq) * 128:(dh * 4 + dq + 1) * 128],
                             trs[i2][:, :], wait=[("sc", vt2)], inc=sem)
                b._trs_tok[i2] = (sem, stok)
    bcur = 0
    lvb = b.dma("sync", bcol[0][:, :], bexp_pp[:, :],
                wait=[("ve", b.cnt["ve"])], inc="lvb")
    mm_prev = None
    for bi in range(32):
        mmt = b.on("tensor",
                   lambda e, p=ph[0][:, 0:1], l=chainM[:, bi * 128:(bi + 1) * 128],
                          r=bcol[0][:, bi:bi + 1]:
                   e.matmul(p, l, r, start=True, stop=True),
                   wait=[("lvb", lvb), ("ve", b.cnt["ve"])] if bi == 0 else None, inc="mm")
        b.on("vector", lambda e, o=bcol[1][:, bi:bi + 1], i=ph[0][:, 0:1]:
             e.tensor_copy(out=o, in_=i), wait=[("mm", mmt)], inc="ve")
    bcur = 1
    for s in range(7, 12):
        Sc = 2 ** (s - 7)
        base = (s - 7) * 64
        bar = b.on("vector", lambda e, o=bcolt[:, 0:1]: e.memset(o, 0.0), inc="ve")
        for cb in range(32):
            pb = cb ^ Sc
            t1 = b.on("vector", lambda e, o=bcolt[:, 0:1], i=bcol[bcur][:, cb:cb + 1],
                             a=cupsb[:, base + cb:base + cb + 1]:
                      e.tensor_scalar(out=o, in0=i, scalar1=a, scalar2=None, op0=OP.mult),
                      wait=[("ve", b.cnt["ve"])], inc="ve")
            b.on("vector", lambda e, o=bcol[1 - bcur][:, cb:cb + 1], i=bcol[bcur][:, pb:pb + 1],
                        bc=cupsb[:, base + 32 + cb:base + 33 + cb], t=bcolt[:, 0:1]:
                 e.scalar_tensor_tensor(out=o, in0=i, scalar=bc, in1=t,
                                        op0=OP.mult, op1=OP.add),
                 wait=[("ve", t1)], inc="ve")
        bcur = 1 - bcur
    b.on("vector", lambda e, o=gbias[:, :], a=bcol[bcur][:, :], c=upbias_sb[:, :]:
         e.tensor_tensor(out=o, in0=a, in1=c, op=OP.add),
         wait=[("ve", b.cnt["ve"]), ("lc", lc_tot)], inc="ve")
    gbias_tok = b.cnt["ve"]

    wsb16 = [b.sb(f"H_wsb16_{i}", [128, 512], BF16) for i in range(2)]
    b._w16_tok = [None, None]
    ve_chain_dn = build_chain("dn", dns0, [dnst[s] for s in range(6)], None)
    for dh in range(2):
        W = 512
        lv = None
        for bi in range(32):
            lv = b.dma("sync", bufA[:, bi * W:(bi + 1) * W],
                       wcon[bi * 128:(bi + 1) * 128, dh * 512:(dh + 1) * 512],
                       wait=[("ve", b.cnt["ve"]), ("gp", b.cnt["gp"]),
                             ("mm", b.cnt["mm"]), ("sw0", b.cnt.get("sw0", 0)),
                             ("sw1", b.cnt.get("sw1", 0))] if bi == 0 else None,
                       inc="lv")
        b.on("vector", None, wait=[("lv", lv)])
        b.on("gpsimd", None, wait=[("lv", lv)])
        cur = stages_pass(None, W, bufs, 0, range(11, 6, -1), cdnsb, None)
        for bi in range(32):
            pi, pt, ptok = ph_rot.get()
            mmt = b.on("tensor",
                       lambda e, p=pt[:, 0:W], l=chainM[:, bi * 128:(bi + 1) * 128],
                              r=bufs[cur](bi, W):
                       e.matmul(p, l, r, start=True, stop=True),
                       wait=[("ve", b.cnt["ve"]), ("gp", b.cnt["gp"]),
                             ("sc", ve_chain_dn)] + ([ptok] if ptok else []),
                       inc="mm")
            vt = b.on("vector", lambda e, o=bufs[1 - cur](bi, W).bitcast(BF16)[:, 0:W],
                             i=pt[:, 0:W]:
                      e.tensor_copy(out=o, in_=i), wait=[("mm", mmt)], inc="ve")
            ph_rot.tok[pi] = ("ve", vt)
            b.dma("sync", wcon_scr[bi, :, dh * 512:(dh + 1) * 512],
                  bufs[1 - cur](bi, W).bitcast(BF16)[:, 0:W],
                  wait=[("ve", vt)], inc="sw0")
    sw_done = b.cnt["sw0"]

    b.pop_scope()
    b.barrier()
    b.push_scope()
    pstat1 = b.ps("H_pstat1", [1, TT])
    pstat2 = b.ps("H_pstat2", [1, TT])
    pbc1 = b.ps("H_pbc1", [128, TT])
    pbc2 = b.ps("H_pbc2", [128, TT])
    lwc = None
    for hcl in range(32):
        lwc = b.dma("sync", wcon_full_sl(hcl * 1024, (hcl + 1) * 1024), wcon_scr[hcl, :, :],
                    wait=[("sw0", sw_done), ("mm", b.cnt["mm"]), ("ve", b.cnt["ve"])]
                    if hcl == 0 else None, inc="lwc")
    x3_cons = [None, None]
    for tt in range(NTT):
        ts = tt * TT
        par = tt % 2
        x3f = x3f_[par]
        x3bt = x3bt_[par]
        lx = None
        for cc in range(8):
            w0 = None
            if cc == 0:
                w0 = [x3_cons[par]] if x3_cons[par] else []
                if tt < 2:
                    w0 += [("ve", b.cnt["ve"]), ("gp", b.cnt["gp"]), ("sc", b.cnt["sc"])]
            lx = b.dma("sync", x3f[:, cc * TT:(cc + 1) * TT],
                       x3T[cc * 128:(cc + 1) * 128, ts:ts + TT],
                       wait=w0, inc=f"lx{par}")
        cvx = b.on("vector", lambda e, o=x3bt[:, :], i=x3f[:, :]: e.tensor_copy(out=o, in_=i),
                   wait=[(f"lx{par}", lx)], inc="ve")
        wslot_tok = [None, None]
        for hc in range(32):
            i2 = hc % 2
            sem = f"lw{i2}"
            ldw = b.dma("sync", wexp_sl[i2][:, :], wexp_scr[hc, :, :],
                        wait=[wslot_tok[i2]] if wslot_tok[i2] else
                             ([("sw0", b.cnt["sw0"]), ("sw1", b.cnt["sw1"])] if hc < 2 and tt == 0 else None),
                        inc=sem)
            pi, pt, ptok = ph_rot.get()
            for cc in range(8):
                w = None
                if cc == 0:
                    w = [(sem, ldw), ("ve", cvx)] + ([ptok] if ptok else [])
                b.on("tensor",
                     lambda e, p=pt[:, 0:TT], l=wexp_sl[i2][:, cc * 128:(cc + 1) * 128],
                            r=x3bt[:, cc * TT:(cc + 1) * TT],
                            st=(cc == 0), sp=(cc == 7): e.matmul(p, l, r, start=st, stop=sp),
                     wait=w, inc="mm" if cc == 7 else None)
            mmh = b.cnt["mm"]
            wslot_tok[i2] = ("mm", mmh)
            sct = b.on("scalar",
                       lambda e, o=h3_sl(hc * TT, (hc + 1) * TT), i=pt[:, 0:TT],
                              bi=gbias[:, hc:hc + 1]:
                       e.activation(out=o, in_=i, func=AF.Gelu, bias=bi, scale=1.0),
                       wait=[("mm", mmh), ("ve", gbias_tok)], inc="sc")
            ph_rot.tok[pi] = ("sc", sct)
        h3_sc = b.cnt["sc"]
        for dc in range(8):
            pi, pt, ptok = py_rot.get()
            for hcl in range(32):
                w = None
                if hcl == 0:
                    w = [("lwc", lwc), ("sc", h3_sc)] + ([ptok] if ptok else [])
                b.on("tensor",
                     lambda e, p=pt[:, 0:TT],
                            l=wcon_full_sl(hcl * 1024 + dc * 128, hcl * 1024 + (dc + 1) * 128),
                            r=h3_sl(hcl * TT, (hcl + 1) * TT),
                            st=(hcl == 0), sp=(hcl == 31): e.matmul(p, l, r, start=st, stop=sp),
                     wait=w, inc="mm" if hcl == 31 else None)
            mmy = b.cnt["mm"]
            vt = b.on("vector",
                      lambda e, o=zbuf[:, dc * TT:(dc + 1) * TT], i=pt[:, 0:TT],
                             bi=ybias[:, dc:dc + 1], c=x3f[:, dc * TT:(dc + 1) * TT]:
                      e.scalar_tensor_tensor(out=o, in0=i, scalar=bi, in1=c,
                                             op0=OP.add, op1=OP.add),
                      wait=[("mm", mmy), ("sc", b.cnt["sc"])], inc="ve")
            py_rot.tok[pi] = ("ve", vt)
        x3_cons[par] = ("ve", b.cnt["ve"])
        ve_out = emit_post_ln(b, zbuf, TT, sqtile, rows, pstat1, pstat2, pbc1, pbc2,
                              onescol, onesrow, epscol,
                              outf=lambda cc: x4buf[:, cc * TT:(cc + 1) * TT])
        stt = None
        for cc in range(8):
            stt = b.dma("sync", x4T[cc * 128:(cc + 1) * 128, ts:ts + TT],
                        x4buf[:, cc * TT:(cc + 1) * TT],
                        wait=[("ve", ve_out)] if cc == 0 else None, inc="st")
        b.wait("vector", "st", stt)
    b.pop_scope()
    b.wait("sync", "st", b.cnt["st"])


def l1_ffn_host_inputs(W_exp, b_exp, up_w, upb, dn_w, W_con, b_con, d_bias):
    d = {"vaug": np.ascontiguousarray(W_exp.T.astype(np.float32)),
         "bexp_pp": pp_col(b_exp),
         "wcon": np.ascontiguousarray(W_con),
         "upbias_pp": pp_col(upb), "bcon_pp": pp_col(b_con), "dbias_pp": pp_col(d_bias)}
    d.update(butterfly_host_arrays(up_w, dn_w))
    return d


def build_l1_ffn_launch():
    b = B()
    x3T = b.dram_in("x3T", [D, T])
    vaug = b.dram_in("vaug", [4096, 1024])
    bexp_pp = b.dram_in("bexp_pp", [128, 32])
    wcon = b.dram_in("wcon", [4096, 1024])
    ups = b.dram_in("ups", [6, 32, 128, 128])
    ups6t = b.dram_in("ups6t", [32, 128, 128])
    dns0 = b.dram_in("dns0", [32, 128, 128])
    dnst = b.dram_in("dnst", [6, 32, 128, 128])
    cup = b.dram_in("cup", [128, 320])
    cdn = b.dram_in("cdn", [128, 320])
    upbias_pp = b.dram_in("upbias_pp", [128, 32])
    bcon_pp = b.dram_in("bcon_pp", [128, 8])
    dbias_pp = b.dram_in("dbias_pp", [128, 8])
    wexp_scr = b.dram_tmp("wexp_scr", [32, 128, 1024], BF16)
    wcon_scr = b.dram_tmp("wcon_scr", [32, 128, 1024], BF16)
    x4T = b.dram_out("x4T", [D, T])
    emit_l1_ffn(b, x3T, x4T, vaug, bexp_pp, wcon,
                ups, ups6t, dns0, dnst, cup, cdn,
                upbias_pp, bcon_pp, dbias_pp, wexp_scr, wcon_scr)
    return b.finalize()


_CACHE = {}


def _programs():
    if "p" not in _CACHE:
        _CACHE["p"] = (build_l0_attn_launch(), build_l0_ffn_launch(),
                       build_l1_attn_launch(), build_l1_ffn_launch())
    return _CACHE["p"]


def kernel_unfused(**inputs):
    x0 = np.asarray(inputs["L0_attn_x"], dtype=np.float32)
    NCORE = 8
    nc1, nc2, nc3, nc4 = _programs()

    h1 = l0_attn_host_inputs(None, inputs["L0_attn_Wqkv"], inputs["L0_attn_Wgate"],
                             inputs["L0_attn_Wout"], inputs["L0_attn_bgate"],
                             inputs["L0_attn_decay_param"])
    del h1["x0T"]
    im1 = []
    for c in range(NCORE):
        d = dict(h1)
        d["x0T"] = np.ascontiguousarray(x0[c].T)
        im1.append(d)
    r1 = run_bass_kernel_spmd(nc1, im1, core_ids=list(range(NCORE)))

    h2 = l0_ffn_host_inputs(inputs["L0_ffn_Wg"], inputs["L0_ffn_W1"], inputs["L0_ffn_b1"],
                            inputs["L0_ffn_W2"], inputs["L0_ffn_b2"])
    im2 = []
    for c in range(NCORE):
        d = dict(h2)
        d["x1T"] = np.ascontiguousarray(r1.results[c]["x1T"])
        im2.append(d)
    r2 = run_bass_kernel_spmd(nc2, im2, core_ids=list(range(NCORE)))

    h3 = l1_attn_host_inputs(inputs["L1_attn_W_qkv"], inputs["L1_attn_b_qkv"],
                             inputs["L1_attn_W_proj"], inputs["L1_attn_b_proj"])
    im3 = []
    for c in range(NCORE):
        d = dict(h3)
        d["x2T"] = np.ascontiguousarray(r2.results[c]["x2T"])
        im3.append(d)
    r3 = run_bass_kernel_spmd(nc3, im3, core_ids=list(range(NCORE)))

    h4 = l1_ffn_host_inputs(inputs["L1_ffn_W_exp"], inputs["L1_ffn_b_exp"],
                            inputs["L1_ffn_up_weights"], inputs["L1_ffn_up_bias"],
                            inputs["L1_ffn_down_weights"], inputs["L1_ffn_W_con"],
                            inputs["L1_ffn_b_con"], inputs["L1_ffn_down_bias"])
    im4 = []
    for c in range(NCORE):
        d = dict(h4)
        d["x3T"] = np.ascontiguousarray(r3.results[c]["x3T"])
        im4.append(d)
    r4 = run_bass_kernel_spmd(nc4, im4, core_ids=list(range(NCORE)))

    out = np.stack([np.ascontiguousarray(r4.results[c]["x4T"].T) for c in range(NCORE)])
    return out.astype(np.float32, copy=False)


def build_fused():
    b = B()
    x0T = b.dram_in("x0T", [D, T])
    wqkvT = b.dram_in("wqkvT", [D, 3 * D])
    wgateT = b.dram_in("wgateT", [D, D])
    woutT = b.dram_in("woutT", [D, D])
    bgate_pp = b.dram_in("bgate_pp", [128, 8])
    decay_p = b.dram_in("decay_p", [16, 1])
    consts0 = {k: b.dram_in(k, v.shape) for k, v in host_consts_l0().items()}
    wg = b.dram_in("wg", [D, 8])
    w1 = b.dram_in("w1", [8, D, 2048])
    b1pp = b.dram_in("b1pp", [128, 128])
    w2 = b.dram_in("w2", [8, 2048, D])
    b2 = b.dram_in("b2", [8, D])
    constsF = {k: b.dram_in(k, v.shape, BF16) for k, v in host_consts_ffn().items()}
    wqkv1 = b.dram_in("wqkv1", [D, 3 * D])
    bqkv_pp = b.dram_in("bqkv_pp", [128, 24])
    wproj = b.dram_in("wproj", [D, D])
    bproj_pp = b.dram_in("bproj_pp", [128, 8])
    vaug = b.dram_in("vaug", [4096, 1024])
    bexp_pp = b.dram_in("bexp_pp", [128, 32])
    wcon = b.dram_in("wcon", [4096, 1024])
    ups = b.dram_in("ups", [6, 32, 128, 128])
    ups6t = b.dram_in("ups6t", [32, 128, 128])
    dns0 = b.dram_in("dns0", [32, 128, 128])
    dnst = b.dram_in("dnst", [6, 32, 128, 128])
    cup = b.dram_in("cup", [128, 320])
    cdn = b.dram_in("cdn", [128, 320])
    upbias_pp = b.dram_in("upbias_pp", [128, 32])
    bcon_pp = b.dram_in("bcon_pp", [128, 8])
    dbias_pp = b.dram_in("dbias_pp", [128, 8])
    x4T = b.dram_out("x4T", [D, T])
    x1T = b.dram_tmp("x1T", [D, T])
    x2T = b.dram_tmp("x2T", [D, T])
    x3T = b.dram_tmp("x3T", [D, T])
    w1b = b.dram_tmp("w1b", [8, 16, 128, 1024], BF16)
    wqkvb = b.dram_tmp("wqkvb", [D, 3 * D], BF16)
    wgateb = b.dram_tmp("wgateb", [D, D], BF16)
    woutb = b.dram_tmp("woutb", [D, D], BF16)
    w2b = b.dram_tmp("w2b", [8, 16, 128, 1024], BF16)
    attn_dram = b.dram_tmp("attn_dram", [D, T], BF16)
    wexp_scr = b.dram_tmp("wexp_scr", [32, 128, 1024], BF16)
    wcon_scr = b.dram_tmp("wcon_scr", [32, 128, 1024], BF16)

    b.begin_phase()
    emit_l0_attn(b, x0T, x1T, wqkvT, wgateT, woutT, bgate_pp, decay_p, consts0,
                 wqkvb, wgateb, woutb)
    b.end_phase()
    b.barrier()
    b.begin_phase()
    emit_l0_ffn(b, x1T, x2T, wg, w1, b1pp, w2, b2, w1b, w2b, constsF)
    b.end_phase()
    b.barrier()
    b.begin_phase()
    emit_l1_attn(b, x2T, x3T, wqkv1, bqkv_pp, wproj, bproj_pp, attn_dram)
    b.end_phase()
    b.barrier()
    b.begin_phase()
    emit_l1_ffn(b, x3T, x4T, vaug, bexp_pp, wcon,
                ups, ups6t, dns0, dnst, cup, cdn,
                upbias_pp, bcon_pp, dbias_pp, wexp_scr, wcon_scr)
    b.end_phase()
    return b.finalize()


def kernel(**inputs):
    x0 = np.asarray(inputs["L0_attn_x"], dtype=np.float32)
    NCORE = 8
    if "fused" not in _CACHE:
        _CACHE["fused"] = build_fused()
    nc = _CACHE["fused"]
    h1 = l0_attn_host_inputs(None, inputs["L0_attn_Wqkv"], inputs["L0_attn_Wgate"],
                             inputs["L0_attn_Wout"], inputs["L0_attn_bgate"],
                             inputs["L0_attn_decay_param"])
    del h1["x0T"]
    h2 = l0_ffn_host_inputs(inputs["L0_ffn_Wg"], inputs["L0_ffn_W1"], inputs["L0_ffn_b1"],
                            inputs["L0_ffn_W2"], inputs["L0_ffn_b2"])
    h3 = l1_attn_host_inputs(inputs["L1_attn_W_qkv"], inputs["L1_attn_b_qkv"],
                             inputs["L1_attn_W_proj"], inputs["L1_attn_b_proj"])
    h3 = {("wqkv1" if k == "wqkv" else k): v for k, v in h3.items()}
    h4 = l1_ffn_host_inputs(inputs["L1_ffn_W_exp"], inputs["L1_ffn_b_exp"],
                            inputs["L1_ffn_up_weights"], inputs["L1_ffn_up_bias"],
                            inputs["L1_ffn_down_weights"], inputs["L1_ffn_W_con"],
                            inputs["L1_ffn_b_con"], inputs["L1_ffn_down_bias"])
    shared = {}
    shared.update(h1); shared.update(h2); shared.update(h3); shared.update(h4)
    im = []
    for c in range(NCORE):
        d = dict(shared)
        d["x0T"] = np.ascontiguousarray(x0[c].T)
        im.append(d)
    res = run_bass_kernel_spmd(nc, im, core_ids=list(range(NCORE)))
    out = np.stack([np.ascontiguousarray(res.results[c]["x4T"].T) for c in range(NCORE)])
    return out.astype(np.float32, copy=False)


# version 30
# speedup vs baseline: 1.1746x; 1.0059x over previous
import numpy as np
from contextlib import ExitStack
import concourse.bass as bass
import concourse.mybir as mybir
from concourse.bass_utils import run_bass_kernel_spmd
from concourse.alu_op_type import AluOpType as OP

F32 = mybir.dt.float32
BF16 = mybir.dt.bfloat16
AF = mybir.ActivationFunctionType
AX = mybir.AxisListType.X

D = 1024
T = 4096
EPS = 1e-5


class B:
    ENGINES = ("sync", "tensor", "vector", "scalar", "gpsimd")

    def __init__(self):
        self.nc = bass.Bass("TRN2", target_bir_lowering=False)
        self.qs = {e: [] for e in self.ENGINES}
        self.ctx = ExitStack()
        self.phase_ctx = None
        self.semh = {}
        self.cnt = {}

    def push_scope(self):
        if not hasattr(self, "_scopes"):
            self._scopes = []
        self._scopes.append(ExitStack())

    def pop_scope(self):
        self._scopes.pop().close()

    def begin_phase(self):
        self.phase_ctx = ExitStack()

    def end_phase(self):
        if self.phase_ctx is not None:
            self.phase_ctx.close()
            self.phase_ctx = None

    def barrier(self):
        snap = [(s, v) for s, v in self.cnt.items() if v > 0]
        for e in self.ENGINES:
            self.on(e, None, wait=snap)

    def sem(self, name):
        if name not in self.semh:
            self.semh[name] = self.ctx.enter_context(self.nc.semaphore(name))
            self.cnt[name] = 0
        return self.semh[name]

    def _ctx(self):
        if getattr(self, "_scopes", None):
            return self._scopes[-1]
        return self.phase_ctx if self.phase_ctx is not None else self.ctx

    def sb(self, name, shape, dt):
        return self._ctx().enter_context(self.nc.sbuf_tensor(name, list(shape), dt))

    def ps(self, name, shape, dt=F32):
        return self._ctx().enter_context(self.nc.psum_tensor(name, list(shape), dt))

    def dram_in(self, name, shape, dt=F32):
        return self.nc.dram_tensor(name, list(shape), dt, kind="ExternalInput").ap()

    def dram_out(self, name, shape, dt=F32):
        return self.nc.dram_tensor(name, list(shape), dt, kind="ExternalOutput").ap()

    def dram_tmp(self, name, shape, dt=F32):
        return self.nc.dram_tensor(name, list(shape), dt, kind="Internal").ap()

    def presem(self):
        for s in ("mm", "ve", "sc", "gp", "st", "lc"):
            self.sem(s)

    def on(self, eng, fn, wait=None, inc=None, inc_n=1):
        waits = []
        if wait:
            for w in wait:
                if w is None:
                    continue
                if isinstance(w, str):
                    waits.append((self.sem(w), self.cnt[w]))
                else:
                    waits.append((self.sem(w[0]), w[1]))
        ret = None
        if inc is not None:
            semh = self.sem(inc)
            self.cnt[inc] += inc_n
            ret = self.cnt[inc]

            def f(e, fn=fn, waits=waits, semh=semh, n=inc_n):
                for s, v in waits:
                    e.wait_ge(s, v)
                fn(e).then_inc(semh, n)
        else:
            def f(e, fn=fn, waits=waits):
                for s, v in waits:
                    e.wait_ge(s, v)
                if fn is not None:
                    fn(e)
        self.qs[eng].append(f)
        return ret

    def wait(self, eng, sem, val=None):
        self.on(eng, None, wait=[sem if val is None else (sem, val)])

    def dma(self, eng, out, in_, inc="ld", wait=None):
        return self.on(eng, lambda e, o=out, i=in_: e.dma_start(out=o, in_=i),
                       wait=wait, inc=inc, inc_n=16)

    def finalize(self):
        with self.nc.Block() as block:
            for e in self.ENGINES:
                q = self.qs[e]
                if not q:
                    continue

                def runner(eng, q=q):
                    for f in q:
                        f(eng)
                getattr(block, e)(runner)
        self.ctx.close()
        return self.nc


class Rot:

    def __init__(self, slots):
        self.slots = list(slots)
        self.tok = [None] * len(self.slots)
        self.i = 0

    def get(self):
        i = self.i
        self.i = (self.i + 1) % len(self.slots)
        return i, self.slots[i], self.tok[i]


def pp_col(v):
    v = np.asarray(v)
    return np.ascontiguousarray(v.reshape(-1, 128).T.astype(np.float32, copy=False))


def host_consts_l0():
    onesblk = np.zeros((128, 2), np.float32)
    onesblk[:64, 0] = 1.0
    onesblk[64:, 1] = 1.0
    onesblk_t = np.ascontiguousarray(onesblk.T)
    sel = np.zeros((16, 128), np.float32)
    for e in range(16):
        sel[e, (e % 2) * 64:(e % 2) * 64 + 64] = 1.0
    decmask = np.zeros((16, 8), np.float32)
    for e in range(16):
        decmask[e, e // 2] = 1.0
    return {"c_onesblk": onesblk, "c_onesblk_t": onesblk_t,
            "c_sel": sel, "c_decmask": decmask}


def emit_post_ln(b, zbuf, TT, sqtile, rows, pstat1, pstat2, pbc1, pbc2,
                 onescol, onesrow, epscol, outf=None, outb=None, zoff=0):
    zc = lambda cc: zbuf[:, zoff + cc * TT:zoff + (cc + 1) * TT]
    zready = ("ve", b.cnt["ve"])
    scready = ("sc", b.cnt["sc"])
    zbf = b._ln_zbf
    m_ = rows[0:1, 0 * TT:1 * TT]
    msq = rows[0:1, 1 * TT:2 * TT]
    var = rows[0:1, 1 * TT:2 * TT]
    tmp = rows[0:1, 2 * TT:3 * TT]
    rb2 = b._ln_rows_bf
    mm_prev = None
    for cc in range(8):
        sct = b.on("scalar",
                   lambda e, o=zbf[:, 0:TT], i=zc(cc): e.copy(out=o, in_=i),
                   wait=([zready, scready] if cc == 0 else [mm_prev]), inc="sc")
        b.on("tensor",
             lambda e, p=pstat1[0:1, 0:TT], l=onescol[:, 0:1], r=zbf[:, 0:TT],
                    st=(cc == 0), sp=(cc == 7): e.matmul(p, l, r, start=st, stop=sp),
             wait=[("sc", sct)], inc="mm")
        sct2 = b.on("scalar",
                    lambda e, o=sqtile[:, 0:TT], i=zc(cc):
                    e.activation(out=o, in_=i, func=AF.Square),
                    wait=[scready] if cc == 0 else None, inc="sc")
        tok = b.on("tensor",
                   lambda e, p=pstat2[0:1, 0:TT], l=onescol[:, 0:1], r=sqtile[:, 0:TT],
                          st=(cc == 0), sp=(cc == 7): e.matmul(p, l, r, start=st, stop=sp),
                   wait=[("sc", sct2)], inc="mm")
        mm_prev = ("mm", tok)
    mm_sq = b.cnt["mm"]
    b.on("vector", lambda e, o=m_, i=pstat1[0:1, 0:TT]:
         e.tensor_scalar(out=o, in0=i, scalar1=1.0 / 1024, scalar2=None, op0=OP.mult),
         wait=[("mm", mm_sq)])
    b.on("vector", lambda e, o=msq, i=pstat2[0:1, 0:TT]:
         e.tensor_scalar(out=o, in0=i, scalar1=1.0 / 1024, scalar2=None, op0=OP.mult),
         inc="ve")
    ve_ms = b.cnt["ve"]
    b.on("vector", lambda e, o=tmp, a=m_, c=m_: e.tensor_tensor(out=o, in0=a, in1=c, op=OP.mult),
         wait=[("ve", ve_ms)])
    b.on("vector", lambda e, o=var, a=msq, c=tmp:
         e.tensor_tensor(out=o, in0=a, in1=c, op=OP.subtract), inc="ve")
    ve_var = b.cnt["ve"]
    sct = b.on("scalar", lambda e, o=tmp, i=var, bi=epscol[0:1, 0:1]:
               e.activation(out=o, in_=i, func=AF.Ln, bias=bi, scale=1.0),
               wait=[("ve", ve_var)], inc="sc")
    sct = b.on("scalar", lambda e, o=rb2[0:1, 0:TT], i=tmp:
               e.activation(out=o, in_=i, func=AF.Exp, scale=-0.5),
               wait=[("sc", sct)], inc="sc")
    b.on("vector", lambda e, o=rb2[0:1, TT:2 * TT], a=m_, c=rb2[0:1, 0:TT]:
         e.tensor_tensor(out=o, in0=a, in1=c, op=OP.mult),
         wait=[("sc", sct)], inc="ve")
    ve_rows = b.cnt["ve"]
    b.on("tensor", lambda e, p=pbc1[:, 0:TT], l=onesrow[0:1, :], r=rb2[0:1, 0:TT]:
         e.matmul(p, l, r, start=True, stop=True), wait=[("ve", ve_rows)])
    b.on("tensor", lambda e, p=pbc2[:, 0:TT], l=onesrow[0:1, :], r=rb2[0:1, TT:2 * TT]:
         e.matmul(p, l, r, start=True, stop=True), inc="mm")
    mm_bc = b.cnt["mm"]
    for cc in range(8):
        b.on("vector",
             lambda e, o=sqtile[:, 0:TT], a=zc(cc), c=pbc1[:, 0:TT]:
             e.tensor_tensor(out=o, in0=a, in1=c, op=OP.mult),
             wait=[("mm", mm_bc), scready] if cc == 0 else None)
        if outf is not None:
            b.on("vector",
                 lambda e, o=outf(cc), a=sqtile[:, 0:TT], c=pbc2[:, 0:TT]:
                 e.tensor_tensor(out=o, in0=a, in1=c, op=OP.subtract), inc="ve")
        if outb is not None:
            b.on("vector",
                 lambda e, o=outb(cc), a=sqtile[:, 0:TT], c=pbc2[:, 0:TT]:
                 e.tensor_tensor(out=o, in0=a, in1=c, op=OP.subtract), inc="ve")
    return b.cnt["ve"]


def emit_l0_attn(b, x0T, x1T, wqkvT, wgateT, woutT, bgate_pp, decay_p, consts,
                 wqkvb=None, wgateb=None, woutb=None):
    b.presem()
    use_scr = wqkvb is not None
    if use_scr:
        for srcw, dstw, n in ((wqkvT, wqkvb, 3072), (wgateT, wgateb, 1024),
                              (woutT, woutb, 1024)):
            for r0 in range(0, 1024, 256):
                b.dma("gpsimd", dstw[r0:r0 + 256, 0:n], srcw[r0:r0 + 256, 0:n],
                      inc="cast0")
    cast0_tot = b.cnt.get("cast0", 0)
    TH = 1024
    TT = 512
    NTT = TH // TT
    NH = T // TH

    x0b = b.sb("A_x0b", [128, 8 * TH], BF16)
    vfull = b.sb("A_vfull", [128, 8 * TH], BF16)
    apre = b.sb("A_apre", [128, 8 * TH], BF16)
    qb_ = [b.sb(f"A_qb{i}", [128, TH], F32) for i in range(2)]
    kb_ = [b.sb(f"A_kb{i}", [128, TH], F32) for i in range(2)]
    kvb_ = [b.sb(f"A_kvb{i}", [128, TH], F32) for i in range(2)]
    sck_ = [b.sb(f"A_sck{i}", [128, TH], F32) for i in range(2)]
    sckv_ = [b.sb(f"A_sckv{i}", [128, TH], F32) for i in range(2)]
    mixs = kvb_[0]
    stg2 = sck_[0]
    tmps = b.sb("A_tmps", [128, TT], F32)
    tmps2 = b.sb("A_tmps2", [128, TT], F32)
    wstage = [b.sb(f"A_wstage{i}", [128, 1024], F32) for i in range(2)]
    wbf = [b.sb(f"A_wbf{i}", [128, 1024], BF16) for i in range(2)]
    carry = b.sb("A_carry", [128, 32], F32)
    dec16 = b.sb("A_dec16", [16, 1], F32)
    decsig = b.sb("A_decsig", [16, 1], F32)
    decrhs = b.sb("A_decrhs", [16, 8], F32)
    deccols = b.sb("A_deccols", [128, 8], F32)
    onesblk_f = b.sb("A_onesblkf", [128, 2], F32)
    onesblk_tf = b.sb("A_onesblk_tf", [2, 128], F32)
    onesblk = b.sb("A_onesblk", [128, 2], BF16)
    onesblk_t = b.sb("A_onesblk_t", [2, 128], BF16)
    prodb = b.sb("A_prodb", [128, 1024], BF16)
    den2b = b.sb("A_den2b", [2, TT], BF16)
    selm = b.sb("A_selm", [16, 128], F32)
    decmask = b.sb("A_decmask", [16, 8], F32)
    bgate_sb = b.sb("A_bgate", [128, 8], F32)
    onescol = b.sb("A_onescol", [128, 1], BF16)
    onesrow = b.sb("A_onesrow", [1, 128], BF16)
    epscol = b.sb("A_epscol", [128, 1], F32)
    den2 = b.sb("A_den2", [2, TT], F32)
    x0f = b.sb("A_x0f", [128, 16 * TT], F32)
    zbufq = b.sb("A_zbufq", [128, 16 * TT], F32)
    zbuf = zbufq
    x1buf = b.sb("A_x1buf", [128, 8 * TT], F32)
    sqtile = b.sb("A_sqtile", [128, TT], BF16)
    b._ln_zbf = b.sb("A_lnzbf", [128, TT], BF16)
    b._ln_rows_bf = b.sb("A_lnrbf", [1, 2 * TT], BF16)
    rows = b.sb("A_rows", [1, 4 * TT], F32)

    pmm = [b.ps(f"A_pmm{i}", [128, TT]) for i in range(2)]
    pden = b.ps("A_pden", [2, TT])
    pdbc = b.ps("A_pdbc", [128, TT])
    pstat1 = b.ps("A_pstat1", [1, TT])
    pstat2 = b.ps("A_pstat2", [1, TT])
    pbc1 = b.ps("A_pbc1", [128, TT])
    pbc2 = b.ps("A_pbc2", [128, TT])

    pmm_rot = Rot(pmm)
    wrot = Rot(list(range(2)))

    b.on("vector", lambda e, o=onesblk[:, :], i=onesblk_f[:, :]: e.tensor_copy(out=o, in_=i),
         wait=[("lc", 32)])
    b.on("vector", lambda e, o=onesblk_t[:, :], i=onesblk_tf[:, :]: e.tensor_copy(out=o, in_=i),
         wait=[("lc", 32)], inc="ve")
    for dst, key in [(onesblk_f, "c_onesblk"), (onesblk_tf, "c_onesblk_t"),
                     (selm, "c_sel"), (decmask, "c_decmask")]:
        b.dma("sync", dst[:, :], consts[key][:, :], inc="lc")
    b.dma("sync", bgate_sb[:, :], bgate_pp[:, :], inc="lc")
    ld_dec = b.dma("sync", dec16[:, 0:1], decay_p[:, :], inc="lc")
    b.on("vector", lambda e, o=onescol[:, :]: e.memset(o, 1.0))
    b.on("vector", lambda e, o=epscol[:, :]: e.memset(o, EPS))
    b.on("vector", lambda e, o=onesrow[:, :]: e.memset(o, 1.0), inc="ve")
    sct = b.on("scalar", lambda e, o=decsig[:, 0:1], i=dec16[:, 0:1]:
               e.activation(out=o, in_=i, func=AF.Sigmoid),
               wait=[("lc", ld_dec)], inc="sc")
    vet = b.on("vector", lambda e, o=decrhs[:, :], a=decmask[:, :], s=decsig[:, 0:1]:
               e.tensor_scalar(out=o, in0=a, scalar1=s, scalar2=None, op0=OP.mult),
               wait=[("sc", sct)], inc="ve")
    mmt = b.on("tensor", lambda e, p=pdbc[:, 0:8], l=selm[:, :], r=decrhs[:, :]:
               e.matmul(p, l, r, start=True, stop=True),
               wait=[("ve", vet)], inc="mm")
    b.on("vector", lambda e, o=deccols[:, :], i=pdbc[:, 0:8]: e.tensor_copy(out=o, in_=i),
         wait=[("mm", mmt)], inc="ve")

    def load_w_tiles(srcT, fbase, scr=None):
        i, _, tok = wrot.get()
        sem = f"lw{i}"
        if use_scr and scr is not None:
            srcv = scr.rearrange("(cc p) f -> p cc f", p=128)
            w0 = [tok] if tok else []
            if not b._cast0_waited:
                w0.append(("cast0", cast0_tot))
                b._cast0_waited = True
            lds = b.dma("sync", wbf[i][:, :].rearrange("p (cc j) -> p cc j", j=128),
                        srcv[:, :, fbase:fbase + 128],
                        wait=w0 if w0 else None, inc=sem)
            tok2 = (sem, lds)
            b._wconv[i] = None
            b._wdirect[i] = tok2
            return i, tok2
        srcv = srcT.rearrange("(cc p) f -> p cc f", p=128)
        lds = b.dma("sync", wstage[i][:, :].rearrange("p (cc j) -> p cc j", j=128),
                    srcv[:, :, fbase:fbase + 128],
                    wait=[("ve", b._wconv[i])] if b._wconv[i] else None, inc=sem)
        vtok = b.on("vector",
                    lambda e, o=wbf[i][:, :], s=wstage[i][:, :]: e.tensor_copy(out=o, in_=s),
                    wait=[(sem, lds)] + ([tok] if tok else []), inc="ve")
        b._wconv[i] = vtok
        return i, ("ve", vtok)

    b._wconv = [None, None]
    b._wdirect = [None, None]
    b._cast0_waited = False

    for h in range(NH):
        tbase = h * TH
        stgs = [zbuf, stg2]
        stg_tok = [("ve", b.cnt["ve"]), ("ve", b.cnt["ve"])]
        for cc in range(8):
            s = stgs[cc % 2]
            sem = f"lx{cc % 2}"
            ldc = b.dma("sync", s[:, 0:TH],
                        x0T[cc * 128:(cc + 1) * 128, tbase:tbase + TH],
                        wait=[stg_tok[cc % 2]], inc=sem)
            t = b.on("vector", lambda e, o=x0b[:, cc * TH:(cc + 1) * TH], src=s[:, 0:TH]:
                     e.tensor_copy(out=o, in_=src),
                     wait=[(sem, ldc)], inc="ve")
            stg_tok[cc % 2] = ("ve", t)
        x0b_ready = b.cnt["ve"]

        def part1(j):
            par = j % 2
            qb, kb, kvb = qb_[par], kb_[par], kvb_[par]
            bufs = {"q": qb, "k": kb, "v": kvb}
            for s_name, sbase in (("q", 0), ("k", 1024), ("v", 2048)):
                wi, wtok = load_w_tiles(wqkvT, sbase + j * 128, wqkvb)
                for tt in range(NTT):
                    pi, pt, ptok = pmm_rot.get()
                    for cc in range(8):
                        w = None
                        if cc == 0:
                            w = [wtok, ("ve", x0b_ready)] + ([ptok] if ptok else [])
                        b.on("tensor",
                             lambda e, p=pt[:, 0:TT], l=wbf[wi][:, cc * 128:(cc + 1) * 128],
                                    r=x0b[:, cc * TH + tt * TT: cc * TH + (tt + 1) * TT],
                                    st=(cc == 0), sp=(cc == 7):
                             e.matmul(p, l, r, start=st, stop=sp),
                             wait=w, inc="mm" if cc == 7 else None)
                    mm_done = b.cnt["mm"]
                    wrot.tok[wi] = ("mm", mm_done)
                    dst = bufs[s_name][:, tt * TT:(tt + 1) * TT]
                    if s_name == "v":
                        b.on("scalar",
                             lambda e, o=vfull[:, j * TH + tt * TT: j * TH + (tt + 1) * TT],
                                    i=pt[:, 0:TT]: e.copy(out=o, in_=i),
                             wait=[("mm", mm_done)], inc="sc")
                        vt = b.on("vector",
                                  lambda e, o=kvb[:, tt * TT:(tt + 1) * TT], i=pt[:, 0:TT],
                                         k_=kb[:, tt * TT:(tt + 1) * TT]:
                                  e.tensor_tensor(out=o, in0=i, in1=k_, op=OP.mult),
                                  wait=[("mm", mm_done), ("sc", b.cnt["sc"])], inc="ve")
                        pmm_rot.tok[pi] = ("ve", vt)
                    else:
                        v1 = b.on("vector",
                                  lambda e, o=tmps[:, 0:TT], i=pt[:, 0:TT]:
                                  e.tensor_scalar(out=o, in0=i, scalar1=0.0, scalar2=None,
                                                  op0=OP.min),
                                  wait=[("mm", mm_done)], inc="ve")
                        s1 = b.on("scalar",
                                  lambda e, o=tmps2[:, 0:TT], i=tmps[:, 0:TT]:
                                  e.activation(out=o, in_=i, func=AF.Exp),
                                  wait=[("ve", v1)], inc="sc")
                        v2 = b.on("vector",
                                  lambda e, o=dst, i=pt[:, 0:TT], t2=tmps2[:, 0:TT]:
                                  e.scalar_tensor_tensor(out=o, in0=i, scalar=0.0, in1=t2,
                                                         op0=OP.max, op1=OP.add),
                                  wait=[("sc", s1)], inc="ve")
                        pmm_rot.tok[pi] = ("ve", v2)

        def part2(j):
            par = j % 2
            qb, kb, kvb, sck, sckv = qb_[par], kb_[par], kvb_[par], sck_[par], sckv_[par]
            d_b = deccols[:, j:j + 1].broadcast_to((128, TH))
            ini_kv = 0.0 if h == 0 else carry[:, j:j + 1]
            ini_k = 0.0 if h == 0 else carry[:, 16 + j:17 + j]
            b.on("vector", lambda e, o=sckv[:, :], d0=d_b, d1=kvb[:, :], ini=ini_kv:
                 e.tensor_tensor_scan(out=o, data0=d0, data1=d1, initial=ini,
                                      op0=OP.mult, op1=OP.add), inc="ve")
            b.on("vector", lambda e, o=sck[:, :], d0=d_b, d1=kb[:, :], ini=ini_k:
                 e.tensor_tensor_scan(out=o, data0=d0, data1=d1, initial=ini,
                                      op0=OP.mult, op1=OP.add), inc="ve")
            if h < NH - 1:
                b.on("vector", lambda e, o=carry[:, j:j + 1], i=sckv[:, TH - 1:TH]:
                     e.tensor_copy(out=o, in_=i))
                b.on("vector", lambda e, o=carry[:, 16 + j:17 + j], i=sck[:, TH - 1:TH]:
                     e.tensor_copy(out=o, in_=i), inc="ve")
            b.on("vector", lambda e, o=prodb[:, :], a=qb[:, :], c=sck[:, :]:
                 e.tensor_tensor(out=o, in0=a, in1=c, op=OP.mult))
            b.on("vector", lambda e, o=qb[:, :], a=qb[:, :], c=sckv[:, :]:
                 e.tensor_tensor(out=o, in0=a, in1=c, op=OP.mult), inc="ve")
            ve_prod = b.cnt["ve"]
            for tt in range(NTT):
                mm1 = b.on("tensor", lambda e, p=pden[:, 0:TT], l=onesblk[:, :],
                                  r=prodb[:, tt * TT:(tt + 1) * TT]:
                           e.matmul(p, l, r, start=True, stop=True),
                           wait=[("ve", ve_prod)], inc="mm")
                vd = b.on("vector", lambda e, o=den2[:, 0:TT], i=pden[:, 0:TT]:
                          e.tensor_scalar(out=o, in0=i, scalar1=1e-6, scalar2=None, op0=OP.max),
                          wait=[("mm", mm1)], inc="ve")
                s1_ = b.on("scalar", lambda e, o=den2[:, 0:TT], i=den2[:, 0:TT]:
                           e.activation(out=o, in_=i, func=AF.Ln),
                           wait=[("ve", vd)], inc="sc")
                vneg = b.on("vector", lambda e, o=den2[:, 0:TT], i=den2[:, 0:TT]:
                            e.tensor_scalar(out=o, in0=i, scalar1=-1.0, scalar2=None, op0=OP.mult),
                            wait=[("sc", s1_)], inc="ve")
                vrec_s = b.on("scalar", lambda e, o=den2b[:, 0:TT], i=den2[:, 0:TT]:
                              e.activation(out=o, in_=i, func=AF.Exp),
                              wait=[("ve", vneg)], inc="sc")
                mm2 = b.on("tensor", lambda e, p=pdbc[:, 0:TT], l=onesblk_t[:, :],
                                  r=den2b[:, 0:TT]:
                           e.matmul(p, l, r, start=True, stop=True),
                           wait=[("sc", vrec_s)], inc="mm")
                b.on("vector", lambda e, o=apre[:, j * TH + tt * TT: j * TH + (tt + 1) * TT],
                            a=qb[:, tt * TT:(tt + 1) * TT], c=pdbc[:, 0:TT]:
                     e.tensor_tensor(out=o, in0=a, in1=c, op=OP.mult),
                     wait=[("mm", mm2)], inc="ve")

        for j in range(8):
            part1(j)
            if j > 0:
                part2(j - 1)
        part2(7)

        apre_ready = b.cnt["ve"]
        if getattr(b, "dbg", None) is not None and h == 0:
            for cc in range(8):
                b.dma("sync", b.dbg["apre"][cc * 128:(cc + 1) * 128, :],
                      apre[:, cc * TH:(cc + 1) * TH],
                      wait=[("ve", apre_ready)] if cc == 0 else None, inc="st")
            for cc in range(8):
                b.dma("sync", b.dbg["v"][cc * 128:(cc + 1) * 128, :],
                      vfull[:, cc * TH:(cc + 1) * TH], inc="st")
        mm_x0_free = b.cnt["mm"]
        for fc in range(8):
            wi, wtok = load_w_tiles(wgateT, fc * 128, wgateb)
            for tt in range(NTT):
                pi, pt, ptok = pmm_rot.get()
                for cc in range(8):
                    w = None
                    if cc == 0:
                        w = [wtok, ("ve", apre_ready)] + ([ptok] if ptok else [])
                    b.on("tensor",
                         lambda e, p=pt[:, 0:TT], l=wbf[wi][:, cc * 128:(cc + 1) * 128],
                                r=apre[:, cc * TH + tt * TT: cc * TH + (tt + 1) * TT],
                                st=(cc == 0), sp=(cc == 7):
                         e.matmul(p, l, r, start=st, stop=sp),
                         wait=w, inc="mm" if cc == 7 else None)
                mm_done = b.cnt["mm"]
                wrot.tok[wi] = ("mm", mm_done)
                sct = b.on("scalar",
                           lambda e, o=x0b[:, fc * TH + tt * TT: fc * TH + (tt + 1) * TT],
                                  i=pt[:, 0:TT], bias=bgate_sb[:, fc:fc + 1]:
                           e.activation(out=o, in_=i, func=AF.Sigmoid, bias=bias, scale=1.0),
                           wait=[("mm", mm_done)], inc="sc")
                pmm_rot.tok[pi] = ("sc", sct)
        g_ready = b.cnt["sc"]
        b.wait("vector", "sc", g_ready)
        for fc in range(8):
            b.on("vector", lambda e, o=mixs[:, :], a=apre[:, fc * TH:(fc + 1) * TH],
                        c=vfull[:, fc * TH:(fc + 1) * TH]:
                 e.tensor_tensor(out=o, in0=a, in1=c, op=OP.subtract))
            b.on("vector", lambda e, o=mixs[:, :], a=x0b[:, fc * TH:(fc + 1) * TH],
                        c=mixs[:, :]:
                 e.tensor_tensor(out=o, in0=a, in1=c, op=OP.mult))
            b.on("vector", lambda e, o=apre[:, fc * TH:(fc + 1) * TH], a=mixs[:, :],
                        c=vfull[:, fc * TH:(fc + 1) * TH]:
                 e.tensor_tensor(out=o, in0=a, in1=c, op=OP.add), inc="ve")
        gated_ready = b.cnt["ve"]
        if getattr(b, "dbg", None) is not None and h == 0:
            for cc in range(8):
                b.dma("sync", b.dbg["gated"][cc * 128:(cc + 1) * 128, :],
                      apre[:, cc * TH:(cc + 1) * TH],
                      wait=[("ve", gated_ready)] if cc == 0 else None, inc="st")

        ldr = None
        for cc in range(8):
            for tt in range(NTT):
                ldr = b.dma("sync", x0f[:, (cc * NTT + tt) * TT:(cc * NTT + tt + 1) * TT],
                            x0T[cc * 128:(cc + 1) * 128, tbase + tt * TT: tbase + (tt + 1) * TT],
                            wait=[("ve", b.cnt["ve"])] if (cc == 0 and tt == 0) else None,
                            inc="lxf")
        for dc in range(8):
            wi, wtok = load_w_tiles(woutT, dc * 128, woutb)
            for tt in range(NTT):
                pi, pt, ptok = pmm_rot.get()
                for cc in range(8):
                    w = None
                    if cc == 0:
                        w = [wtok, ("ve", gated_ready)] + ([ptok] if ptok else [])
                    b.on("tensor",
                         lambda e, p=pt[:, 0:TT], l=wbf[wi][:, cc * 128:(cc + 1) * 128],
                                r=apre[:, cc * TH + tt * TT: cc * TH + (tt + 1) * TT],
                                st=(cc == 0), sp=(cc == 7):
                         e.matmul(p, l, r, start=st, stop=sp),
                         wait=w, inc="mm" if cc == 7 else None)
                mm_done = b.cnt["mm"]
                wrot.tok[wi] = ("mm", mm_done)
                vt = b.on("vector",
                          lambda e, o=zbufq[:, (tt * 8 + dc) * TT:(tt * 8 + dc + 1) * TT],
                                 a=pt[:, 0:TT],
                                 c=x0f[:, (dc * NTT + tt) * TT:(dc * NTT + tt + 1) * TT]:
                          e.tensor_tensor(out=o, in0=a, in1=c, op=OP.add),
                          wait=[("mm", mm_done), ("lxf", ldr), ("sc", b.cnt["sc"])], inc="ve")
                pmm_rot.tok[pi] = ("ve", vt)
        for tt in range(NTT):
            ve_out = emit_post_ln(b, zbufq, TT,
                                  sqtile, rows, pstat1, pstat2, pbc1, pbc2,
                                  onescol, onesrow, epscol,
                                  outf=lambda cc: x1buf[:, cc * TT:(cc + 1) * TT],
                                  zoff=tt * 8 * TT)
            stt = None
            for cc in range(8):
                stt = b.dma("sync", x1T[cc * 128:(cc + 1) * 128,
                                        tbase + tt * TT: tbase + (tt + 1) * TT],
                            x1buf[:, cc * TT:(cc + 1) * TT],
                            wait=[("ve", ve_out)] if cc == 0 else None, inc="st")
            b.wait("vector", "st", stt)
    b.wait("sync", "st", b.cnt["st"])


def l0_attn_host_inputs(x0_b, Wqkv, Wgate, Wout, bgate, decay_param):
    d = {
        "x0T": None if x0_b is None else np.ascontiguousarray(x0_b.T),
        "wqkvT": np.ascontiguousarray(Wqkv.T),
        "wgateT": np.ascontiguousarray(Wgate.T),
        "woutT": np.ascontiguousarray(Wout.T),
        "bgate_pp": pp_col(bgate),
        "decay_p": np.ascontiguousarray(decay_param.reshape(16, 1)),
    }
    d.update(host_consts_l0())
    return d


def build_l0_attn_launch(dbg=False):
    b = B()
    x0T = b.dram_in("x0T", [D, T])
    wqkvT = b.dram_in("wqkvT", [D, 3 * D])
    wgateT = b.dram_in("wgateT", [D, D])
    woutT = b.dram_in("woutT", [D, D])
    bgate_pp = b.dram_in("bgate_pp", [128, 8])
    decay_p = b.dram_in("decay_p", [16, 1])
    consts = {k: b.dram_in(k, v.shape) for k, v in host_consts_l0().items()}
    x1T = b.dram_out("x1T", [D, T])
    if dbg:
        b.dbg = {k: b.dram_out("dbg_" + k, [D, 1024], BF16) for k in ("apre", "v", "gated")}
    else:
        b.dbg = None
    emit_l0_attn(b, x0T, x1T, wqkvT, wgateT, woutT, bgate_pp, decay_p, consts)
    return b.finalize()


def host_consts_ffn():
    import ml_dtypes
    esel = np.zeros((8, 8 * 128), ml_dtypes.bfloat16)
    for e in range(8):
        esel[e, e * 128:(e + 1) * 128] = 1.0
    return {"c_esel": esel}


def emit_l0_ffn(b, x1T, x2T, wg, w1, b1pp, w2, b2, w1b, w2b, consts):
    b.presem()
    TT = 512
    NTT = 8

    x1f_ = [b.sb(f"F_x1f{i}", [128, 8 * TT], F32) for i in range(2)]
    x1bt_ = [b.sb(f"F_x1bt{i}", [128, 8 * TT], BF16) for i in range(2)]
    y_acc = b.sb("F_yacc", [128, 8 * TT], F32)
    zbuf = b.sb("F_zbuf", [128, 8 * TT], F32)
    x2buf = b.sb("F_x2buf", [128, 8 * TT], F32)
    hbuf = b.sb("F_hbuf", [128, 2 * 16 * TT], BF16)
    hrel = b.sb("F_hrel", [128, TT], BF16)
    wsb1 = [b.sb(f"F_wsb1_{i}", [128, 1024], BF16) for i in range(3)]
    wsb2 = b.sb("F_wsb2", [128, 16 * 1024], BF16)
    wgsb = b.sb("F_wgsb", [128, 64], F32)
    wgb = b.sb("F_wgb", [128, 64], BF16)
    b1sb = b.sb("F_b1sb", [128, 128], F32)
    b2sb = b.sb("F_b2sb", [8, 1024], F32)
    b2b = b.sb("F_b2b", [8, 1024], BF16)
    eselm = b.sb("F_eselm", [8, 8 * 128], BF16)
    wT8b = b.sb("F_wT8b", [8, T], BF16)
    eselmb = eselm
    wbc = b.sb("F_wbc", [128, 8 * TT], BF16)
    probs = b.sb("F_probs", [128, 8], F32)
    pmsk = b.sb("F_pmsk", [128, 8], F32)
    wtok = b.sb("F_wtok", [128, 8], F32)
    mxc = b.sb("F_mxc", [128, 1], F32)
    sumc = b.sb("F_sumc", [128, 1], F32)
    recc = b.sb("F_recc", [128, 1], F32)
    mx1c = b.sb("F_mx1c", [128, 1], F32)
    mx2c = b.sb("F_mx2c", [128, 1], F32)
    ones128f = b.sb("F_ones128f", [128, 128], F32)
    identf = b.sb("F_identf", [128, 128], F32)
    onescol = b.sb("F_onescol", [128, 1], BF16)
    onesrow = b.sb("F_onesrow", [1, 128], BF16)
    epscol = b.sb("F_epscol", [128, 1], F32)
    sqtile = b.sb("F_sqtile", [128, TT], BF16)
    b._ln_zbf = b.sb("F_lnzbf", [128, TT], BF16)
    b._ln_rows_bf = b.sb("F_lnrbf", [1, 2 * TT], BF16)
    rows = b.sb("F_rows", [1, 4 * TT], F32)

    ph = [b.ps(f"F_ph{i}", [128, TT]) for i in range(2)]
    py = [b.ps(f"F_py{i}", [128, TT]) for i in range(2)]
    pstat1 = b.ps("F_pstat1", [1, TT])
    pstat2 = b.ps("F_pstat2", [1, TT])
    pbc1 = b.ps("F_pbc1", [128, TT])
    pbc2 = b.ps("F_pbc2", [128, TT])
    ptw = pbc2

    ph_rot = Rot(ph)
    py_rot = Rot(py)
    w1_rot = Rot(list(range(3)))

    for cc in range(8):
        b.dma("sync", wgsb[:, cc * 8:(cc + 1) * 8], wg[cc * 128:(cc + 1) * 128, :], inc="lc")
    b.dma("sync", b1sb[:, :], b1pp[:, :], inc="lc")
    b.dma("sync", b2sb[:, :], b2[:, :], inc="lc")
    b.dma("sync", eselm[:, :], consts["c_esel"][:, :], inc="lc")
    lc_tot = b.cnt["lc"]
    b.on("vector", lambda e, o=onescol[:, :]: e.memset(o, 1.0))
    b.on("vector", lambda e, o=onesrow[:, :]: e.memset(o, 1.0))
    b.on("vector", lambda e, o=epscol[:, :]: e.memset(o, EPS))
    b.on("vector", lambda e, o=ones128f[:, :]: e.memset(o, 1.0), inc="ve")
    b.on("vector", lambda e, o=wgb[:, :], i=wgsb[:, :]: e.tensor_copy(out=o, in_=i),
         wait=[("lc", lc_tot)])
    b.on("vector", lambda e, o=b2b[:, :], i=b2sb[:, :]: e.tensor_copy(out=o, in_=i), inc="ve")
    ve_consts = b.cnt["ve"]
    gp_id = b.on("gpsimd", lambda e, o=identf[:, :], i=ones128f[:, :]:
                 e.affine_select(out=o, in_=i, pattern=[[1, 128]], compare_op=OP.is_equal,
                                 fill=0.0, base=0, channel_multiplier=-1),
                 wait=[("ve", ve_consts)], inc="gp")

    for e_i in range(8):
        for fcl in range(16):
            src = w1[e_i, :, :].rearrange("(cc p) (fcl j) -> fcl p cc j", p=128, j=128)
            b.dma("gpsimd", w1b[e_i, fcl, :, :], src[fcl, :, :, :], inc="cast")
        for fcl in range(16):
            b.dma("gpsimd", w2b[e_i, fcl, :, :], w2[e_i, fcl * 128:(fcl + 1) * 128, :],
                  inc="cast")
    cast_tot = b.cnt["cast"]

    x1_cons = [None, None]
    for tt in range(NTT):
        ts = tt * TT
        par = tt % 2
        x1f = x1f_[par]
        x1bt = x1bt_[par]
        hb_off = par * 16 * TT
        lx = None
        for cc in range(8):
            w0 = None
            if cc == 0:
                w0 = [x1_cons[par]] if x1_cons[par] else []
            lx = b.dma("sync", x1f[:, cc * TT:(cc + 1) * TT],
                       x1T[cc * 128:(cc + 1) * 128, ts:ts + TT],
                       wait=w0, inc=f"lx{par}")
        cv = b.on("vector", lambda e, o=x1bt[:, :], i=x1f[:, :]: e.tensor_copy(out=o, in_=i),
                  wait=[(f"lx{par}", lx)], inc="ve")
        x1_ready = cv

        for tc in range(4):
            pli, plt, pltok = ph_rot.get()
            pl = plt[:, 0:8]
            mmw = [("ve", x1_ready), (f"lx{par}", lx), ("lc", lc_tot)]
            if pltok:
                mmw.append(pltok)
            for cc in range(8):
                b.on("tensor",
                     lambda e, p=pl, l=x1f[:, cc * TT + tc * 128: cc * TT + (tc + 1) * 128],
                            r=wgsb[:, cc * 8:(cc + 1) * 8],
                            st=(cc == 0), sp=(cc == 7): e.matmul(p, l, r, start=st, stop=sp),
                     wait=mmw if cc == 0 else None, inc="mm" if cc == 7 else None)
            mml = b.cnt["mm"]
            t = b.on("vector", lambda e, o=mxc[:, 0:1], i=pl:
                     e.reduce_max(out=o, in_=i, axis=AX, negate=True),
                     wait=[("mm", mml)], inc="ve")
            sct = b.on("scalar", lambda e, o=probs[:, :], i=pl, bi=mxc[:, 0:1]:
                       e.activation(out=o, in_=i, func=AF.Exp, bias=bi, scale=1.0),
                       wait=[("ve", t)], inc="sc")
            ph_rot.tok[pli] = ("sc", sct)
            t = b.on("vector", lambda e, o=sumc[:, 0:1], i=probs[:, :]:
                     e.reduce_sum(out=o, in_=i, axis=AX), wait=[("sc", sct)], inc="ve")
            t = b.on("vector", lambda e, o=recc[:, 0:1], i=sumc[:, 0:1]:
                     e.reciprocal(out=o, in_=i), wait=[("ve", t)], inc="ve")
            t = b.on("vector", lambda e, o=probs[:, :], i=probs[:, :], s=recc[:, 0:1]:
                     e.tensor_scalar(out=o, in0=i, scalar1=s, scalar2=None, op0=OP.mult),
                     wait=[("ve", t)], inc="ve")
            t = b.on("vector", lambda e, o=mx1c[:, 0:1], i=probs[:, :]:
                     e.reduce_max(out=o, in_=i, axis=AX), wait=[("ve", t)], inc="ve")
            t = b.on("vector", lambda e, o=pmsk[:, :], i=probs[:, :], s=mx1c[:, 0:1]:
                     e.tensor_scalar(out=o, in0=i, scalar1=s, scalar2=None, op0=OP.is_ge),
                     wait=[("ve", t)], inc="ve")
            t = b.on("vector", lambda e, o=pmsk[:, :], i=pmsk[:, :], p2=probs[:, :]:
                     e.scalar_tensor_tensor(out=o, in0=i, scalar=-1e30, in1=p2,
                                            op0=OP.mult, op1=OP.add),
                     wait=[("ve", t)], inc="ve")
            t = b.on("vector", lambda e, o=mx2c[:, 0:1], i=pmsk[:, :]:
                     e.reduce_max(out=o, in_=i, axis=AX), wait=[("ve", t)], inc="ve")
            t = b.on("vector", lambda e, o=pmsk[:, :], i=probs[:, :], s=mx2c[:, 0:1]:
                     e.tensor_scalar(out=o, in0=i, scalar1=s, scalar2=None, op0=OP.is_ge),
                     wait=[("ve", t)], inc="ve")
            vw = b.on("vector", lambda e, o=wtok[:, :], a=probs[:, :], c=pmsk[:, :]:
                      e.tensor_tensor(out=o, in0=a, in1=c, op=OP.mult),
                      wait=[("ve", t)], inc="ve")
            mmtr = b.on("tensor", lambda e, p=ptw[0:8, 0:128], i=wtok[:, :], idn=identf[:, :]:
                        e.transpose(p, i, idn),
                        wait=[("ve", vw), ("gp", gp_id), ("ve", b.cnt["ve"])], inc="mm")
            b.on("vector", lambda e, o=wT8b[0:8, ts + tc * 128: ts + (tc + 1) * 128],
                        i=ptw[0:8, 0:128]: e.tensor_copy(out=o, in_=i),
                 wait=[("mm", mmtr)], inc="ve")
        w_ready = b.cnt["ve"]

        for e_i in range(8):
            mmb = b.on("tensor",
                       lambda e, p=pbc1[:, 0:TT], l=eselmb[0:8, e_i * 128:(e_i + 1) * 128],
                              r=wT8b[0:8, ts:ts + TT]: e.matmul(p, l, r, start=True, stop=True),
                       wait=[("ve", w_ready), ("ve", b.cnt["ve"])], inc="mm")
            b.on("vector", lambda e, o=wbc[:, e_i * TT:(e_i + 1) * TT], i=pbc1[:, 0:TT]:
                 e.tensor_copy(out=o, in_=i), wait=[("mm", mmb)], inc="ve")

        y_mm_prev = None
        if not hasattr(b, "_hb_guard"):
            b._hb_guard = {0: None, 1: None}
        for e_i in range(8):
            for fcl in range(16):
                wi, _, wtk = w1_rot.get()
                sem = f"lw{wi}"
                ldw = b.dma("sync", wsb1[wi][:, :], w1b[e_i, fcl, :, :],
                            wait=[wtk, ("cast", cast_tot)] if wtk or True else None, inc=sem)
                pi, pt, ptok = ph_rot.get()
                for cc in range(8):
                    w = None
                    if cc == 0:
                        w = [(sem, ldw), ("ve", x1_ready)] + ([ptok] if ptok else [])
                    b.on("tensor",
                         lambda e, p=pt[:, 0:TT], l=wsb1[wi][:, cc * 128:(cc + 1) * 128],
                                r=x1bt[:, cc * TT:(cc + 1) * TT],
                                st=(cc == 0), sp=(cc == 7): e.matmul(p, l, r, start=st, stop=sp),
                         wait=w, inc="mm" if cc == 7 else None)
                mmh = b.cnt["mm"]
                w1_rot.tok[wi] = ("mm", mmh)
                sct = b.on("scalar",
                           lambda e, o=hrel[:, 0:TT], i=pt[:, 0:TT],
                                  bi=b1sb[:, e_i * 16 + fcl: e_i * 16 + fcl + 1]:
                           e.activation(out=o, in_=i, func=AF.Relu, bias=bi, scale=1.0),
                           wait=[("mm", mmh), ("ve", b.cnt["ve"])], inc="sc")
                hbg = b._hb_guard[(e_i % 2)]
                vh = b.on("vector",
                          lambda e, o=hbuf[:, (e_i % 2) * 16 * TT + fcl * TT:
                                              (e_i % 2) * 16 * TT + (fcl + 1) * TT],
                                 a=hrel[:, 0:TT],
                                 c=wbc[:, e_i * TT:(e_i + 1) * TT]:
                          e.tensor_tensor(out=o, in0=a, in1=c, op=OP.mult),
                          wait=[("sc", sct)] + ([hbg] if hbg else []), inc="ve")
                ph_rot.tok[pi] = ("sc", sct)
            h_ready = b.cnt["ve"]
            lw2 = None
            for fcl in range(16):
                lw2 = b.dma("sync", wsb2[:, fcl * 1024:(fcl + 1) * 1024], w2b[e_i, fcl, :, :],
                            wait=([y_mm_prev] if (fcl == 0 and y_mm_prev) else None) if fcl == 0 else None,
                            inc="lw2")
            for dc in range(8):
                pi, pt, ptok = py_rot.get()
                first = True
                if e_i == 0:
                    b.on("tensor",
                         lambda e, p=pt[:, 0:TT], l=b2b[0:8, dc * 128:(dc + 1) * 128],
                                r=wT8b[0:8, ts:ts + TT]: e.matmul(p, l, r, start=True, stop=False),
                         wait=[("lw2", lw2), ("ve", h_ready), ("ve", w_ready)] + ([ptok] if ptok else []),
                         )
                    first = False
                for fcl in range(16):
                    w = None
                    if first and fcl == 0:
                        w = [("lw2", lw2), ("ve", h_ready)] + ([ptok] if ptok else [])
                    b.on("tensor",
                         lambda e, p=pt[:, 0:TT],
                                l=wsb2[:, fcl * 1024 + dc * 128: fcl * 1024 + (dc + 1) * 128],
                                r=hbuf[:, (e_i % 2) * 16 * TT + fcl * TT:
                                          (e_i % 2) * 16 * TT + (fcl + 1) * TT],
                                st=(first and fcl == 0), sp=(fcl == 15):
                         e.matmul(p, l, r, start=st, stop=sp),
                         wait=w, inc="mm" if fcl == 15 else None)
                mmy = b.cnt["mm"]
                if e_i == 0:
                    vy = b.on("vector", lambda e, o=y_acc[:, dc * TT:(dc + 1) * TT], i=pt[:, 0:TT]:
                              e.tensor_copy(out=o, in_=i), wait=[("mm", mmy)], inc="ve")
                else:
                    vy = b.on("vector", lambda e, o=y_acc[:, dc * TT:(dc + 1) * TT],
                                     a=y_acc[:, dc * TT:(dc + 1) * TT], i=pt[:, 0:TT]:
                              e.tensor_tensor(out=o, in0=a, in1=i, op=OP.add),
                              wait=[("mm", mmy)], inc="ve")
                py_rot.tok[pi] = ("ve", vy)
            y_mm_prev = ("mm", b.cnt["mm"])
            b._hb_guard[e_i % 2] = ("mm", b.cnt["mm"])

        if getattr(b, "dbgF", None) is not None and tt == 0:
            for dc in range(8):
                b.dma("sync", b.dbgF["y"][dc * 128:(dc + 1) * 128, :],
                      y_acc[:, dc * TT:(dc + 1) * TT],
                      wait=[("ve", b.cnt["ve"])] if dc == 0 else None, inc="st")
        for dc in range(8):
            b.on("vector", lambda e, o=zbuf[:, dc * TT:(dc + 1) * TT],
                        a=y_acc[:, dc * TT:(dc + 1) * TT], c=x1f[:, dc * TT:(dc + 1) * TT]:
                 e.tensor_tensor(out=o, in0=a, in1=c, op=OP.add),
                 wait=[("sc", b.cnt["sc"])] if dc == 0 else None, inc="ve" if dc == 7 else None)
        x1_cons[par] = ("ve", b.cnt["ve"])
        ve_out = emit_post_ln(b, zbuf, TT, sqtile, rows, pstat1, pstat2, pbc1, pbc2,
                              onescol, onesrow, epscol,
                              outf=lambda cc: x2buf[:, cc * TT:(cc + 1) * TT])
        stt = None
        for cc in range(8):
            stt = b.dma("sync", x2T[cc * 128:(cc + 1) * 128, ts:ts + TT],
                        x2buf[:, cc * TT:(cc + 1) * TT],
                        wait=[("ve", ve_out)] if cc == 0 else None, inc="st")
        b.wait("vector", "st", stt)
    if getattr(b, "dbgF", None) is not None:
        b.dma("sync", b.dbgF["w"][:, :], wT8b[:, :], inc="st")
    b.wait("sync", "st", b.cnt["st"])


def l0_ffn_host_inputs(Wg, W1, b1, W2, b2):
    b1pp = np.concatenate([pp_col(b1[e]) for e in range(8)], axis=1)
    d = {"wg": np.ascontiguousarray(Wg), "w1": np.ascontiguousarray(W1),
         "b1pp": np.ascontiguousarray(b1pp), "w2": np.ascontiguousarray(W2),
         "b2": np.ascontiguousarray(b2)}
    d.update(host_consts_ffn())
    return d


def build_l0_ffn_launch(dbg=False):
    b = B()
    x1T = b.dram_in("x1T", [D, T])
    wg = b.dram_in("wg", [D, 8])
    w1 = b.dram_in("w1", [8, D, 2048])
    b1pp = b.dram_in("b1pp", [128, 128])
    w2 = b.dram_in("w2", [8, 2048, D])
    b2 = b.dram_in("b2", [8, D])
    consts = {k: b.dram_in(k, v.shape, BF16) for k, v in host_consts_ffn().items()}
    w1b = b.dram_tmp("w1b", [8, 16, 128, 1024], BF16)
    wqkvb = b.dram_tmp("wqkvb", [D, 3 * D], BF16)
    wgateb = b.dram_tmp("wgateb", [D, D], BF16)
    woutb = b.dram_tmp("woutb", [D, D], BF16)
    w2b = b.dram_tmp("w2b", [8, 16, 128, 1024], BF16)
    x2T = b.dram_out("x2T", [D, T])
    if dbg:
        b.dbgF = {"w": b.dram_out("dbgF_w", [8, T], BF16),
                  "y": b.dram_out("dbgF_y", [D, 512], F32)}
    else:
        b.dbgF = None
    emit_l0_ffn(b, x1T, x2T, wg, w1, b1pp, w2, b2, w1b, w2b, consts)
    return b.finalize()


def emit_l1_attn(b, x2T, x3T, wqkv, bqkv_pp, wproj, bproj_pp, attn_dram):
    b.presem()
    TT = 512
    NTT = 8
    b.identb = b.sb("G_identb", [128, 128], BF16)
    _onesb = b.sb("G_onesb", [128, 128], BF16)
    b.on("vector", lambda e, o=_onesb[:, :]: e.memset(o, 1.0), inc="ve")
    b.on("gpsimd", lambda e, o=b.identb[:, :], i=_onesb[:, :]:
         e.affine_select(out=o, in_=i, pattern=[[1, 128]], compare_op=OP.is_equal,
                         fill=0.0, base=0, channel_multiplier=-1),
         wait=[("ve", b.cnt["ve"])], inc="gp")
    b._identb_gp = b.cnt["gp"]

    x2b = b.sb("G_x2b", [128, 8 * T], BF16)
    qTb = b.sb("G_qTb", [128, T], BF16)
    kTb = b.sb("G_kTb", [128, T], BF16)
    vtmp = b.sb("G_vtmp", [128, T], BF16)
    vtok = b.sb("G_vtok", [128, 2 * 32 * 65], BF16)
    ptb = [b.sb(f"G_ptb{i}", [128, TT], BF16) for i in range(4)]
    obuf = b.sb("G_obuf", [64, TT], F32)
    rec = b.sb("G_rec", [1, TT], F32)
    recb = b.sb("G_recb", [1, TT], BF16)
    atile = b.sb("G_atile", [128, TT], BF16)
    attsb = b.sb("G_attsb", [128, 8 * TT], BF16)
    wstage = [b.sb(f"G_wstage{i}", [128, 1024], F32) for i in range(2)]
    wbf = [b.sb(f"G_wbf{i}", [128, 1024], BF16) for i in range(2)]
    bqkv_sb = b.sb("G_bqkv", [128, 24], F32)
    bqs_sb = b.sb("G_bqs", [128, 8], F32)
    bproj_sb = b.sb("G_bproj", [128, 8], F32)
    stg = [b.sb(f"G_stg{i}", [128, 2048], F32) for i in range(2)]
    x2f = b.sb("G_x2f", [128, 8 * TT], F32)
    zbuf = b.sb("G_zbuf", [128, 8 * TT], F32)
    x3buf = b.sb("G_x3buf", [128, 8 * TT], F32)
    onescol = b.sb("G_onescol", [128, 1], BF16)
    onesrow = b.sb("G_onesrow", [1, 128], BF16)
    epscol = b.sb("G_epscol", [128, 1], F32)
    sqtile = b.sb("G_sqtile", [128, TT], BF16)
    b._ln_zbf = b.sb("G_lnzbf", [128, TT], BF16)
    b._ln_rows_bf = b.sb("G_lnrbf", [1, 2 * TT], BF16)
    rows = b.sb("G_rows", [1, 4 * TT], F32)

    pS = [b.ps(f"G_pS{i}", [128, TT]) for i in range(2)]
    b.push_scope()
    pS2 = [b.ps(f"G_pS2_{i}", [128, TT]) for i in range(2)]
    pO_ = [b.ps(f"G_pO{i}", [65, TT]) for i in range(2)]
    pB = b.ps("G_pB", [64, TT])
    pTR = b.ps("G_pTR", [128, 128], BF16)

    pS_rot = Rot(pS)
    wrot = Rot(list(range(2)))
    b._wconvG = [None, None]
    b._sa_tok = {}

    pSatt_rot = Rot(pS + pS2)
    b.dma("sync", bqkv_sb[:, :], bqkv_pp[:, :], inc="lc")
    b.dma("sync", bproj_sb[:, :], bproj_pp[:, :], inc="lc")
    lc_tot = b.cnt["lc"]
    b.on("vector", lambda e, o=onescol[:, :]: e.memset(o, 1.0))
    b.on("vector", lambda e, o=onesrow[:, :]: e.memset(o, 1.0))
    b.on("vector", lambda e, o=epscol[:, :]: e.memset(o, EPS))
    b.on("vector", lambda e, o=bqs_sb[:, :], i=bqkv_sb[:, 0:8]:
         e.tensor_scalar(out=o, in0=i, scalar1=0.125, scalar2=None, op0=OP.mult),
         wait=[("lc", lc_tot)], inc="ve")
    ve_consts = b.cnt["ve"]

    stg_tok = [("ve", b.cnt["ve"]), ("ve", b.cnt["ve"])]
    for cc in range(8):
        for half in range(2):
            i = (cc * 2 + half) % 2
            sem = f"lx{i}"
            ldc = b.dma("sync", stg[i][:, :],
                        x2T[cc * 128:(cc + 1) * 128, half * 2048:(half + 1) * 2048],
                        wait=[stg_tok[i]], inc=sem)
            t = b.on("vector",
                     lambda e, o=x2b[:, cc * T + half * 2048: cc * T + (half + 1) * 2048],
                            s=stg[i][:, :]: e.tensor_copy(out=o, in_=s),
                     wait=[(sem, ldc)], inc="ve")
            stg_tok[i] = ("ve", t)
    x2_ready = b.cnt["ve"]

    def load_w_tiles(srcT, fbase):
        i, _, tok = wrot.get()
        sem = f"lw{i}"
        srcv = srcT.rearrange("(cc p) f -> p cc f", p=128)
        lds = b.dma("sync", wstage[i][:, :].rearrange("p (cc j) -> p cc j", j=128),
                    srcv[:, :, fbase:fbase + 128],
                    wait=[("ve", b._wconvG[i])] if b._wconvG[i] else None, inc=sem)
        vtok_ = b.on("vector",
                     lambda e, o=wbf[i][:, :], s=wstage[i][:, :]: e.tensor_copy(out=o, in_=s),
                     wait=[(sem, lds)] + ([tok] if tok else []), inc="ve")
        b._wconvG[i] = vtok_
        return i, vtok_

    for hg in range(8):
        vtok_ms = b.on("vector", lambda e, o=vtok[:, :]: e.memset(o, 1.0),
                       wait=[("mm", b.cnt["mm"])], inc="ve")
        for s_i, s_name in ((0, "q"), (1, "k"), (2, "v")):
            wi, wtok = load_w_tiles(wqkv, s_i * 1024 + hg * 128)
            for tt in range(NTT):
                pi, pt, ptok = pSatt_rot.get()
                for cc in range(8):
                    w = None
                    if cc == 0:
                        w = [("ve", wtok), ("ve", x2_ready)] + ([ptok] if ptok else [])
                    b.on("tensor",
                         lambda e, p=pt[:, 0:TT], l=wbf[wi][:, cc * 128:(cc + 1) * 128],
                                r=x2b[:, cc * T + tt * TT: cc * T + (tt + 1) * TT],
                                st=(cc == 0), sp=(cc == 7): e.matmul(p, l, r, start=st, stop=sp),
                         wait=w, inc="mm" if cc == 7 else None)
                mm_done = b.cnt["mm"]
                wrot.tok[wi] = ("mm", mm_done)
                if s_name == "q":
                    vet = b.on("vector",
                               lambda e, o=qTb[:, tt * TT:(tt + 1) * TT], i=pt[:, 0:TT],
                                      bi=bqs_sb[:, hg:hg + 1].broadcast_to((128, TT)):
                               e.scalar_tensor_tensor(out=o, in0=i, scalar=0.125, in1=bi,
                                                      op0=OP.mult, op1=OP.add),
                               wait=[("mm", mm_done), ("ve", ve_consts)], inc="ve")
                else:
                    dst = kTb if s_name == "k" else vtmp
                    bias = bqkv_sb[:, 8 + hg:9 + hg] if s_name == "k" else bqkv_sb[:, 16 + hg:17 + hg]
                    vet = b.on("vector",
                               lambda e, o=dst[:, tt * TT:(tt + 1) * TT], i=pt[:, 0:TT], bi=bias:
                               e.tensor_scalar(out=o, in0=i, scalar1=bi, scalar2=None, op0=OP.add),
                               wait=[("mm", mm_done), ("ve", ve_consts)], inc="ve")
                pSatt_rot.tok[pi] = ("ve", vet)
        qkv_done = b.cnt["ve"]
        tr_prev = [None, None]
        for h2 in range(2):
            for kc in range(32):
                half = kc % 2
                mmt = b.on("tensor",
                           lambda e, p=pTR[:, half * 64:half * 64 + 64],
                                  i=vtmp[h2 * 64:(h2 + 1) * 64, kc * 128:(kc + 1) * 128],
                                  idn=b.identb[h2 * 64:h2 * 64 + 64, h2 * 64:h2 * 64 + 64]: e.transpose(p, i, idn),
                           wait=[("ve", qkv_done), ("ve", vtok_ms), ("gp", b._identb_gp)] + ([tr_prev[half]] if tr_prev[half] else []),
                           inc="mm")
                vtr = b.on("vector",
                           lambda e, o=vtok[:, (h2 * 32 + kc) * 65:(h2 * 32 + kc) * 65 + 64],
                                  i=pTR[:, half * 64:half * 64 + 64]: e.tensor_copy(out=o, in_=i),
                           wait=[("mm", mmt)], inc="ve")
                tr_prev[half] = ("ve", vtr)
        vtok_ready = b.cnt["ve"]

        if not isinstance(getattr(b, "_pO_tok", None), dict):
            b._pO_tok = {}
        for qb in range(NTT):
            def emit_S(h2, kc):
                hb = h2 * 64
                pi, pt, ptok = pSatt_rot.get()
                w = [("ve", vtok_ready)] if kc == 0 else []
                if ptok:
                    w.append(ptok)
                b.on("tensor",
                     lambda e, p=pt[:, 0:TT],
                            l=kTb[hb:hb + 64, kc * 128:(kc + 1) * 128],
                            r=qTb[hb:hb + 64, qb * TT:(qb + 1) * TT]:
                     e.matmul(p, l, r, start=True, stop=True),
                     wait=w if w else None, inc="mm")
                return pi, pt, b.cnt["mm"]

            def emit_exp(h2, kc, pi, pt, mmS):
                pj = (kc % 2) * 2 + h2
                sct = b.on("scalar",
                           lambda e, o=ptb[pj][:, 0:TT], i=pt[:, 0:TT]:
                           e.activation(out=o, in_=i, func=AF.Exp),
                           wait=[("mm", mmS)], inc="sc")
                pSatt_rot.tok[pi] = ("sc", sct)
                return sct

            def emit_O(h2, kc, sct):
                pj = (kc % 2) * 2 + h2
                w2 = [("sc", sct)]
                if kc == 0 and b._pO_tok.get(h2):
                    w2.append(b._pO_tok[h2])
                b.on("tensor",
                     lambda e, p=pO_[h2][:, 0:TT],
                            l=vtok[:, (h2 * 32 + kc) * 65:(h2 * 32 + kc + 1) * 65],
                            r=ptb[pj][:, 0:TT],
                            st=(kc == 0), sp=(kc == 31): e.matmul(p, l, r, start=st, stop=sp),
                     wait=w2, inc="mm" if kc == 31 else None)
                return b.cnt["mm"]

            scts = {}
            pis = {}
            for h2 in range(2):
                pis[(h2, 0)] = emit_S(h2, 0)
                scts[(h2, 0)] = emit_exp(h2, 0, *pis[(h2, 0)])
            for kc in range(1, 32):
                for h2 in range(2):
                    pis[(h2, kc)] = emit_S(h2, kc)
                for h2 in range(2):
                    emit_O(h2, kc - 1, scts[(h2, kc - 1)])
                for h2 in range(2):
                    scts[(h2, kc)] = emit_exp(h2, kc, *pis[(h2, kc)])
            mmOs = {}
            for h2 in range(2):
                mmOs[h2] = emit_O(h2, 31, scts[(h2, 31)])
            for h2 in range(2):
                hb = h2 * 64
                pO = pO_[h2]
                mmO = mmOs[1]
                def _vrecip(e, o=recb[0:1, 0:TT], i=pO[64:65, 0:TT], _nc=b.nc):
                    with _nc.allow_low_precision(reason="bf16 denom for broadcast matmul"):
                        return e.reciprocal(out=o, in_=i)
                vr = b.on("vector", _vrecip, wait=[("mm", mmO)], inc="ve")
                mmB = b.on("tensor", lambda e, p=pB[:, 0:TT], l=onesrow[0:1, 0:64],
                                  r=recb[0:1, 0:TT]: e.matmul(p, l, r, start=True, stop=True),
                           wait=[("ve", vr)], inc="mm")
                vo = b.on("vector", lambda e, o=obuf[:, 0:TT], i=pO[0:64, 0:TT]:
                          e.tensor_copy(out=o, in_=i),
                          wait=[("mm", mmO), ("ve", b.cnt["ve"])], inc="ve")
                sa = f"sa{h2}"
                satok = b._sa_tok.get(sa)
                va = b.on("vector",
                          lambda e, o=atile[hb:hb + 64, 0:TT], a=obuf[:, 0:TT], c=pB[:, 0:TT]:
                          e.tensor_tensor(out=o, in0=a, in1=c, op=OP.mult),
                          wait=[("mm", mmB), ("ve", vo)] + ([satok] if satok else []), inc="ve")
                b._pO_tok[h2] = ("ve", vo)
                sat = b.dma("sync", attn_dram[hg * 128 + hb: hg * 128 + hb + 64,
                                              qb * TT:(qb + 1) * TT],
                            atile[hb:hb + 64, 0:TT], wait=[("ve", va)], inc=sa)
                b._sa_tok[sa] = (sa, sat)


    b.pop_scope()
    b.barrier()
    pS_rot = Rot(pS)
    b.push_scope()
    pstat1 = b.ps("G_pstat1", [1, TT])
    pstat2 = b.ps("G_pstat2", [1, TT])
    pbc1 = b.ps("G_pbc1", [128, TT])
    pbc2 = b.ps("G_pbc2", [128, TT])
    for tt in range(NTT):
        lds = None
        for cc in range(8):
            w0 = None
            if cc == 0:
                w0 = [("ve", b.cnt["ve"]), ("sa0", b.cnt.get("sa0", 0)),
                      ("sa1", b.cnt.get("sa1", 0))]
            lds = b.dma("sync", attsb[:, cc * TT:(cc + 1) * TT],
                        attn_dram[cc * 128:(cc + 1) * 128, tt * TT:(tt + 1) * TT],
                        wait=w0, inc="la")
        ldx = None
        for cc in range(8):
            ldx = b.dma("sync", x2f[:, cc * TT:(cc + 1) * TT],
                        x2T[cc * 128:(cc + 1) * 128, tt * TT:(tt + 1) * TT],
                        wait=[("ve", b.cnt["ve"])] if cc == 0 else None, inc="lxf")
        for dc in range(8):
            wi, wtok = load_w_tiles(wproj, dc * 128)
            pi, pt, ptok = pS_rot.get()
            for cc in range(8):
                w = None
                if cc == 0:
                    w = [("ve", wtok), ("la", lds)] + ([ptok] if ptok else [])
                b.on("tensor",
                     lambda e, p=pt[:, 0:TT], l=wbf[wi][:, cc * 128:(cc + 1) * 128],
                            r=attsb[:, cc * TT:(cc + 1) * TT],
                            st=(cc == 0), sp=(cc == 7): e.matmul(p, l, r, start=st, stop=sp),
                     wait=w, inc="mm" if cc == 7 else None)
            mm_done = b.cnt["mm"]
            wrot.tok[wi] = ("mm", mm_done)
            vt = b.on("vector",
                      lambda e, o=zbuf[:, dc * TT:(dc + 1) * TT], i=pt[:, 0:TT],
                             bi=bproj_sb[:, dc:dc + 1], c=x2f[:, dc * TT:(dc + 1) * TT]:
                      e.scalar_tensor_tensor(out=o, in0=i, scalar=bi, in1=c,
                                             op0=OP.add, op1=OP.add),
                      wait=[("mm", mm_done), ("lxf", ldx), ("sc", b.cnt["sc"])], inc="ve")
            pS_rot.tok[pi] = ("ve", vt)
        ve_out = emit_post_ln(b, zbuf, TT, sqtile, rows, pstat1, pstat2, pbc1, pbc2,
                              onescol, onesrow, epscol,
                              outf=lambda cc: x3buf[:, cc * TT:(cc + 1) * TT])
        stt = None
        for cc in range(8):
            stt = b.dma("sync", x3T[cc * 128:(cc + 1) * 128, tt * TT:(tt + 1) * TT],
                        x3buf[:, cc * TT:(cc + 1) * TT],
                        wait=[("ve", ve_out)] if cc == 0 else None, inc="st")
        b.wait("vector", "st", stt)
    b.pop_scope()
    b.wait("sync", "st", b.cnt["st"])


def l1_attn_host_inputs(W_qkv, b_qkv, W_proj, b_proj):
    return {"wqkv": np.ascontiguousarray(W_qkv),
            "bqkv_pp": pp_col(b_qkv),
            "wproj": np.ascontiguousarray(W_proj),
            "bproj_pp": pp_col(b_proj)}


def build_l1_attn_launch():
    b = B()
    x2T = b.dram_in("x2T", [D, T])
    wqkv = b.dram_in("wqkv", [D, 3 * D])
    bqkv_pp = b.dram_in("bqkv_pp", [128, 24])
    wproj = b.dram_in("wproj", [D, D])
    bproj_pp = b.dram_in("bproj_pp", [128, 8])
    attn_dram = b.dram_tmp("attn_dram", [D, T], BF16)
    x3T = b.dram_out("x3T", [D, T])
    emit_l1_attn(b, x2T, x3T, wqkv, bqkv_pp, wproj, bproj_pp, attn_dram)
    return b.finalize()


def butterfly_host_arrays(up_w, dn_w):
    DH = 4096

    def stage_block_mats(w, s):
        S = 2 ** s
        M = np.zeros((32, 128, 128), np.float32)
        for c in range(DH):
            blk, i = divmod(c, 128)
            if (c // S) % 2 == 0:
                k = (c // (2 * S)) * S + (c % S)
                M[blk, i, i] = w[s, k, 0, 0]
                M[blk, i, i + S] = w[s, k, 0, 1]
            else:
                cp = c - S
                k = (cp // (2 * S)) * S + (cp % S)
                M[blk, i, i] = w[s, k, 1, 1]
                M[blk, i, i - S] = w[s, k, 1, 0]
        return M

    def coeffs(w, s, transposed):
        S = 2 ** s
        A = np.zeros(DH, np.float32); Bv = np.zeros(DH, np.float32)
        for c in range(DH):
            if (c // S) % 2 == 0:
                k = (c // (2 * S)) * S + (c % S)
                A[c] = w[s, k, 0, 0]
                Bv[c] = w[s, k, 0, 1] if not transposed else w[s, k, 1, 0]
            else:
                cp = c - S
                k = (cp // (2 * S)) * S + (cp % S)
                A[c] = w[s, k, 1, 1]
                Bv[c] = w[s, k, 1, 0] if not transposed else w[s, k, 0, 1]
        return pp_col(A), pp_col(Bv)

    ups = np.stack([stage_block_mats(up_w, s) for s in range(6)])
    ups6t = np.ascontiguousarray(stage_block_mats(up_w, 6).transpose(0, 2, 1))
    dns = [stage_block_mats(dn_w, s) for s in range(7)]
    dns0 = dns[0]
    dnst = np.stack([np.ascontiguousarray(dns[s].transpose(0, 2, 1)) for s in range(1, 7)])
    cup = np.zeros((128, 320), np.float32)
    cdn = np.zeros((128, 320), np.float32)
    for s in range(7, 12):
        A, Bv = coeffs(up_w, s, False)
        cup[:, (s - 7) * 64:(s - 7) * 64 + 32] = A
        cup[:, (s - 7) * 64 + 32:(s - 7) * 64 + 64] = Bv
        A, Bv = coeffs(dn_w, s, True)
        cdn[:, (s - 7) * 64:(s - 7) * 64 + 32] = A
        cdn[:, (s - 7) * 64 + 32:(s - 7) * 64 + 64] = Bv
    return {"ups": ups, "ups6t": ups6t, "dns0": dns0, "dnst": dnst,
            "cup": cup, "cdn": cdn}


def emit_l1_ffn(b, x3T, x4T, vaug, bexp_pp, wcon, ups, ups6t, dns0, dnst, cup, cdn,
                upbias_pp, bcon_pp, dbias_pp, wexp_scr, wcon_scr):
    b.presem()
    TT = 512
    NTT = 8
    DH = 4096

    pool1 = b.sb("H_pool1", [128, 16384], F32)
    pool2 = b.sb("H_pool2", [128, 16384], F32)
    bufA = pool1
    bufB = pool2
    pool1b = pool1[:, :].bitcast(BF16)
    pool2b = pool2[:, :].bitcast(BF16)
    def wcon_full_sl(a, c):
        return pool1b[:, a:c]
    def h3_sl(a, c):
        return pool2b[:, a:c]
    zbuf = pool2[:, 8192:12288]
    x4buf = pool2[:, 12288:16384]
    chainM = b.sb("H_chainM", [128, 4096], F32)
    tb = [b.sb(f"H_tb{i}", [128, 128], BF16) for i in range(2)]
    trs = [b.sb(f"H_trs{i}", [128, 128], BF16) for i in range(2)]
    cupsb = b.sb("H_cup", [128, 320], F32)
    cdnsb = b.sb("H_cdn", [128, 320], F32)
    upbias_sb = b.sb("H_upb", [128, 32], F32)
    gbias = b.sb("H_gbias", [128, 32], F32)
    beffc = b.sb("H_beffc", [128, 32], F32)
    bcol = [b.sb(f"H_bcol{i}", [128, 32], F32) for i in range(2)]
    bcolt = b.sb("H_bcolt", [128, 1], F32)
    ybias = b.sb("H_ybias", [128, 8], F32)
    bcon_sb = b.sb("H_bcon", [128, 8], F32)
    dbias_sb = b.sb("H_dbias", [128, 8], F32)
    x3f_ = [b.sb(f"H_x3f{i}", [128, 8 * TT], F32) for i in range(2)]
    x3bt_ = [b.sb(f"H_x3bt{i}", [128, 8 * TT], BF16) for i in range(2)]
    ttile = x3f_[1][:, 0:TT]
    ttile2 = x3f_[1][:, TT:2 * TT]
    scscr = [x3f_[1][:, 2 * TT:3 * TT], x3f_[1][:, 3 * TT:4 * TT]]
    b._stg_slot = [None, None]
    wexp_sl = [b.sb(f"H_wexp{i}", [128, 1024], BF16) for i in range(2)]
    onesb = b.sb("H_onesb", [128, 128], BF16)
    identb = b.sb("H_identb", [128, 128], BF16)
    onescol = b.sb("H_onescol", [128, 1], BF16)
    onesrow = b.sb("H_onesrow", [1, 128], BF16)
    epscol = b.sb("H_epscol", [128, 1], F32)
    sqtile = chainM[:, 2048:2304].bitcast(BF16)
    b._ln_zbf = chainM[:, 2304:2560].bitcast(BF16)
    b._ln_rows_bf = chainM[0:1, 2560:3072].bitcast(BF16)
    rows = chainM[0:1, 0:2048]

    ph = [b.ps(f"H_ph{i}", [128, TT]) for i in range(2)]
    py = [b.ps(f"H_py{i}", [128, TT]) for i in range(2)]
    b.push_scope()
    pTRb = b.ps("H_pTRb", [128, 128], BF16)

    ph_rot = Rot(ph)
    py_rot = Rot(py)

    for dst, src in [(cupsb, cup), (cdnsb, cdn), (upbias_sb, upbias_pp),
                     (bcon_sb, bcon_pp), (dbias_sb, dbias_pp)]:
        b.dma("sync", dst[:, :], src[:, :], inc="lc")
    lc_tot = b.cnt["lc"]
    b.on("vector", lambda e, o=onescol[:, :]: e.memset(o, 1.0))
    b.on("vector", lambda e, o=onesrow[:, :]: e.memset(o, 1.0))
    b.on("vector", lambda e, o=epscol[:, :]: e.memset(o, EPS))
    b.on("vector", lambda e, o=onesb[:, :]: e.memset(o, 1.0), inc="ve")
    b.on("vector", lambda e, o=ybias[:, :], a=bcon_sb[:, :], c=dbias_sb[:, :]:
         e.tensor_tensor(out=o, in0=a, in1=c, op=OP.add), wait=[("lc", lc_tot)], inc="ve")
    gp_id = b.on("gpsimd", lambda e, o=identb[:, :], i=onesb[:, :]:
                 e.affine_select(out=o, in_=i, pattern=[[1, 128]], compare_op=OP.is_equal,
                                 fill=0.0, base=0, channel_multiplier=-1),
                 wait=[("ve", b.cnt["ve"])], inc="gp")

    def build_chain(dst_name, seed_arr, mats, ascending):
        ld0 = None
        for bi in range(32):
            ld0 = b.dma("sync", chainM[:, bi * 128:(bi + 1) * 128], seed_arr[bi, :, :],
                        wait=[("ve", b.cnt["ve"]), ("mm", b.cnt["mm"])] if bi == 0 else None,
                        inc="lm0")
        ve_prev = ("lm0", ld0)
        for si, mat in enumerate(mats):
            half = si % 2
            base = half * 4096
            sem = f"ls{half}"
            lds = None
            for bi in range(32):
                w0 = None
                if bi == 0:
                    w0 = [("mm", b._chain_halfmm[half])] if b._chain_halfmm[half] else None
                    if si == 0:
                        w0 = (w0 or []) + [("ve", b.cnt["ve"]), ("gp", b.cnt["gp"]),
                                           ("mm", b.cnt["mm"])]
                lds = b.dma("sync", bufA[:, base + bi * 128:base + (bi + 1) * 128],
                            mat[bi, :, :], wait=w0, inc=sem)
            for bi in range(32):
                pi, pt, ptok = ph_rot.get()
                w = [(sem, lds), ve_prev] if bi == 0 else ([ptok] if ptok else None)
                if bi == 0 and ptok:
                    w.append(ptok)
                mmt = b.on("tensor",
                           lambda e, p=pt[:, 0:128],
                                  l=bufA[:, base + bi * 128:base + (bi + 1) * 128],
                                  r=chainM[:, bi * 128:(bi + 1) * 128]:
                           e.matmul(p, l, r, start=True, stop=True),
                           wait=w, inc="mm")
                vct = b.on("scalar",
                           lambda e, o=chainM[:, bi * 128:(bi + 1) * 128], i=pt[:, 0:128]:
                           e.copy(out=o, in_=i), wait=[("mm", mmt)], inc="sc")
                ph_rot.tok[pi] = ("sc", vct)
            b._chain_halfmm[half] = b.cnt["mm"]
            ve_prev = ("sc", b.cnt["sc"])
        return b.cnt["sc"]
    b._chain_halfmm = [None, None]

    def stages_pass(src_getcol, W, bufs, cur, stage_list, coefsb, tok0):
        for s in stage_list:
            Sc = 2 ** (s - 7)
            base = (s - 7) * 64
            ve_bar = b.cnt["ve"]; gp_bar = b.cnt["gp"]; sc_bar = b.cnt["sc"]
            nsc = 0
            for cb in range(32):
                pb = cb ^ Sc
                eng = "vector" if cb % 5 != 4 else "gpsimd"
                acol = coefsb[:, base + cb:base + cb + 1]
                bcol_ = coefsb[:, base + 32 + cb:base + 33 + cb]
                w0 = [("ve", ve_bar), ("gp", gp_bar), ("sc", sc_bar)] + ([tok0] if tok0 else [])
                if eng == "vector":
                    slot = nsc % 2
                    nsc += 1
                    sg = b._stg_slot[slot]
                    sct = b.on("scalar",
                               lambda e, o=scscr[slot], i=bufs[cur](cb, W), a=acol:
                               e.activation(out=o, in_=i, func=AF.Identity, scale=a),
                               wait=w0 + ([sg] if sg else []), inc="sc")
                    vt1 = b.on("vector",
                               lambda e, o=bufs[1 - cur](cb, W), i=bufs[cur](pb, W), bc=bcol_,
                                      t=scscr[slot]:
                               e.scalar_tensor_tensor(out=o, in0=i, scalar=bc, in1=t,
                                                      op0=OP.mult, op1=OP.add),
                               wait=[("sc", sct)], inc="ve")
                    b._stg_slot[slot] = ("ve", vt1)
                else:
                    t1 = b.on(eng, lambda e, o=ttile2, i=bufs[cur](cb, W),
                                     a=acol.broadcast_to((128, W)):
                              e.tensor_tensor(out=o, in0=i, in1=a, op=OP.mult),
                              wait=w0, inc="gp")
                    t2 = b.on(eng, lambda e, o=bufs[1 - cur](cb, W), i=bufs[cur](pb, W),
                                     bc=bcol_.broadcast_to((128, W)):
                              e.tensor_tensor(out=o, in0=i, in1=bc, op=OP.mult),
                              wait=[("gp", t1)], inc="gp")
                    b.on(eng, lambda e, o=bufs[1 - cur](cb, W), a=bufs[1 - cur](cb, W),
                              t=ttile2:
                         e.tensor_tensor(out=o, in0=a, in1=t, op=OP.add),
                         wait=[("gp", t2)], inc="gp")
            tok0 = None
            cur = 1 - cur
        b.on("vector", None, wait=[("gp", b.cnt["gp"])])
        b.on("gpsimd", None, wait=[("ve", b.cnt["ve"])])
        return cur

    bufAc = lambda cb, W: bufA[:, cb * W:(cb + 1) * W]
    bufBc = lambda cb, W: bufB[:, cb * W:(cb + 1) * W]
    bufs = {0: bufAc, 1: bufBc}

    ve_chain_up = build_chain("up", ups6t, [ups[s] for s in range(5, -1, -1)], None)
    for dh in range(2):
        W = 512
        lv = None
        for bi in range(32):
            lv = b.dma("sync", bufA[:, bi * W:(bi + 1) * W],
                       vaug[bi * 128:(bi + 1) * 128, dh * 512:(dh + 1) * 512],
                       wait=[("ve", b.cnt["ve"]), ("gp", b.cnt["gp"]),
                             ("mm", b.cnt["mm"]), ("sw0", b.cnt.get("sw0", 0)),
                             ("sw1", b.cnt.get("sw1", 0))] if bi == 0 else None,
                       inc="lv")
        for bi in range(32):
            pi, pt, ptok = ph_rot.get()
            mmt = b.on("tensor",
                       lambda e, p=pt[:, 0:W], l=chainM[:, bi * 128:(bi + 1) * 128],
                              r=bufA[:, bi * W:(bi + 1) * W]:
                       e.matmul(p, l, r, start=True, stop=True),
                       wait=[("lv", lv), ("sc", ve_chain_up)] + ([ptok] if ptok else []),
                       inc="mm")
            vt = b.on("scalar", lambda e, o=bufB[:, bi * W:(bi + 1) * W], i=pt[:, 0:W]:
                      e.copy(out=o, in_=i), wait=[("mm", mmt)], inc="sc")
            ph_rot.tok[pi] = ("sc", vt)
        cur = stages_pass(None, W, bufs, 1, range(7, 12), cupsb, None)
        if not hasattr(b, "_tb_tok"):
            b._tb_tok = [None, None]
            b._trs_tok = [None, None]
            b._ptr_free = None
        for bi in range(32):
            for dq in range(4):
                i2 = (bi * 4 + dq) % 2
                vt = b.on("vector",
                          lambda e, o=tb[i2][:, :],
                                 i=bufs[cur](bi, W)[:, dq * 128:(dq + 1) * 128]:
                          e.tensor_copy(out=o, in_=i),
                          wait=[b._tb_tok[i2]] if b._tb_tok[i2] else None, inc="ve")
                mmt = b.on("tensor", lambda e, p=pTRb[:, :], i=tb[i2][:, :], idn=identb[:, :]:
                           e.transpose(p, i, idn),
                           wait=[("ve", vt), ("gp", gp_id)] +
                                ([b._ptr_free] if b._ptr_free else []), inc="mm")
                b._tb_tok[i2] = ("mm", mmt)
                vt2 = b.on("scalar", lambda e, o=trs[i2][:, :], i=pTRb[:, :]:
                           e.copy(out=o, in_=i),
                           wait=[("mm", mmt)] + ([b._trs_tok[i2]] if b._trs_tok[i2] else []),
                           inc="sc")
                b._ptr_free = ("sc", vt2)
                sem = f"sw{i2}"
                stok = b.dma("sync",
                             wexp_scr[bi, :, (dh * 4 + dq) * 128:(dh * 4 + dq + 1) * 128],
                             trs[i2][:, :], wait=[("sc", vt2)], inc=sem)
                b._trs_tok[i2] = (sem, stok)
    bcur = 0
    lvb = b.dma("sync", bcol[0][:, :], bexp_pp[:, :],
                wait=[("ve", b.cnt["ve"])], inc="lvb")
    mm_prev = None
    for bi in range(32):
        mmt = b.on("tensor",
                   lambda e, p=ph[0][:, 0:1], l=chainM[:, bi * 128:(bi + 1) * 128],
                          r=bcol[0][:, bi:bi + 1]:
                   e.matmul(p, l, r, start=True, stop=True),
                   wait=[("lvb", lvb), ("ve", b.cnt["ve"])] if bi == 0 else None, inc="mm")
        b.on("vector", lambda e, o=bcol[1][:, bi:bi + 1], i=ph[0][:, 0:1]:
             e.tensor_copy(out=o, in_=i), wait=[("mm", mmt)], inc="ve")
    bcur = 1
    for s in range(7, 12):
        Sc = 2 ** (s - 7)
        base = (s - 7) * 64
        bar = b.on("vector", lambda e, o=bcolt[:, 0:1]: e.memset(o, 0.0), inc="ve")
        for cb in range(32):
            pb = cb ^ Sc
            t1 = b.on("vector", lambda e, o=bcolt[:, 0:1], i=bcol[bcur][:, cb:cb + 1],
                             a=cupsb[:, base + cb:base + cb + 1]:
                      e.tensor_scalar(out=o, in0=i, scalar1=a, scalar2=None, op0=OP.mult),
                      wait=[("ve", b.cnt["ve"])], inc="ve")
            b.on("vector", lambda e, o=bcol[1 - bcur][:, cb:cb + 1], i=bcol[bcur][:, pb:pb + 1],
                        bc=cupsb[:, base + 32 + cb:base + 33 + cb], t=bcolt[:, 0:1]:
                 e.scalar_tensor_tensor(out=o, in0=i, scalar=bc, in1=t,
                                        op0=OP.mult, op1=OP.add),
                 wait=[("ve", t1)], inc="ve")
        bcur = 1 - bcur
    b.on("vector", lambda e, o=gbias[:, :], a=bcol[bcur][:, :], c=upbias_sb[:, :]:
         e.tensor_tensor(out=o, in0=a, in1=c, op=OP.add),
         wait=[("ve", b.cnt["ve"]), ("lc", lc_tot)], inc="ve")
    gbias_tok = b.cnt["ve"]

    wsb16 = [b.sb(f"H_wsb16_{i}", [128, 512], BF16) for i in range(2)]
    b._w16_tok = [None, None]
    ve_chain_dn = build_chain("dn", dns0, [dnst[s] for s in range(6)], None)
    for dh in range(2):
        W = 512
        lv = None
        for bi in range(32):
            lv = b.dma("sync", bufA[:, bi * W:(bi + 1) * W],
                       wcon[bi * 128:(bi + 1) * 128, dh * 512:(dh + 1) * 512],
                       wait=[("ve", b.cnt["ve"]), ("gp", b.cnt["gp"]),
                             ("mm", b.cnt["mm"]), ("sw0", b.cnt.get("sw0", 0)),
                             ("sw1", b.cnt.get("sw1", 0))] if bi == 0 else None,
                       inc="lv")
        b.on("vector", None, wait=[("lv", lv)])
        b.on("gpsimd", None, wait=[("lv", lv)])
        cur = stages_pass(None, W, bufs, 0, range(11, 6, -1), cdnsb, None)
        for bi in range(32):
            pi, pt, ptok = ph_rot.get()
            mmt = b.on("tensor",
                       lambda e, p=pt[:, 0:W], l=chainM[:, bi * 128:(bi + 1) * 128],
                              r=bufs[cur](bi, W):
                       e.matmul(p, l, r, start=True, stop=True),
                       wait=[("ve", b.cnt["ve"]), ("gp", b.cnt["gp"]),
                             ("sc", ve_chain_dn)] + ([ptok] if ptok else []),
                       inc="mm")
            vt = b.on("vector", lambda e, o=bufs[1 - cur](bi, W).bitcast(BF16)[:, 0:W],
                             i=pt[:, 0:W]:
                      e.tensor_copy(out=o, in_=i), wait=[("mm", mmt)], inc="ve")
            ph_rot.tok[pi] = ("ve", vt)
            b.dma("sync", wcon_scr[bi, :, dh * 512:(dh + 1) * 512],
                  bufs[1 - cur](bi, W).bitcast(BF16)[:, 0:W],
                  wait=[("ve", vt)], inc="sw0")
    sw_done = b.cnt["sw0"]

    b.pop_scope()
    b.barrier()
    b.push_scope()
    pstat1 = b.ps("H_pstat1", [1, TT])
    pstat2 = b.ps("H_pstat2", [1, TT])
    pbc1 = b.ps("H_pbc1", [128, TT])
    pbc2 = b.ps("H_pbc2", [128, TT])
    lwc = None
    for hcl in range(32):
        lwc = b.dma("sync", wcon_full_sl(hcl * 1024, (hcl + 1) * 1024), wcon_scr[hcl, :, :],
                    wait=[("sw0", sw_done), ("mm", b.cnt["mm"]), ("ve", b.cnt["ve"])]
                    if hcl == 0 else None, inc="lwc")
    x3_cons = [None, None]
    for tt in range(NTT):
        ts = tt * TT
        par = tt % 2
        x3f = x3f_[par]
        x3bt = x3bt_[par]
        lx = None
        for cc in range(8):
            w0 = None
            if cc == 0:
                w0 = [x3_cons[par]] if x3_cons[par] else []
                if tt < 2:
                    w0 += [("ve", b.cnt["ve"]), ("gp", b.cnt["gp"]), ("sc", b.cnt["sc"])]
            lx = b.dma("sync", x3f[:, cc * TT:(cc + 1) * TT],
                       x3T[cc * 128:(cc + 1) * 128, ts:ts + TT],
                       wait=w0, inc=f"lx{par}")
        cvx = b.on("vector", lambda e, o=x3bt[:, :], i=x3f[:, :]: e.tensor_copy(out=o, in_=i),
                   wait=[(f"lx{par}", lx)], inc="ve")
        wslot_tok = [None, None]
        for hc in range(32):
            i2 = hc % 2
            sem = f"lw{i2}"
            ldw = b.dma("sync", wexp_sl[i2][:, :], wexp_scr[hc, :, :],
                        wait=[wslot_tok[i2]] if wslot_tok[i2] else
                             ([("sw0", b.cnt["sw0"]), ("sw1", b.cnt["sw1"])] if hc < 2 and tt == 0 else None),
                        inc=sem)
            pi, pt, ptok = ph_rot.get()
            for cc in range(8):
                w = None
                if cc == 0:
                    w = [(sem, ldw), ("ve", cvx)] + ([ptok] if ptok else [])
                b.on("tensor",
                     lambda e, p=pt[:, 0:TT], l=wexp_sl[i2][:, cc * 128:(cc + 1) * 128],
                            r=x3bt[:, cc * TT:(cc + 1) * TT],
                            st=(cc == 0), sp=(cc == 7): e.matmul(p, l, r, start=st, stop=sp),
                     wait=w, inc="mm" if cc == 7 else None)
            mmh = b.cnt["mm"]
            wslot_tok[i2] = ("mm", mmh)
            sct = b.on("scalar",
                       lambda e, o=h3_sl(hc * TT, (hc + 1) * TT), i=pt[:, 0:TT],
                              bi=gbias[:, hc:hc + 1]:
                       e.activation(out=o, in_=i, func=AF.Gelu, bias=bi, scale=1.0),
                       wait=[("mm", mmh), ("ve", gbias_tok)], inc="sc")
            ph_rot.tok[pi] = ("sc", sct)
        h3_sc = b.cnt["sc"]
        for dc in range(8):
            pi, pt, ptok = py_rot.get()
            for hcl in range(32):
                w = None
                if hcl == 0:
                    w = [("lwc", lwc), ("sc", h3_sc)] + ([ptok] if ptok else [])
                b.on("tensor",
                     lambda e, p=pt[:, 0:TT],
                            l=wcon_full_sl(hcl * 1024 + dc * 128, hcl * 1024 + (dc + 1) * 128),
                            r=h3_sl(hcl * TT, (hcl + 1) * TT),
                            st=(hcl == 0), sp=(hcl == 31): e.matmul(p, l, r, start=st, stop=sp),
                     wait=w, inc="mm" if hcl == 31 else None)
            mmy = b.cnt["mm"]
            vt = b.on("vector",
                      lambda e, o=zbuf[:, dc * TT:(dc + 1) * TT], i=pt[:, 0:TT],
                             bi=ybias[:, dc:dc + 1], c=x3f[:, dc * TT:(dc + 1) * TT]:
                      e.scalar_tensor_tensor(out=o, in0=i, scalar=bi, in1=c,
                                             op0=OP.add, op1=OP.add),
                      wait=[("mm", mmy), ("sc", b.cnt["sc"])], inc="ve")
            py_rot.tok[pi] = ("ve", vt)
        x3_cons[par] = ("ve", b.cnt["ve"])
        ve_out = emit_post_ln(b, zbuf, TT, sqtile, rows, pstat1, pstat2, pbc1, pbc2,
                              onescol, onesrow, epscol,
                              outf=lambda cc: x4buf[:, cc * TT:(cc + 1) * TT])
        stt = None
        for cc in range(8):
            stt = b.dma("sync", x4T[cc * 128:(cc + 1) * 128, ts:ts + TT],
                        x4buf[:, cc * TT:(cc + 1) * TT],
                        wait=[("ve", ve_out)] if cc == 0 else None, inc="st")
        b.wait("vector", "st", stt)
    b.pop_scope()
    b.wait("sync", "st", b.cnt["st"])


def l1_ffn_host_inputs(W_exp, b_exp, up_w, upb, dn_w, W_con, b_con, d_bias):
    d = {"vaug": np.ascontiguousarray(W_exp.T.astype(np.float32)),
         "bexp_pp": pp_col(b_exp),
         "wcon": np.ascontiguousarray(W_con),
         "upbias_pp": pp_col(upb), "bcon_pp": pp_col(b_con), "dbias_pp": pp_col(d_bias)}
    d.update(butterfly_host_arrays(up_w, dn_w))
    return d


def build_l1_ffn_launch():
    b = B()
    x3T = b.dram_in("x3T", [D, T])
    vaug = b.dram_in("vaug", [4096, 1024])
    bexp_pp = b.dram_in("bexp_pp", [128, 32])
    wcon = b.dram_in("wcon", [4096, 1024])
    ups = b.dram_in("ups", [6, 32, 128, 128])
    ups6t = b.dram_in("ups6t", [32, 128, 128])
    dns0 = b.dram_in("dns0", [32, 128, 128])
    dnst = b.dram_in("dnst", [6, 32, 128, 128])
    cup = b.dram_in("cup", [128, 320])
    cdn = b.dram_in("cdn", [128, 320])
    upbias_pp = b.dram_in("upbias_pp", [128, 32])
    bcon_pp = b.dram_in("bcon_pp", [128, 8])
    dbias_pp = b.dram_in("dbias_pp", [128, 8])
    wexp_scr = b.dram_tmp("wexp_scr", [32, 128, 1024], BF16)
    wcon_scr = b.dram_tmp("wcon_scr", [32, 128, 1024], BF16)
    x4T = b.dram_out("x4T", [D, T])
    emit_l1_ffn(b, x3T, x4T, vaug, bexp_pp, wcon,
                ups, ups6t, dns0, dnst, cup, cdn,
                upbias_pp, bcon_pp, dbias_pp, wexp_scr, wcon_scr)
    return b.finalize()


_CACHE = {}


def _programs():
    if "p" not in _CACHE:
        _CACHE["p"] = (build_l0_attn_launch(), build_l0_ffn_launch(),
                       build_l1_attn_launch(), build_l1_ffn_launch())
    return _CACHE["p"]


def kernel_unfused(**inputs):
    x0 = np.asarray(inputs["L0_attn_x"], dtype=np.float32)
    NCORE = 8
    nc1, nc2, nc3, nc4 = _programs()

    h1 = l0_attn_host_inputs(None, inputs["L0_attn_Wqkv"], inputs["L0_attn_Wgate"],
                             inputs["L0_attn_Wout"], inputs["L0_attn_bgate"],
                             inputs["L0_attn_decay_param"])
    del h1["x0T"]
    im1 = []
    for c in range(NCORE):
        d = dict(h1)
        d["x0T"] = np.ascontiguousarray(x0[c].T)
        im1.append(d)
    r1 = run_bass_kernel_spmd(nc1, im1, core_ids=list(range(NCORE)))

    h2 = l0_ffn_host_inputs(inputs["L0_ffn_Wg"], inputs["L0_ffn_W1"], inputs["L0_ffn_b1"],
                            inputs["L0_ffn_W2"], inputs["L0_ffn_b2"])
    im2 = []
    for c in range(NCORE):
        d = dict(h2)
        d["x1T"] = np.ascontiguousarray(r1.results[c]["x1T"])
        im2.append(d)
    r2 = run_bass_kernel_spmd(nc2, im2, core_ids=list(range(NCORE)))

    h3 = l1_attn_host_inputs(inputs["L1_attn_W_qkv"], inputs["L1_attn_b_qkv"],
                             inputs["L1_attn_W_proj"], inputs["L1_attn_b_proj"])
    im3 = []
    for c in range(NCORE):
        d = dict(h3)
        d["x2T"] = np.ascontiguousarray(r2.results[c]["x2T"])
        im3.append(d)
    r3 = run_bass_kernel_spmd(nc3, im3, core_ids=list(range(NCORE)))

    h4 = l1_ffn_host_inputs(inputs["L1_ffn_W_exp"], inputs["L1_ffn_b_exp"],
                            inputs["L1_ffn_up_weights"], inputs["L1_ffn_up_bias"],
                            inputs["L1_ffn_down_weights"], inputs["L1_ffn_W_con"],
                            inputs["L1_ffn_b_con"], inputs["L1_ffn_down_bias"])
    im4 = []
    for c in range(NCORE):
        d = dict(h4)
        d["x3T"] = np.ascontiguousarray(r3.results[c]["x3T"])
        im4.append(d)
    r4 = run_bass_kernel_spmd(nc4, im4, core_ids=list(range(NCORE)))

    out = np.stack([np.ascontiguousarray(r4.results[c]["x4T"].T) for c in range(NCORE)])
    return out.astype(np.float32, copy=False)


def build_fused():
    b = B()
    x0T = b.dram_in("x0T", [D, T])
    wqkvT = b.dram_in("wqkvT", [D, 3 * D])
    wgateT = b.dram_in("wgateT", [D, D])
    woutT = b.dram_in("woutT", [D, D])
    bgate_pp = b.dram_in("bgate_pp", [128, 8])
    decay_p = b.dram_in("decay_p", [16, 1])
    consts0 = {k: b.dram_in(k, v.shape) for k, v in host_consts_l0().items()}
    wg = b.dram_in("wg", [D, 8])
    w1 = b.dram_in("w1", [8, D, 2048])
    b1pp = b.dram_in("b1pp", [128, 128])
    w2 = b.dram_in("w2", [8, 2048, D])
    b2 = b.dram_in("b2", [8, D])
    constsF = {k: b.dram_in(k, v.shape, BF16) for k, v in host_consts_ffn().items()}
    wqkv1 = b.dram_in("wqkv1", [D, 3 * D])
    bqkv_pp = b.dram_in("bqkv_pp", [128, 24])
    wproj = b.dram_in("wproj", [D, D])
    bproj_pp = b.dram_in("bproj_pp", [128, 8])
    vaug = b.dram_in("vaug", [4096, 1024])
    bexp_pp = b.dram_in("bexp_pp", [128, 32])
    wcon = b.dram_in("wcon", [4096, 1024])
    ups = b.dram_in("ups", [6, 32, 128, 128])
    ups6t = b.dram_in("ups6t", [32, 128, 128])
    dns0 = b.dram_in("dns0", [32, 128, 128])
    dnst = b.dram_in("dnst", [6, 32, 128, 128])
    cup = b.dram_in("cup", [128, 320])
    cdn = b.dram_in("cdn", [128, 320])
    upbias_pp = b.dram_in("upbias_pp", [128, 32])
    bcon_pp = b.dram_in("bcon_pp", [128, 8])
    dbias_pp = b.dram_in("dbias_pp", [128, 8])
    x4T = b.dram_out("x4T", [D, T])
    x1T = b.dram_tmp("x1T", [D, T])
    x2T = b.dram_tmp("x2T", [D, T])
    x3T = b.dram_tmp("x3T", [D, T])
    w1b = b.dram_tmp("w1b", [8, 16, 128, 1024], BF16)
    wqkvb = b.dram_tmp("wqkvb", [D, 3 * D], BF16)
    wgateb = b.dram_tmp("wgateb", [D, D], BF16)
    woutb = b.dram_tmp("woutb", [D, D], BF16)
    w2b = b.dram_tmp("w2b", [8, 16, 128, 1024], BF16)
    attn_dram = b.dram_tmp("attn_dram", [D, T], BF16)
    wexp_scr = b.dram_tmp("wexp_scr", [32, 128, 1024], BF16)
    wcon_scr = b.dram_tmp("wcon_scr", [32, 128, 1024], BF16)

    b.begin_phase()
    emit_l0_attn(b, x0T, x1T, wqkvT, wgateT, woutT, bgate_pp, decay_p, consts0,
                 wqkvb, wgateb, woutb)
    b.end_phase()
    b.barrier()
    b.begin_phase()
    emit_l0_ffn(b, x1T, x2T, wg, w1, b1pp, w2, b2, w1b, w2b, constsF)
    b.end_phase()
    b.barrier()
    b.begin_phase()
    emit_l1_attn(b, x2T, x3T, wqkv1, bqkv_pp, wproj, bproj_pp, attn_dram)
    b.end_phase()
    b.barrier()
    b.begin_phase()
    emit_l1_ffn(b, x3T, x4T, vaug, bexp_pp, wcon,
                ups, ups6t, dns0, dnst, cup, cdn,
                upbias_pp, bcon_pp, dbias_pp, wexp_scr, wcon_scr)
    b.end_phase()
    return b.finalize()


def kernel(**inputs):
    x0 = np.asarray(inputs["L0_attn_x"], dtype=np.float32)
    NCORE = 8
    if "fused" not in _CACHE:
        _CACHE["fused"] = build_fused()
    nc = _CACHE["fused"]
    h1 = l0_attn_host_inputs(None, inputs["L0_attn_Wqkv"], inputs["L0_attn_Wgate"],
                             inputs["L0_attn_Wout"], inputs["L0_attn_bgate"],
                             inputs["L0_attn_decay_param"])
    del h1["x0T"]
    h2 = l0_ffn_host_inputs(inputs["L0_ffn_Wg"], inputs["L0_ffn_W1"], inputs["L0_ffn_b1"],
                            inputs["L0_ffn_W2"], inputs["L0_ffn_b2"])
    h3 = l1_attn_host_inputs(inputs["L1_attn_W_qkv"], inputs["L1_attn_b_qkv"],
                             inputs["L1_attn_W_proj"], inputs["L1_attn_b_proj"])
    h3 = {("wqkv1" if k == "wqkv" else k): v for k, v in h3.items()}
    h4 = l1_ffn_host_inputs(inputs["L1_ffn_W_exp"], inputs["L1_ffn_b_exp"],
                            inputs["L1_ffn_up_weights"], inputs["L1_ffn_up_bias"],
                            inputs["L1_ffn_down_weights"], inputs["L1_ffn_W_con"],
                            inputs["L1_ffn_b_con"], inputs["L1_ffn_down_bias"])
    shared = {}
    shared.update(h1); shared.update(h2); shared.update(h3); shared.update(h4)
    im = []
    for c in range(NCORE):
        d = dict(shared)
        d["x0T"] = np.ascontiguousarray(x0[c].T)
        im.append(d)
    res = run_bass_kernel_spmd(nc, im, core_ids=list(range(NCORE)))
    out = np.stack([np.ascontiguousarray(res.results[c]["x4T"].T) for c in range(NCORE)])
    return out.astype(np.float32, copy=False)
